# version 09
# baseline (speedup 1.0000x reference)
import sys
if "/opt/trn_rl_repo" not in sys.path:
    sys.path.insert(0, "/opt/trn_rl_repo")
"""Builders for the mk_seq_postln_d2 megakernel stages (dev version).

T = 8192 tokens, D = 2048 dims, N = 16 ssm state, R = 128 dt_rank
NC = 8 cores; DC = 256 dims/core (stage 1); TC = 1024 tokens/core
"xT" means feature-major [d on partitions, t on free].
"""
import os
import numpy as np
import ml_dtypes
import concourse.bass as bass
import concourse.mybir as mybir

F32 = mybir.dt.float32
BF16 = mybir.dt.bfloat16
I32 = mybir.dt.int32
AF = mybir.ActivationFunctionType
ALU = mybir.AluOpType

T, D, N, R = 8192, 2048, 16, 128
NC = 8
DC = D // NC
TC = T // NC
HALF = T // 2


def emit_stage1(nc, xT_tc, xT_dc, WxT, WdtT_c, bdt_c, A_c, Dp_c, ypreT,
                ag_dt_in, ag_dt_out, ag_bc_in, ag_bc_out, bc_packed,
                sems=None, ag_y=None, a2a_y=None, post_ag_hook=None):
    TT = 512
    NB = DC // 128
    NT = TC // TT
    NTILE = T // TT

    import contextlib
    ctx = contextlib.ExitStack()
    with ctx:
        wdt = ctx.enter_context(nc.sbuf_tensor("s1_wdt", [128, DC], BF16))
        bdt = ctx.enter_context(nc.sbuf_tensor("s1_bdt", [128, NB], F32))
        acol = ctx.enter_context(nc.sbuf_tensor("s1_acol", [128, NB * N], F32))
        dcol = ctx.enter_context(nc.sbuf_tensor("s1_dcol", [128, NB], F32))
        dp1 = ctx.enter_context(nc.sbuf_tensor("s1_dp1", [128, NB], F32))

        s_ld = ctx.enter_context(nc.semaphore("s1_ld"))
        s_mm = ctx.enter_context(nc.semaphore("s1_mm"))
        s_ev = ctx.enter_context(nc.semaphore("s1_ev"))
        s_ag = ctx.enter_context(nc.semaphore("s1_ag"))
        s_dma = ctx.enter_context(nc.semaphore("s1_dma"))
        s_dts = [ctx.enter_context(nc.semaphore("s1_dts0")),
                 ctx.enter_context(nc.semaphore("s1_dts1"))]
        s_sp = ctx.enter_context(nc.semaphore("s1_sp"))
        s_xb = ctx.enter_context(nc.semaphore("s1_xb"))
        s_dxr = ctx.enter_context(nc.semaphore("s1_dxr"))
        s_bc = [ctx.enter_context(nc.semaphore("s1_bc0")),
                ctx.enter_context(nc.semaphore("s1_bc1"))]
        s_ad = ctx.enter_context(nc.semaphore("s1_ad"))
        s_adc = ctx.enter_context(nc.semaphore("s1_adc"))
        s_tm = ctx.enter_context(nc.semaphore("s1_tm"))
        s_yo = ctx.enter_context(nc.semaphore("s1_yo"))
        ch_v = ctx.enter_context(nc.semaphore("s1_chv"))
        ch_sc = ctx.enter_context(nc.semaphore("s1_chsc"))
        ch_g = ctx.enter_context(nc.semaphore("s1_chg"))
        s_rp = ctx.enter_context(nc.semaphore("s1_rp"))

        with (
            nc.sbuf_tensor("s1_wx", [128, 16 * 160], BF16) as wx,
            nc.sbuf_tensor("s1_xttc", [128, 16 * TC], BF16) as xttc,
            nc.sbuf_tensor("s1_dtT", [128, TC], BF16) as dtT,
            nc.sbuf_tensor("s1_bcs", [32, TC], BF16) as bcs,
            nc.psum_tensor("s1_psdt", [128, 2 * TT], F32) as ps_dt,
            nc.psum_tensor("s1_psbc", [32, 2 * TT], F32) as ps_bc,
            nc.Block() as block,
        ):
            @block.gpsimd
            def _(g):
                g.dma_start(out=wx[:, :].rearrange("p (k e) -> p k e", k=16),
                            in_=WxT.rearrange("(k p) e -> p k e", k=16)).then_inc(s_ld, 16)
                g.dma_start(out=xttc[:, :].rearrange("p (k t) -> p k t", k=16),
                            in_=xT_tc.rearrange("(k p) t -> p k t", k=16)).then_inc(s_ld, 16)
                g.dma_start(out=wdt[:, :], in_=WdtT_c).then_inc(s_ld, 16)

            @block.sync
            def _(s):
                s.dma_start(out=bdt[:, :], in_=bdt_c.rearrange("(b p) -> p b", p=128), allow_slow_non_contiguous=True).then_inc(s_dma, 16)
                s.dma_start(out=acol[:, :].rearrange("p (b n) -> p b n", n=16), in_=A_c.rearrange("(b p) n -> p b n", p=128)).then_inc(s_dma, 16)
                s.dma_start(out=dcol[:, :], in_=Dp_c.rearrange("(b p) -> p b", p=128), allow_slow_non_contiguous=True).then_inc(s_dma, 16)

            @block.tensor
            def _(te):
                te.wait_ge(s_ld, 48)
                for tt in range(NT):
                    for k in range(16):
                        te.matmul(ps_dt[:, tt * TT:(tt + 1) * TT],
                                  wx[:, k * 160:k * 160 + 128],
                                  xttc[:, k * TC + tt * TT:k * TC + (tt + 1) * TT],
                                  start=(k == 0), stop=(k == 15))
                    for k in range(16):
                        mm = te.matmul(ps_bc[:, tt * TT:(tt + 1) * TT],
                                  wx[:, k * 160 + 128:k * 160 + 160],
                                  xttc[:, k * TC + tt * TT:k * TC + (tt + 1) * TT],
                                  start=(k == 0), stop=(k == 15))
                    mm.then_inc(s_mm, 1)

            @block.scalar
            def _(sc):
                for tt in range(NT):
                    sc.wait_ge(s_mm, tt + 1)
                    sc.copy(dtT[:, tt * TT:(tt + 1) * TT],
                            ps_dt[:, tt * TT:(tt + 1) * TT]).then_inc(s_ev, 1)

            @block.vector
            def _(v):
                v.wait_ge(s_dma, 48)
                v.tensor_scalar(dp1[:, :], dcol[:, :], 1.0, None, ALU.add).then_inc(s_ev, 1)
                for tt in range(NT):
                    v.wait_ge(s_mm, tt + 1)
                    v.tensor_copy(bcs[:, tt * TT:(tt + 1) * TT],
                                  ps_bc[:, tt * TT:(tt + 1) * TT]).then_inc(s_ev, 1)

            @block.sync
            def _(s):
                s.wait_ge(s_ev, 1 + 2 * NT)
                s.dma_start(out=ag_dt_in[:, :], in_=dtT[:, :]).then_inc(s_dma, 16)
                s.dma_start(out=ag_bc_in[:, :], in_=bcs[:, :]).then_inc(s_dma, 16)

            @block.gpsimd
            def _(g):
                g.wait_ge(s_dma, 80)
                g.collective_compute("AllGather", ALU.bypass,
                                     replica_groups=[list(range(NC))],
                                     ins=[ag_dt_in.ap().opt()],
                                     outs=[ag_dt_out.ap().opt()]).then_inc(s_ag, 1)
                g.collective_compute("AllGather", ALU.bypass,
                                     replica_groups=[list(range(NC))],
                                     ins=[ag_bc_in.ap().opt()],
                                     outs=[ag_bc_out.ap().opt()]).then_inc(s_ag, 1)
                g.wait_ge(s_ag, 2)
                g.dma_start(
                    out=bc_packed.ap().rearrange("n (r t) -> n r t", r=NC),
                    in_=ag_bc_out.ap().rearrange("(r n) t -> n r t", n=2 * N),
                ).then_inc(s_rp, 16)
                if post_ag_hook is not None:
                    post_ag_hook(g)

        nc.all_engine_barrier()

        STOP = os.environ.get("S1_STOP", "full")
        if STOP == "A":
            return
        with (
            nc.sbuf_tensor("s1_deltaT", [128, T], BF16) as deltaT,
            nc.sbuf_tensor("s1_dx", [128, T], BF16) as dx,
            nc.sbuf_tensor("s1_yb", [128, T], F32) as yb,
            nc.sbuf_tensor("s1_xbf", [128, T], F32) as xbf,
            nc.sbuf_tensor("s1_dts", [128, 2 * TT], BF16) as dts,
            nc.sbuf_tensor("s1_bmb", [128, 2 * HALF], BF16) as bmb,
            nc.sbuf_tensor("s1_cmb", [128, 2 * HALF], BF16) as cmb,
            nc.sbuf_tensor("s1_adt", [128, 2 * HALF], BF16) as adt,
            nc.sbuf_tensor("s1_bxt", [128, HALF], BF16) as bxt,
            nc.sbuf_tensor("s1_ht", [128, HALF], BF16) as ht,
            nc.sbuf_tensor("s1_tmp", [128, 2 * HALF], BF16) as tmp,
            nc.sbuf_tensor("s1_carry", [128, 1], F32) as carry,
            nc.psum_tensor("s1_psde", [128, 2 * TT], F32) as ps_de,
            nc.Block() as block,
        ):
            STREAMONLY = STOP == "stream"

            @block.sync
            def _(s):
                for b in range(NB):
                    for tt in range(NTILE):
                        j = b * NTILE + tt
                        if j >= 2 and not STREAMONLY and STOP != "noscalar2":
                            s.wait_ge(s_mm, NT + j - 1)
                        s.dma_start(
                            out=dts[:, (j % 2) * TT:(j % 2 + 1) * TT],
                            in_=ag_dt_out.ap().rearrange("(c p) t -> p c t", p=R)[
                                :, (tt * TT) // TC,
                                (tt * TT) % TC:(tt * TT) % TC + TT],
                        ).then_inc(s_dts[j % 2], 16)
                    if STOP in ("delta", "stream", "noscalar"):
                        continue
                    if b == 0:
                        s.wait_ge(s_rp, 16)
                    for n in range(N):
                        for hf in range(2):
                            idx = (b * N + n) * 2 + hf
                            buf = idx % 2
                            if idx >= 2:
                                s.wait_ge(s_tm, idx - 1)
                            src_b = bc_packed.ap()[n:n + 1, hf * HALF:(hf + 1) * HALF
                                                   ].broadcast_to([128, HALF])
                            s.dma_start(out=bmb[:, buf * HALF:(buf + 1) * HALF],
                                        in_=src_b).then_inc(s_bc[buf], 16)

            @block.tensor
            def _(te):
                if STOP == "stream":
                    return
                for b in range(NB):
                    for tt in range(NTILE):
                        j = b * NTILE + tt
                        te.wait_ge(s_dts[j % 2], (j // 2 + 1) * 16)
                        if j >= 2 and STOP != "noscalar":
                            te.wait_ge(s_sp, j - 1)
                        te.matmul(ps_de[:, (j % 2) * TT:(j % 2 + 1) * TT],
                                  wdt[:, b * 128:(b + 1) * 128],
                                  dts[:, (j % 2) * TT:(j % 2 + 1) * TT],
                                  start=True, stop=True).then_inc(s_mm, 1)

            @block.scalar
            def _(sc):
                if STOP in ("stream", "noscalar"):
                    return
                for b in range(NB):
                    if b > 0:
                        sc.wait_ge(s_dxr, b)
                        sc.wait_ge(s_ad, b * N * 2)
                    for tt in range(NTILE):
                        j = b * NTILE + tt
                        sc.wait_ge(s_mm, NT + j + 1)
                        sl = slice(tt * TT, (tt + 1) * TT)
                        psl = slice((j % 2) * TT, (j % 2 + 1) * TT)
                        if os.environ.get("S1_SOFT", "") == "copy":
                            sc.copy(deltaT[:, sl], ps_de[:, psl]).then_inc(ch_sc, 1)
                            sc.wait_ge(ch_sc, NTILE * b + tt + 1)
                            sc.nop().then_inc(s_sp, 1)
                        elif os.environ.get("S1_SOFT", "") == "exponly":
                            sc.activation(deltaT[:, sl], ps_de[:, psl], AF.Exp,
                                          bias=bdt[:, b:b + 1], scale=1.0).then_inc(ch_sc, 1)
                            sc.wait_ge(ch_sc, NTILE * b + tt + 1)
                            sc.nop().then_inc(s_sp, 1)
                        else:
                            sc.activation(deltaT[:, sl], ps_de[:, psl], AF.Exp,
                                          bias=bdt[:, b:b + 1], scale=1.0).then_inc(ch_sc, 1)
                            sc.wait_ge(ch_sc, NTILE * b + tt + 1)
                            sc.activation(deltaT[:, sl], deltaT[:, sl], AF.Ln,
                                          bias=1.0, scale=1.0).then_inc(s_sp, 1)
                    if STOP == "delta":
                        continue
                    sc.wait_ge(s_sp, (b + 1) * NTILE)
                    for n in range(N):
                        for hf in range(2):
                            idx = (b * N + n) * 2 + hf
                            if idx >= 2:
                                sc.wait_ge(s_adc, idx - 1)
                            sc.activation(adt[:, (idx % 2) * HALF:(idx % 2 + 1) * HALF],
                                          deltaT[:, hf * HALF:(hf + 1) * HALF], AF.Exp,
                                          scale=acol[:, b * N + n:b * N + n + 1]
                                          ).then_inc(s_ad, 1)

            CHV = [0]
            CHG = [0]

            @block.vector
            def _(v):
                if STOP in ("delta", "stream", "noscalar"):
                    return
                for b in range(NB):
                    v.wait_ge(s_xb, (b + 1) * 16)
                    v.wait_ge(s_sp, (b + 1) * NTILE)
                    if b > 0:
                        v.wait_ge(s_yo, b * 16)
                    v.tensor_scalar(yb[:, :], xbf[:, :], dp1[:, b:b + 1], None, ALU.mult)
                    v.tensor_tensor(dx[:, :], deltaT[:, :], xbf[:, :], ALU.mult).then_inc(s_dxr, 1)
                    v.wait_ge(s_dxr, b + 1)
                    for n in range(N):
                        for hf in range(2):
                            idx = (b * N + n) * 2 + hf
                            hsl = slice(hf * HALF, (hf + 1) * HALF)
                            bsl = slice((idx % 2) * HALF, (idx % 2 + 1) * HALF)
                            v.wait_ge(s_bc[idx % 2], (idx // 2 + 1) * 32)
                            v.wait_ge(s_ad, idx + 1)
                            v.wait_ge(s_adc, idx)
                            v.tensor_tensor(bxt[:, :], dx[:, hsl], bmb[:, bsl],
                                            ALU.mult).then_inc(ch_v, 1)
                            CHV[0] += 1
                            if hf == 1:
                                v.tensor_copy(carry[:, 0:1], ht[:, HALF - 1:HALF]).then_inc(ch_v, 1)
                                CHV[0] += 1
                            v.wait_ge(ch_v, CHV[0])
                            v.wait_ge(s_tm, idx)
                            init = 0.0 if hf == 0 else carry[:, 0:1]
                            v.tensor_tensor_scan(ht[:, :], adt[:, bsl], bxt[:, :], init,
                                                 ALU.mult, ALU.add).then_inc(s_adc, 1)
                            if idx >= 2:
                                v.wait_ge(ch_g, idx - 1)
                            v.wait_ge(s_adc, idx + 1)
                            v.tensor_tensor(tmp[:, bsl], ht[:, :], cmb[:, bsl],
                                            ALU.mult).then_inc(s_tm, 1)

            @block.gpsimd
            def _(g):
                if STOP in ("delta", "stream", "noscalar"):
                    return

                def cmb_load(idx2):
                    buf2 = idx2 % 2
                    hf2 = idx2 % 2
                    n2 = (idx2 // 2) % N
                    src_c = bc_packed.ap()[N + n2:N + n2 + 1,
                                           hf2 * HALF:(hf2 + 1) * HALF
                                           ].broadcast_to([128, HALF])
                    g.dma_start(out=cmb[:, buf2 * HALF:(buf2 + 1) * HALF],
                                in_=src_c).then_inc(s_bc[buf2], 16)
                g.wait_ge(s_rp, 16)
                cmb_load(0)
                cmb_load(1)
                for b in range(NB):
                    if b > 0:
                        g.wait_ge(s_dxr, b)
                    g.dma_start(out=xbf[:, :],
                                in_=xT_dc[b * 128:(b + 1) * 128, :]).then_inc(s_xb, 16)
                    for n in range(N):
                        for hf in range(2):
                            idx = (b * N + n) * 2 + hf
                            hsl = slice(hf * HALF, (hf + 1) * HALF)
                            bsl = slice((idx % 2) * HALF, (idx % 2 + 1) * HALF)
                            g.wait_ge(s_tm, idx + 1)
                            if idx + 2 < NB * N * 2:
                                cmb_load(idx + 2)
                            g.wait_ge(ch_g, CHG[0])
                            g.tensor_tensor(yb[:, hsl], yb[:, hsl], tmp[:, bsl],
                                            ALU.add).then_inc(ch_g, 1)
                            CHG[0] += 1
                    g.wait_ge(ch_g, CHG[0])
                    if a2a_y is not None:
                        a2a_ins, a2a_outs, s_agy = a2a_y
                        g.dma_start(
                            out=a2a_ins[b].ap().rearrange("(j p) t -> p j t", p=128),
                            in_=yb[:, :].rearrange("p (j t) -> p j t", j=NC),
                        ).then_inc(s_yo, 16)
                        g.wait_ge(s_yo, (b + 1) * 16)
                        from concourse.aot_env import allow_ici_all_to_all
                        with allow_ici_all_to_all():
                            g.collective_compute(
                                "AllToAll", ALU.bypass,
                                replica_groups=[list(range(NC))],
                                ins=[a2a_ins[b].ap().opt()],
                                outs=[a2a_outs[b].ap().opt()]).then_inc(s_agy, 1)
                    else:
                        g.dma_start(out=ypreT[b * 128:(b + 1) * 128, :],
                                    in_=yb[:, :]).then_inc(s_yo, 16)
                g.wait_ge(s_yo, NB * 16)
                if ag_y is not None:
                    ag_out_t, s_agy = ag_y
                    g.collective_compute("AllGather", ALU.bypass,
                                         replica_groups=[list(range(NC))],
                                         ins=[ypreT.tensor.ap().opt()],
                                         outs=[ag_out_t.ap().opt()]).then_inc(s_agy)


def build_stage1():
    nc = bass.Bass("TRN2", target_bir_lowering=False)
    xT_tc = nc.dram_tensor("xT_tc", [D, TC], F32, kind="ExternalInput").ap()
    xT_dc = nc.dram_tensor("xT_dc", [DC, T], F32, kind="ExternalInput").ap()
    WxT = nc.dram_tensor("WxT", [D, 160], F32, kind="ExternalInput").ap()
    WdtT_c = nc.dram_tensor("WdtT_c", [R, DC], F32, kind="ExternalInput").ap()
    bdt_c = nc.dram_tensor("bdt_c", [DC], F32, kind="ExternalInput").ap()
    A_c = nc.dram_tensor("A_c", [DC, N], F32, kind="ExternalInput").ap()
    Dp_c = nc.dram_tensor("Dp_c", [DC], F32, kind="ExternalInput").ap()
    ypreT = nc.dram_tensor("ypreT", [DC, T], F32, kind="ExternalOutput").ap()

    ag_dt_in = nc.dram_tensor("ag_dt_in", [R, TC], BF16, kind="Internal")
    ag_dt_out = nc.dram_tensor("ag_dt_out", [NC * R, TC], BF16, kind="Internal",
                               addr_space="Shared")
    ag_bc_in = nc.dram_tensor("ag_bc_in", [2 * N, TC], BF16, kind="Internal")
    ag_bc_out = nc.dram_tensor("ag_bc_out", [NC * 2 * N, TC], BF16, kind="Internal",
                               addr_space="Shared")
    bc_packed = nc.dram_tensor("bc_packed", [2 * N, T], BF16, kind="Internal")

    emit_stage1(nc, xT_tc, xT_dc, WxT, WdtT_c, bdt_c, A_c, Dp_c, ypreT,
                ag_dt_in, ag_dt_out, ag_bc_in, ag_bc_out, bc_packed)
    return nc


def stage1_inputs(x, W_xproj, W_dt, b_dt, A, Dp):
    x = np.ascontiguousarray(x.reshape(T, D))
    xT = np.ascontiguousarray(x.T)
    maps = []
    for c in range(NC):
        dc = slice(c * DC, (c + 1) * DC)
        tc = slice(c * TC, (c + 1) * TC)
        maps.append({
            "xT_tc": np.ascontiguousarray(xT[:, tc]),
            "xT_dc": np.ascontiguousarray(xT[dc, :]),
            "WxT": np.ascontiguousarray(W_xproj.T),
            "WdtT_c": np.ascontiguousarray(W_dt[dc, :].T),
            "bdt_c": np.ascontiguousarray(b_dt[dc]),
            "A_c": np.ascontiguousarray(A[dc, :]),
            "Dp_c": np.ascontiguousarray(Dp[dc]),
        })
    return maps



def _prep_pair(nc, deq, dec, tag=""):
    qsrc, qscol_src, qoutT, QNI, QW = deq
    dsrc, dsc2_src, doutT, DNI, DW = dec
    QNK = QW // 128
    DNK = DW // 128
    DWP = DW // 2
    DNSC = DW // 32
    import contextlib
    ctx = contextlib.ExitStack()
    with ctx:
        _e = ctx.enter_context
        qlw = _e(nc.sbuf_tensor("qlw" + tag, [128, 2 * QW], I32))
        qwq = _e(nc.sbuf_tensor("qwq" + tag, [128, QNI * QW], BF16))
        qscol = _e(nc.sbuf_tensor("qsc" + tag, [128, QNI], F32))
        qbcol = _e(nc.sbuf_tensor("qbc" + tag, [128, QNI], F32))
        qarow = _e(nc.sbuf_tensor("qar" + tag, [128, 2 * QNI * 128], BF16))
        ident = _e(nc.sbuf_tensor("pid" + tag, [128, 128], BF16))
        qp0 = _e(nc.psum_tensor("qp0" + tag, [128, 512], BF16))
        qp1 = _e(nc.psum_tensor("qp1" + tag, [128, 512], BF16))
        dlw = _e(nc.sbuf_tensor("dlw" + tag, [128, 2 * DWP], I32))
        dwq = _e(nc.sbuf_tensor("dwq" + tag, [128, DNI * DW], BF16))
        s1t = _e(nc.sbuf_tensor("ds1" + tag, [128, DNI * DNSC], F32))
        scr = {}
        for en in ("v", "g"):
            scr[en] = {nm: _e(nc.sbuf_tensor(f"d{nm}{en}" + tag, [128, DWP],
                                             I32 if nm in ("lo", "hi", "t0", "ub", "sb")
                                             else F32))
                       for nm in ("lo", "hi", "t0", "ub", "sb", "err", "vm")}
        darow = _e(nc.sbuf_tensor("dar" + tag, [128, 2 * DNI * 128], BF16))
        dp0 = _e(nc.psum_tensor("dp0" + tag, [128, 512], BF16))
        dp1 = _e(nc.psum_tensor("dp1" + tag, [128, 512], BF16))
        sm = {nm: _e(nc.semaphore(nm + tag)) for nm in
              ("ql0", "ql1", "qdq", "qtp", "qev", "qa0", "qa1", "qbi",
               "dl0", "dl1", "chv", "chg", "dtp", "dev", "da0", "da1", "psm")}
        q_l = [sm["ql0"], sm["ql1"]]
        q_ar = [sm["qa0"], sm["qa1"]]
        d_l = [sm["dl0"], sm["dl1"]]
        d_ar = [sm["da0"], sm["da1"]]
        chains = {"v": sm["chv"], "g": sm["chg"]}
        identT = nc.inline_tensor(np.eye(128, dtype=np.float32).astype(
            ml_dtypes.bfloat16), "pidc" + tag)
        eng_of = lambda ib: "v"
        n_v = sum(1 for ib in range(DNI) if eng_of(ib) == "v")
        n_g = DNI - n_v
        done_at = {}
        cnt_e = {"v": 0, "g": 0}
        for ib in range(DNI):
            cnt_e[eng_of(ib)] += 18
            done_at[ib] = cnt_e[eng_of(ib)]

        with nc.Block() as block:
            @block.sync
            def _(s):
                s.dma_start(out=ident[:, :], in_=identT.ap()).then_inc(sm["psm"], 16)
                s.dma_start(out=qscol[:, :],
                            in_=qscol_src.rearrange("(b p) -> p b", p=128),
                            allow_slow_non_contiguous=True).then_inc(sm["psm"], 16)
                s.dma_start(out=s1t[:, :].rearrange("p (b n) -> p b n", n=DNSC),
                            in_=dsc2_src.rearrange("(b p) n -> p b n", p=128)
                            ).then_inc(sm["psm"], 16)
                for ib in range(max(QNI, DNI)):
                    if ib < DNI:
                        if ib >= 2:
                            s.wait_ge(chains[eng_of(ib - 2)], done_at[ib - 2])
                        s.dma_start(out=dlw[:, (ib % 2) * DWP:(ib % 2 + 1) * DWP],
                                    in_=dsrc[ib * 128:(ib + 1) * 128, :]
                                    ).then_inc(d_l[ib % 2], 16)
                    if ib < QNI:
                        if ib >= 2:
                            s.wait_ge(sm["qdq"], ib - 1)
                        s.dma_start(out=qlw[:, (ib % 2) * QW:(ib % 2 + 1) * QW],
                                    in_=qsrc[ib * 128:(ib + 1) * 128, :]
                                    ).then_inc(q_l[ib % 2], 16)
                for k in range(QNK):
                    s.wait_ge(sm["qev"], (k + 1) * QNI)
                    s.dma_start(out=qoutT[k * 128:(k + 1) * 128, :],
                                in_=qarow[:, (k % 2) * QNI * 128:(k % 2 + 1) * QNI * 128]
                                ).then_inc(q_ar[k % 2], 16)
                s.wait_ge(q_ar[0], ((QNK + 1) // 2) * 16)
                s.wait_ge(q_ar[1], (QNK // 2) * 16)

            def decode_tile(en, ib, CH):
                key = eng_of(ib)
                ch = chains[key]
                sc_ = scr[key]
                lo, hi, t0 = sc_["lo"], sc_["hi"], sc_["t0"]
                ub, sb, errt, vm = sc_["ub"], sc_["sb"], sc_["err"], sc_["vm"]
                en.wait_ge(d_l[ib % 2], (ib // 2 + 1) * 16)
                lws = dlw[:, (ib % 2) * DWP:(ib % 2 + 1) * DWP]
                en.tensor_scalar(lo[:, :], lws, 15, None, ALU.bitwise_and).then_inc(ch, 1)
                en.tensor_scalar(hi[:, :], lws, 4, 15, ALU.logical_shift_right,
                                 ALU.bitwise_and).then_inc(ch, 1)
                CH[0] += 2
                for pl, csrc in ((0, lo), (1, hi)):
                    en.wait_ge(ch, CH[0])
                    en.tensor_scalar(t0[:, :], csrc[:, :], 7, None,
                                     ALU.bitwise_and).then_inc(ch, 1)
                    CH[0] += 1
                    en.wait_ge(ch, CH[0])
                    en.tensor_scalar(ub[:, :], t0[:, :], 252, 4194304, ALU.add,
                                     ALU.mult).then_inc(ch, 1)
                    en.tensor_scalar(sb[:, :], csrc[:, :], 8, 28, ALU.bitwise_and,
                                     ALU.logical_shift_left).then_inc(ch, 1)
                    CH[0] += 2
                    en.wait_ge(ch, CH[0])
                    en.tensor_scalar(errt[:, :], t0[:, :], -0.25, 0.5, ALU.mult,
                                     ALU.add).then_inc(ch, 1)
                    CH[0] += 1
                    en.wait_ge(ch, CH[0])
                    en.tensor_scalar(errt[:, :], errt[:, :], 0.0, None,
                                     ALU.max).then_inc(ch, 1)
                    CH[0] += 1
                    en.wait_ge(ch, CH[0])
                    en.tensor_tensor(vm[:, :], ub[:, :].bitcast(F32), errt[:, :],
                                     ALU.subtract).then_inc(ch, 1)
                    CH[0] += 1
                    en.wait_ge(ch, CH[0])
                    en.tensor_tensor(vm[:, :].bitcast(I32), vm[:, :].bitcast(I32),
                                     sb[:, :], ALU.bitwise_or).then_inc(ch, 1)
                    CH[0] += 1
                    en.wait_ge(ch, CH[0])
                    sbc = s1t[:, ib * DNSC:(ib + 1) * DNSC].rearrange(
                        "p (n o) -> p n o", o=1).broadcast_to([128, DNSC, 16])
                    ow = dwq[:, ib * DW + pl:(ib + 1) * DW:2].rearrange(
                        "p (n o) -> p n o", o=16)
                    en.tensor_tensor(ow, vm[:, :].rearrange("p (n o) -> p n o", o=16),
                                     sbc, ALU.mult).then_inc(ch, 1)
                    CH[0] += 1

            @block.vector
            def _(v):
                v.wait_ge(sm["psm"], 48)
                v.tensor_scalar(qbcol[:, :], qscol[:, :], -128.0, None,
                                ALU.mult).then_inc(sm["qbi"], 1)
                CH = [0]
                for ib in range(DNI):
                    if eng_of(ib) == "v":
                        decode_tile(v, ib, CH)

            @block.gpsimd
            def _(g):
                CH = [0]
                if n_g:
                    g.wait_ge(sm["psm"], 48)
                for ib in range(DNI):
                    if eng_of(ib) == "g":
                        decode_tile(g, ib, CH)
                for k in range(DNK):
                    g.wait_ge(sm["dev"], (k + 1) * DNI)
                    g.dma_start(out=doutT[k * 128:(k + 1) * 128, :],
                                in_=darow[:, (k % 2) * DNI * 128:(k % 2 + 1) * DNI * 128]
                                ).then_inc(d_ar[k % 2], 16)
                g.wait_ge(d_ar[0], ((DNK + 1) // 2) * 16)
                g.wait_ge(d_ar[1], (DNK // 2) * 16)

            @block.scalar
            def _(sc):
                sc.wait_ge(sm["qbi"], 1)
                for ib in range(QNI):
                    sc.wait_ge(q_l[ib % 2], (ib // 2 + 1) * 16)
                    sc.activation(qwq[:, ib * QW:(ib + 1) * QW],
                                  qlw[:, (ib % 2) * QW:(ib % 2 + 1) * QW], AF.Identity,
                                  bias=qbcol[:, ib:ib + 1],
                                  scale=qscol[:, ib:ib + 1]).then_inc(sm["qdq"], 1)
                cnt = 0
                for k in range(QNK):
                    for ib in range(QNI):
                        cnt += 1
                        if ib == 0 and k >= 2:
                            sc.wait_ge(q_ar[k % 2], (k // 2) * 16)
                        sc.wait_ge(sm["qtp"], cnt)
                        sc.copy(qarow[:, (k % 2) * QNI * 128 + ib * 128:
                                      (k % 2) * QNI * 128 + (ib + 1) * 128],
                                (qp0 if cnt % 2 == 0 else qp1)[:, 0:128]
                                ).then_inc(sm["qev"], 1)
                cnt = 0
                for k in range(DNK):
                    for ib in range(DNI):
                        cnt += 1
                        if ib == 0 and k >= 2:
                            sc.wait_ge(d_ar[k % 2], (k // 2) * 16)
                        sc.wait_ge(sm["dtp"], cnt)
                        sc.copy(darow[:, (k % 2) * DNI * 128 + ib * 128:
                                      (k % 2) * DNI * 128 + (ib + 1) * 128],
                                (dp0 if cnt % 2 == 0 else dp1)[:, 0:128]
                                ).then_inc(sm["dev"], 1)

            @block.tensor
            def _(te):
                te.wait_ge(sm["qdq"], QNI)
                cnt = 0
                for k in range(QNK):
                    for ib in range(QNI):
                        cnt += 1
                        if cnt > 2:
                            te.wait_ge(sm["qev"], cnt - 2)
                        te.transpose((qp0 if cnt % 2 == 0 else qp1)[:, 0:128],
                                     qwq[:, ib * QW + k * 128:ib * QW + (k + 1) * 128],
                                     ident[:, :]).then_inc(sm["qtp"], 1)
                te.wait_ge(sm["chv"], n_v * 18)
                te.wait_ge(sm["chg"], n_g * 18)
                cnt = 0
                for k in range(DNK):
                    for ib in range(DNI):
                        cnt += 1
                        if cnt > 2:
                            te.wait_ge(sm["dev"], cnt - 2)
                        te.transpose((dp0 if cnt % 2 == 0 else dp1)[:, 0:128],
                                     dwq[:, ib * DW + k * 128:ib * DW + (k + 1) * 128],
                                     ident[:, :]).then_inc(sm["dtp"], 1)
    nc.all_engine_barrier()


def _prep_dequant_matrix(nc, ctx, sems, src, scol_src, outT, NI, W, tag=""):
    import contextlib
    _mctx = contextlib.ExitStack()
    with _mctx:
        sems = [_mctx.enter_context(nc.semaphore(f"pm{tag}_{i}")) for i in range(9)]
        _prep_dequant_body(nc, sems, src, scol_src, outT, NI, W, tag)
    return


def _prep_dequant_body(nc, sems, src, scol_src, outT, NI, W, tag=""):
    s_l0, s_dq, s_tp, s_ev, s_ar0, s_bi, s_sm, s_l1, s_ar1 = sems
    s_l = [s_l0, s_l1]
    s_ar = [s_ar0, s_ar1]
    NK = W // 128
    with (
        nc.sbuf_tensor("p_lw"+tag, [128, 2 * W], I32) as lw,
        nc.sbuf_tensor("p_wq"+tag, [128, NI * W], BF16) as wq,
        nc.sbuf_tensor("p_scol"+tag, [128, NI], F32) as scol,
        nc.sbuf_tensor("p_bcol"+tag, [128, NI], F32) as bcol,
        nc.sbuf_tensor("p_arow"+tag, [128, 2 * NI * 128], BF16) as arow,
        nc.sbuf_tensor("p_ident"+tag, [128, 128], BF16) as ident,
        nc.psum_tensor("p_pst0"+tag, [128, 512], BF16) as pst0,
        nc.psum_tensor("p_pst1"+tag, [128, 512], BF16) as pst1,
        nc.Block() as block,
    ):
        identT = nc.inline_tensor(np.eye(128, dtype=np.float32).astype(
            ml_dtypes.bfloat16), "p_identc"+tag)

        @block.sync
        def _(s):
            s.dma_start(out=ident[:, :], in_=identT.ap()).then_inc(s_sm, 16)
            s.dma_start(out=scol[:, :], in_=scol_src.rearrange("(b p) -> p b", p=128),
                        allow_slow_non_contiguous=True).then_inc(s_sm, 16)
            for ib in range(NI):
                if ib >= 2:
                    s.wait_ge(s_dq, ib - 1)
                s.dma_start(out=lw[:, (ib % 2) * W:(ib % 2 + 1) * W],
                            in_=src[ib * 128:(ib + 1) * 128, :]).then_inc(s_l[ib % 2], 16)

        @block.vector
        def _(v):
            v.wait_ge(s_sm, 32)
            v.tensor_scalar(bcol[:, :], scol[:, :], -128.0, None, ALU.mult).then_inc(s_bi, 1)

        @block.scalar
        def _(sc):
            sc.wait_ge(s_bi, 1)
            for ib in range(NI):
                sc.wait_ge(s_l[ib % 2], (ib // 2 + 1) * 16)
                sc.activation(wq[:, ib * W:(ib + 1) * W],
                              lw[:, (ib % 2) * W:(ib % 2 + 1) * W], AF.Identity,
                              bias=bcol[:, ib:ib + 1],
                              scale=scol[:, ib:ib + 1]).then_inc(s_dq, 1)
            cnt = 0
            for k in range(NK):
                for ib in range(NI):
                    cnt += 1
                    if ib == 0 and k >= 2:
                        sc.wait_ge(s_ar[k % 2], (k // 2) * 16)
                    sc.wait_ge(s_tp, cnt)
                    sc.copy(arow[:, (k % 2) * NI * 128 + ib * 128:
                                 (k % 2) * NI * 128 + (ib + 1) * 128],
                            (pst0 if cnt % 2 == 0 else pst1)[:, 0:128]).then_inc(s_ev, 1)

        @block.tensor
        def _(te):
            te.wait_ge(s_dq, NI)
            cnt = 0
            for k in range(NK):
                for ib in range(NI):
                    cnt += 1
                    if cnt > 2:
                        te.wait_ge(s_ev, cnt - 2)
                    if k >= 2 and ib == 0:
                        pass
                    te.transpose((pst0 if cnt % 2 == 0 else pst1)[:, 0:128],
                                 wq[:, ib * W + k * 128:ib * W + (k + 1) * 128],
                                 ident[:, :]).then_inc(s_tp, 1)

        @block.gpsimd
        def _(g):
            for k in range(NK):
                g.wait_ge(s_ev, (k + 1) * NI)
                g.dma_start(out=outT[k * 128:(k + 1) * 128, :],
                            in_=arow[:, (k % 2) * NI * 128:(k % 2 + 1) * NI * 128]
                            ).then_inc(s_ar[k % 2], 16)
            g.wait_ge(s_ar[0], (NK // 2) * 16)
            g.wait_ge(s_ar[1], (NK // 2) * 16)

    nc.all_engine_barrier()


def _prep_decode_matrix(nc, ctx, sems, src, sc2_src, outT, NI=16, W=2048, tag=""):
    import contextlib
    _mctx = contextlib.ExitStack()
    with _mctx:
        sems = [_mctx.enter_context(nc.semaphore(f"pq{tag}_{i}")) for i in range(9)]
        _prep_decode_body(nc, sems, src, sc2_src, outT, NI, W, tag)
    return


def _prep_decode_body(nc, sems, src, sc2_src, outT, NI=16, W=2048, tag=""):
    s_l0, s_dq, s_tp, s_ev, s_ar0, s_er, s_sm, s_l1, s_ar1 = sems
    s_l = [s_l0, s_l1]
    s_ar = [s_ar0, s_ar1]
    NK = W // 128
    WP = W // 2
    NSC = W // 32
    with (
        nc.sbuf_tensor("q_lw"+tag, [128, 2 * WP], I32) as lw,
        nc.sbuf_tensor("q_wq"+tag, [128, NI * W], BF16) as wq,
        nc.sbuf_tensor("q_s1t"+tag, [128, NI * NSC], F32) as s1t,
        nc.sbuf_tensor("q_lo"+tag, [128, WP], I32) as lo,
        nc.sbuf_tensor("q_hi"+tag, [128, WP], I32) as hi,
        nc.sbuf_tensor("q_t0"+tag, [128, WP], I32) as t0,
        nc.sbuf_tensor("q_ub"+tag, [128, WP], I32) as ub,
        nc.sbuf_tensor("q_sb"+tag, [128, WP], I32) as sb,
        nc.sbuf_tensor("q_err"+tag, [128, WP], F32) as errt,
        nc.sbuf_tensor("q_vm"+tag, [128, WP], F32) as vm,
        nc.sbuf_tensor("q_arow"+tag, [128, 2 * NI * 128], BF16) as arow,
        nc.sbuf_tensor("q_ident"+tag, [128, 128], BF16) as ident,
        nc.psum_tensor("q_pst0"+tag, [128, 512], BF16) as pst0,
        nc.psum_tensor("q_pst1"+tag, [128, 512], BF16) as pst1,
        nc.Block() as block,
    ):
        identT = nc.inline_tensor(np.eye(128, dtype=np.float32).astype(
            ml_dtypes.bfloat16), "q_identc"+tag)
        CH = [0]
        ch = s_dq

        @block.sync
        def _(s):
            s.dma_start(out=ident[:, :], in_=identT.ap()).then_inc(s_sm, 16)
            s.dma_start(out=s1t[:, :].rearrange("p (b n) -> p b n", n=NSC),
                        in_=sc2_src.rearrange("(b p) n -> p b n", p=128)).then_inc(s_sm, 16)
            for ib in range(NI):
                if ib >= 2:
                    s.wait_ge(ch, (ib - 1) * 18)
                s.dma_start(out=lw[:, (ib % 2) * WP:(ib % 2 + 1) * WP],
                            in_=src[ib * 128:(ib + 1) * 128, :]).then_inc(s_l[ib % 2], 16)

        @block.vector
        def _(v):
            v.wait_ge(s_sm, 32)
            for ib in range(NI):
                v.wait_ge(s_l[ib % 2], (ib // 2 + 1) * 16)
                lws = lw[:, (ib % 2) * WP:(ib % 2 + 1) * WP]
                v.tensor_scalar(lo[:, :], lws, 15, None, ALU.bitwise_and).then_inc(ch, 1)
                v.tensor_scalar(hi[:, :], lws, 4, 15, ALU.logical_shift_right,
                                ALU.bitwise_and).then_inc(ch, 1)
                CH[0] += 2
                for pl, csrc in ((0, lo), (1, hi)):
                    v.wait_ge(ch, CH[0])
                    v.tensor_scalar(t0[:, :], csrc[:, :], 7, None,
                                    ALU.bitwise_and).then_inc(ch, 1)
                    CH[0] += 1
                    v.wait_ge(ch, CH[0])
                    v.tensor_scalar(ub[:, :], t0[:, :], 252, 4194304, ALU.add,
                                    ALU.mult).then_inc(ch, 1)
                    v.tensor_scalar(sb[:, :], csrc[:, :], 8, 28, ALU.bitwise_and,
                                    ALU.logical_shift_left).then_inc(ch, 1)
                    CH[0] += 2
                    v.wait_ge(ch, CH[0])
                    v.tensor_scalar(errt[:, :], t0[:, :], -0.25, 0.5, ALU.mult,
                                    ALU.add).then_inc(ch, 1)
                    CH[0] += 1
                    v.wait_ge(ch, CH[0])
                    v.tensor_scalar(errt[:, :], errt[:, :], 0.0, None,
                                    ALU.max).then_inc(ch, 1)
                    CH[0] += 1
                    v.wait_ge(ch, CH[0])
                    v.tensor_tensor(vm[:, :], ub[:, :].bitcast(F32), errt[:, :],
                                    ALU.subtract).then_inc(ch, 1)
                    CH[0] += 1
                    v.wait_ge(ch, CH[0])
                    v.tensor_tensor(vm[:, :].bitcast(I32), vm[:, :].bitcast(I32),
                                    sb[:, :], ALU.bitwise_or).then_inc(ch, 1)
                    CH[0] += 1
                    v.wait_ge(ch, CH[0])
                    sbc = s1t[:, ib * NSC:(ib + 1) * NSC].rearrange(
                        "p (n o) -> p n o", o=1).broadcast_to([128, NSC, 16])
                    ow = wq[:, ib * W + pl:(ib + 1) * W:2].rearrange(
                        "p (n o) -> p n o", o=16)
                    v.tensor_tensor(ow, vm[:, :].rearrange("p (n o) -> p n o", o=16),
                                    sbc, ALU.mult).then_inc(ch, 1)
                    CH[0] += 1

        @block.scalar
        def _(sc):
            tcnt = 0
            for k in range(NK):
                for ib in range(NI):
                    tcnt += 1
                    if ib == 0 and k >= 2:
                        sc.wait_ge(s_ar[k % 2], (k // 2) * 16)
                    sc.wait_ge(s_tp, tcnt)
                    sc.copy(arow[:, (k % 2) * NI * 128 + ib * 128:
                                 (k % 2) * NI * 128 + (ib + 1) * 128],
                            (pst0 if tcnt % 2 == 0 else pst1)[:, 0:128]).then_inc(s_ev, 1)

        @block.tensor
        def _(te):
            te.wait_ge(ch, NI * 18)
            tcnt = 0
            for k in range(NK):
                for ib in range(NI):
                    tcnt += 1
                    if tcnt > 2:
                        te.wait_ge(s_ev, tcnt - 2)
                    te.transpose((pst0 if tcnt % 2 == 0 else pst1)[:, 0:128],
                                 wq[:, ib * W + k * 128:ib * W + (k + 1) * 128],
                                 ident[:, :]).then_inc(s_tp, 1)

        @block.gpsimd
        def _(g):
            for k in range(NK):
                g.wait_ge(s_ev, (k + 1) * NI)
                g.dma_start(out=outT[k * 128:(k + 1) * 128, :],
                            in_=arow[:, (k % 2) * NI * 128:(k % 2 + 1) * NI * 128]
                            ).then_inc(s_ar[k % 2], 16)
            g.wait_ge(s_ar[0], (NK // 2) * 16)
            g.wait_ge(s_ar[1], (NK // 2) * 16)
    nc.all_engine_barrier()


def build_prep():
    nc = bass.Bass("TRN2", target_bir_lowering=False)
    IC0 = 8192 // NC
    IC1 = 16384 // NC
    g0w = nc.dram_tensor("g0w", [IC0, D], I32, kind="ExternalInput").ap()
    g0s = nc.dram_tensor("g0s", [IC0], F32, kind="ExternalInput").ap()
    u0w = nc.dram_tensor("u0w", [IC0, D], I32, kind="ExternalInput").ap()
    u0s = nc.dram_tensor("u0s", [IC0], F32, kind="ExternalInput").ap()
    d0w = nc.dram_tensor("d0w", [D, IC0], I32, kind="ExternalInput").ap()
    d0s = nc.dram_tensor("d0s", [D], F32, kind="ExternalInput").ap()
    g1w = nc.dram_tensor("g1w", [IC1, D // 2], I32, kind="ExternalInput").ap()
    g1s = nc.dram_tensor("g1s", [IC1, D // 32], F32, kind="ExternalInput").ap()
    u1w = nc.dram_tensor("u1w", [IC1, D // 2], I32, kind="ExternalInput").ap()
    u1s = nc.dram_tensor("u1s", [IC1, D // 32], F32, kind="ExternalInput").ap()
    d1w = nc.dram_tensor("d1w", [D, IC1 // 2], I32, kind="ExternalInput").ap()
    d1s = nc.dram_tensor("d1s", [D, IC1 // 32], F32, kind="ExternalInput").ap()
    W0gT = nc.dram_tensor("W0gT", [D, IC0], BF16, kind="ExternalOutput").ap()
    W0uT = nc.dram_tensor("W0uT", [D, IC0], BF16, kind="ExternalOutput").ap()
    W0dT = nc.dram_tensor("W0dT", [IC0, D], BF16, kind="ExternalOutput").ap()
    W1gT = nc.dram_tensor("W1gT", [D, IC1], BF16, kind="ExternalOutput").ap()
    W1uT = nc.dram_tensor("W1uT", [D, IC1], BF16, kind="ExternalOutput").ap()
    W1dT = nc.dram_tensor("W1dT", [IC1, D], BF16, kind="ExternalOutput").ap()

    import contextlib
    ctx = contextlib.ExitStack()
    with ctx:
        sems = [ctx.enter_context(nc.semaphore(f"p_s{i}")) for i in range(6)]
        only = os.environ.get("S0_ONLY", "")
        def want(t):
            return (not only) or t in only.split(",")
        if want("g0"):
            _prep_dequant_matrix(nc, ctx, sems, g0w, g0s, W0gT, NI=IC0 // 128, W=D, tag="g0")
        if want("u0"):
            _prep_dequant_matrix(nc, ctx, sems, u0w, u0s, W0uT, NI=IC0 // 128, W=D, tag="u0")
        if want("d0"):
            _prep_dequant_matrix(nc, ctx, sems, d0w, d0s, W0dT, NI=D // 128, W=IC0, tag="d0")
        if want("g1"):
            _prep_decode_matrix(nc, ctx, sems, g1w, g1s, W1gT, NI=IC1 // 128, W=D, tag="g1")
        if want("u1"):
            _prep_decode_matrix(nc, ctx, sems, u1w, u1s, W1uT, NI=IC1 // 128, W=D, tag="u1")
        if want("d1"):
            _prep_decode_matrix(nc, ctx, sems, d1w, d1s, W1dT, NI=D // 128, W=IC1, tag="d1")
    return nc


def prep_inputs(inp):
    IC0 = 8192 // NC
    IC1 = 16384 // NC
    maps = []
    for c in range(NC):
        i0 = slice(c * IC0, (c + 1) * IC0)
        i1 = slice(c * IC1, (c + 1) * IC1)
        pk1 = slice(c * IC1 // 2, (c + 1) * IC1 // 2)
        sc1 = slice(c * IC1 // 32, (c + 1) * IC1 // 32)
        maps.append({
            "g0w": np.ascontiguousarray(inp["L0_ffn_gate_w"][i0, :]),
            "g0s": np.ascontiguousarray(inp["L0_ffn_gate_scale"][i0]),
            "u0w": np.ascontiguousarray(inp["L0_ffn_up_w"][i0, :]),
            "u0s": np.ascontiguousarray(inp["L0_ffn_up_scale"][i0]),
            "d0w": np.ascontiguousarray(inp["L0_ffn_down_w"][:, i0]),
            "d0s": np.ascontiguousarray(inp["L0_ffn_down_scale"]),
            "g1w": np.ascontiguousarray(inp["L1_ffn_gate_W_packed"][i1, :]),
            "g1s": np.ascontiguousarray(inp["L1_ffn_gate_scales"][i1, :]),
            "u1w": np.ascontiguousarray(inp["L1_ffn_up_W_packed"][i1, :]),
            "u1s": np.ascontiguousarray(inp["L1_ffn_up_scales"][i1, :]),
            "d1w": np.ascontiguousarray(inp["L1_ffn_down_W_packed"][:, pk1]),
            "d1s": np.ascontiguousarray(inp["L1_ffn_down_scales"][:, sc1]),
        })
    return maps



def emit_ffn(nc, layer, ypre_in, WgT, WuT, WdT, out_dram, I, final_ln, x1_dram,
             w_views=None, w_ready=None):
    tag = f"f{layer}"
    NTB = TC // 128
    NK = D // 128
    IC = I // 512
    import contextlib
    ctx = contextlib.ExitStack()
    with ctx:
        x1T = ctx.enter_context(nc.sbuf_tensor("x1T" + tag, [128, NK * TC], BF16))
        s_ev = ctx.enter_context(nc.semaphore("sev" + tag))
        s_ya = ctx.enter_context(nc.semaphore("sya" + tag))
        ya = ctx.enter_context(nc.sbuf_tensor("ya" + tag, [128, NTB * D], F32))
        identb = nc.inline_tensor(np.eye(128, dtype=np.float32).astype(
            ml_dtypes.bfloat16), "idb" + tag)

        with (
            nc.sbuf_tensor("yp" + tag, [128, 2 * D], F32) as yp,
            nc.sbuf_tensor("sq" + tag, [128, D], F32) as sq,
            nc.sbuf_tensor("x1f" + tag, [128, 2 * D], F32) as x1f,
            nc.sbuf_tensor("x1b" + tag, [128, D], BF16) as x1b,
            nc.sbuf_tensor("st" + tag, [128, 8], F32) as st,
            nc.sbuf_tensor("rstd" + tag, [128, 1], F32) as rstd,
            nc.sbuf_tensor("idl" + tag, [128, 128], BF16) as idl,
            nc.psum_tensor("pt0" + tag, [128, 512], BF16) as pt0,
            nc.psum_tensor("pt1" + tag, [128, 512], BF16) as pt1,
            nc.semaphore("sl0" + tag) as s_l0,
            nc.semaphore("sl1" + tag) as s_l1,
            nc.semaphore("ssq" + tag) as s_sq,
            nc.semaphore("sst" + tag) as s_st,
            nc.semaphore("sx1" + tag) as s_x1,
            nc.semaphore("sxb" + tag) as s_xb,
            nc.semaphore("stp" + tag) as s_tp,
            nc.semaphore("sxd" + tag) as s_xd,
            nc.semaphore("sidl" + tag) as s_idl,
            nc.Block() as block,
        ):
            s_l = [s_l0, s_l1]

            @block.sync
            def _(s):
                s.dma_start(out=idl[:, :], in_=identb.ap()).then_inc(s_idl, 16)
                for tb in range(NTB):
                    if tb >= 2:
                        s.wait_ge(s_x1, tb - 1)
                    s.dma_start(out=yp[:, (tb % 2) * D:(tb % 2 + 1) * D],
                                in_=ypre_in[tb * 128:(tb + 1) * 128, :]).then_inc(
                                    s_l[tb % 2], 16)

            @block.scalar
            def _(sc):
                for tb in range(NTB):
                    yps = yp[:, (tb % 2) * D:(tb % 2 + 1) * D]
                    sc.wait_ge(s_l[tb % 2], (tb // 2 + 1) * 16)
                    if tb > 0:
                        sc.wait_ge(s_st, tb * 9)
                    sc.activation(sq[:, :], yps, AF.Square).then_inc(s_sq, 1)
                    sc.wait_ge(s_st, tb * 9 + 7)
                    sc.sqrt(st[:, 6:7], st[:, 5:6]).then_inc(s_sq, 1)
                    sc.wait_ge(s_st, tb * 9 + 9)
                    if tb >= 2:
                        sc.wait_ge(s_xd, tb * 16)
                    sc.activation(x1f[:, (tb % 2) * D:(tb % 2 + 1) * D], yps,
                                  AF.Identity, bias=st[:, 7:8],
                                  scale=rstd[:, 0:1]).then_inc(s_x1, 1)

            @block.vector
            def _(v):
                for tb in range(NTB):
                    yps = yp[:, (tb % 2) * D:(tb % 2 + 1) * D]
                    v.wait_ge(s_l[tb % 2], (tb // 2 + 1) * 16)
                    if tb > 0:
                        v.wait_ge(s_x1, tb)
                    v.reduce_sum(st[:, 0:1], yps, mybir.AxisListType.X).then_inc(s_st, 1)
                    v.wait_ge(s_sq, 2 * tb + 1)
                    v.reduce_sum(st[:, 1:2], sq[:, :], mybir.AxisListType.X).then_inc(s_st, 1)
                    v.wait_ge(s_st, tb * 9 + 2)
                    v.tensor_scalar(st[:, 2:3], st[:, 0:1], 1.0 / D, None, ALU.mult).then_inc(s_st, 1)
                    v.tensor_scalar(st[:, 3:4], st[:, 1:2], 1.0 / D, None, ALU.mult).then_inc(s_st, 1)
                    v.wait_ge(s_st, tb * 9 + 4)
                    v.tensor_tensor(st[:, 4:5], st[:, 2:3], st[:, 2:3], ALU.mult).then_inc(s_st, 1)
                    v.wait_ge(s_st, tb * 9 + 5)
                    v.tensor_tensor(st[:, 5:6], st[:, 3:4], st[:, 4:5], ALU.subtract).then_inc(s_st, 1)
                    v.wait_ge(s_st, tb * 9 + 6)
                    v.tensor_scalar(st[:, 5:6], st[:, 5:6], 1e-5, None, ALU.add).then_inc(s_st, 1)
                    v.wait_ge(s_sq, 2 * tb + 2)
                    v.reciprocal(rstd[:, 0:1], st[:, 6:7]).then_inc(s_st, 1)
                    v.wait_ge(s_st, tb * 9 + 8)
                    v.tensor_scalar(st[:, 7:8], st[:, 2:3], -1.0, rstd[:, 0:1],
                                    ALU.mult, ALU.mult).then_inc(s_st, 1)
                    v.wait_ge(s_x1, tb + 1)
                    if tb > 0:
                        v.wait_ge(s_tp, tb * NK)
                    v.tensor_copy(x1b[:, :], x1f[:, (tb % 2) * D:(tb % 2 + 1) * D]
                                  ).then_inc(s_xb, 1)
                    for k in range(NK):
                        cntv = tb * NK + k + 1
                        v.wait_ge(s_tp, cntv)
                        v.tensor_copy(x1T[:, k * TC + tb * 128:k * TC + (tb + 1) * 128],
                                      (pt0 if cntv % 2 == 0 else pt1)[:, 0:128]
                                      ).then_inc(s_ev, 1)

            @block.gpsimd
            def _(g):
                for tb in range(NTB):
                    g.wait_ge(s_x1, tb + 1)
                    g.dma_start(out=x1_dram[tb * 128:(tb + 1) * 128, :],
                                in_=x1f[:, (tb % 2) * D:(tb % 2 + 1) * D]).then_inc(s_xd, 16)
                g.wait_ge(s_xd, NTB * 16)

            @block.tensor
            def _(te):
                te.wait_ge(s_idl, 16)
                cnt = 0
                for tb in range(NTB):
                    te.wait_ge(s_xb, tb + 1)
                    for k in range(NK):
                        cnt += 1
                        if cnt > 2:
                            te.wait_ge(s_ev, cnt - 2)
                        te.transpose((pt0 if cnt % 2 == 0 else pt1)[:, 0:128],
                                     x1b[:, k * 128:(k + 1) * 128],
                                     idl[:, :]).then_inc(s_tp, 1)

        nc.all_engine_barrier()

        NSUB = 4
        TH = TC // 512
        import contextlib as _ct
        _fctx = _ct.ExitStack()
        with _fctx:
            _e = _fctx.enter_context
            wg = _e(nc.sbuf_tensor("wg" + tag, [128, 2 * NK * 512], BF16))
            wu = _e(nc.sbuf_tensor("wu" + tag, [128, 2 * NK * 512], BF16))
            wd = _e(nc.sbuf_tensor("wd" + tag, [128, 4 * D], BF16))
            sgt = _e(nc.sbuf_tensor("sg" + tag, [128, NSUB * TC], BF16))
            uet = _e(nc.sbuf_tensor("ue" + tag, [128, NSUB * TC], BF16))
            itT = _e(nc.sbuf_tensor("itT" + tag, [128, NSUB * TC], BF16))
            if layer == 1:
                aat = _e(nc.sbuf_tensor("aa" + tag, [128, TC], BF16))
                bbt = _e(nc.sbuf_tensor("bb" + tag, [128, TC], BF16))
            pg = _e(nc.psum_tensor("pg" + tag, [128, TC], F32))
            pu = _e(nc.psum_tensor("pu" + tag, [128, TC], F32))
            py0 = _e(nc.psum_tensor("py0" + tag, [128, D // 2], F32))
            py1 = _e(nc.psum_tensor("py1" + tag, [128, D // 2], F32))
            pys = [py0, py1]
            s_w0 = _e(nc.semaphore("sw0" + tag))
            s_w1 = _e(nc.semaphore("sw1" + tag))
            s_wd = _e(nc.semaphore("swd" + tag))
            s_mg = _e(nc.semaphore("smg" + tag))
            s_mu = _e(nc.semaphore("smu" + tag))
            s_my = _e(nc.semaphore("smy" + tag))
            s_ge = _e(nc.semaphore("sge" + tag))
            s_ue = _e(nc.semaphore("sue" + tag))
            s_it = _e(nc.semaphore("sit" + tag))
            s_qq = _e(nc.semaphore("sqq" + tag))
            s_sg = _e(nc.semaphore("ssg" + tag))
            ch_v = _e(nc.semaphore("svch" + tag))
            block = _e(nc.Block())
            s_w = [s_w0, s_w1]

            @block.sync
            def _(s):
                if w_ready is not None:
                    s.wait_ge(w_ready[0], w_ready[1])
                for ic in range(IC):
                    if ic >= 2:
                        s.wait_ge(s_mg, (ic - 1) * NSUB * TH)
                        s.wait_ge(s_mu, (ic - 1) * NSUB * TH)
                    if w_views is None:
                        gsrc = WgT.rearrange("(k p) i -> p k i", p=128)[
                            :, :, ic * 512:(ic + 1) * 512]
                        usrc = WuT.rearrange("(k p) i -> p k i", p=128)[
                            :, :, ic * 512:(ic + 1) * 512]
                    else:
                        gsrc, usrc = w_views(ic)
                    s.dma_start(
                        out=wg[:, (ic % 2) * NK * 512:(ic % 2 + 1) * NK * 512].rearrange(
                            "p (k i) -> p k i", k=NK),
                        in_=gsrc).then_inc(s_w[ic % 2], 16)
                    s.dma_start(
                        out=wu[:, (ic % 2) * NK * 512:(ic % 2 + 1) * NK * 512].rearrange(
                            "p (k i) -> p k i", k=NK),
                        in_=usrc).then_inc(s_w[ic % 2], 16)

            @block.gpsimd
            def _(g):
                for ic in range(IC):
                    if ic >= 1:
                        g.wait_ge(s_my, ic * NTB * 2)
                    g.dma_start(
                        out=wd[:, :].rearrange("p (r h) -> p r h", r=4),
                        in_=WdT[ic * 512:(ic + 1) * 512, :].rearrange(
                            "(r p) h -> p r h", p=128)).then_inc(s_wd, 16)

            @block.tensor
            def _(te):
                def down(j):
                    te.wait_ge(s_it, (j + 1) * NSUB)
                    te.wait_ge(s_wd, (j + 1) * 16)
                    for tb in range(NTB):
                        for dh in range(2):
                            g = (j * NTB + tb) * 2 + dh
                            if g > 1:
                                te.wait_ge(s_ya, g - 1)
                            for isub in range(NSUB):
                                for hc in range(2):
                                    mm = te.matmul(
                                        pys[g % 2][:, hc * 512:(hc + 1) * 512],
                                        itT[:, isub * TC + tb * 128:
                                            isub * TC + (tb + 1) * 128],
                                        wd[:, isub * D + dh * 1024 + hc * 512:
                                           isub * D + dh * 1024 + (hc + 1) * 512],
                                        start=(isub == 0), stop=(isub == NSUB - 1))
                            mm.then_inc(s_my, 1)
                for ic in range(IC):
                    for isub in range(NSUB):
                        g_ = ic * NSUB + isub
                        if isub == 0:
                            te.wait_ge(s_w[ic % 2], (ic // 2 + 1) * 32)
                        if g_ == 0:
                            te.wait_ge(s_ev, NTB * NK)
                        if g_ > 0:
                            te.wait_ge(s_ge, g_)
                        wslot = (ic % 2) * NK * 512
                        for k in range(NK):
                            for th in range(TH):
                                mm = te.matmul(
                                    pg[:, th * 512:(th + 1) * 512],
                                    wg[:, wslot + k * 512 + isub * 128:
                                       wslot + k * 512 + (isub + 1) * 128],
                                    x1T[:, k * TC + th * 512:k * TC + (th + 1) * 512],
                                    start=(k == 0), stop=(k == NK - 1))
                                if k == NK - 1:
                                    mm.then_inc(s_mg, 1)
                        if g_ > 0:
                            te.wait_ge(s_ue, g_)
                        for k in range(NK):
                            for th in range(TH):
                                mm = te.matmul(
                                    pu[:, th * 512:(th + 1) * 512],
                                    wu[:, wslot + k * 512 + isub * 128:
                                       wslot + k * 512 + (isub + 1) * 128],
                                    x1T[:, k * TC + th * 512:k * TC + (th + 1) * 512],
                                    start=(k == 0), stop=(k == NK - 1))
                                if k == NK - 1:
                                    mm.then_inc(s_mu, 1)
                    if ic >= 1:
                        down(ic - 1)
                down(IC - 1)

            @block.scalar
            def _(sc):
                def sig(j, isub):
                    sc.wait_ge(s_qq, j * NSUB + isub + 1)
                    sc.activation(bbt[:, :], aat[:, :],
                                  AF.Sigmoid).then_inc(s_sg, 1)

                def evac(ic, isub):
                    g_ = ic * NSUB + isub
                    sc.wait_ge(s_mg, g_ * TH + TH)
                    if ic >= 1:
                        sc.wait_ge(s_it, (ic - 1) * NSUB + isub + 1)
                    if layer == 0:
                        sc.activation(sgt[:, isub * TC:(isub + 1) * TC],
                                      pg[:, :], AF.Silu).then_inc(s_ge, 1)
                    else:
                        sc.copy(sgt[:, isub * TC:(isub + 1) * TC],
                                pg[:, :]).then_inc(s_ge, 1)
                for ic in range(IC):
                    if layer == 1 and ic >= 1:
                        sig(ic - 1, 0)
                    evac(ic, 0)
                    if layer == 1 and ic >= 1:
                        sig(ic - 1, 1)
                        sig(ic - 1, 2)
                        sig(ic - 1, 3)
                    evac(ic, 1)
                    evac(ic, 2)
                    evac(ic, 3)
                if layer == 1:
                    for isub in range(NSUB):
                        sig(IC - 1, isub)

            CHV = [0]

            @block.vector
            def _(v):
                def accum(j):
                    for tb in range(NTB):
                        for dh in range(2):
                            g = (j * NTB + tb) * 2 + dh
                            v.wait_ge(s_my, g + 1)
                            if j >= 1:
                                v.wait_ge(s_ya, g - NTB * 2 + 1)
                            ysl = ya[:, tb * D + dh * 1024:tb * D + (dh + 1) * 1024]
                            if j == 0:
                                v.tensor_copy(ysl, pys[g % 2][:, :]).then_inc(s_ya, 1)
                            else:
                                v.tensor_tensor(ysl, ysl, pys[g % 2][:, :],
                                                ALU.add).then_inc(s_ya, 1)
                for ic in range(IC):
                    for isub in range(NSUB):
                        g_ = ic * NSUB + isub
                        v.wait_ge(s_mu, g_ * TH + TH)
                        if ic >= 1:
                            v.wait_ge(s_it, (ic - 1) * NSUB + isub + 1)
                        v.tensor_copy(uet[:, isub * TC:(isub + 1) * TC],
                                      pu[:, :]).then_inc(s_ue, 1)
                    if ic >= 1:
                        accum(ic - 1)
                    v.wait_ge(s_my, ic * NTB * 2)
                    for isub in range(NSUB):
                        g_ = ic * NSUB + isub
                        sgs = sgt[:, isub * TC:(isub + 1) * TC]
                        ues = uet[:, isub * TC:(isub + 1) * TC]
                        its = itT[:, isub * TC:(isub + 1) * TC]
                        if layer == 0:
                            v.wait_ge(s_ge, g_ + 1)
                            v.tensor_tensor(its, sgs, ues, ALU.mult).then_inc(s_it, 1)
                        else:
                            v.wait_ge(s_ge, g_ + 1)
                            v.tensor_tensor(aat[:, :], sgs, sgs, ALU.mult).then_inc(ch_v, 1)
                            CHV[0] += 1
                            v.wait_ge(ch_v, CHV[0])
                            v.tensor_scalar(bbt[:, :], aat[:, :], 0.07135481627,
                                            1.595769122, ALU.mult, ALU.add).then_inc(ch_v, 1)
                            CHV[0] += 1
                            v.wait_ge(ch_v, CHV[0])
                            v.tensor_tensor(aat[:, :], sgs, bbt[:, :],
                                            ALU.mult).then_inc(s_qq, 1)
                            v.wait_ge(s_sg, g_ + 1)
                            v.tensor_tensor(aat[:, :], sgs, bbt[:, :],
                                            ALU.mult).then_inc(ch_v, 1)
                            CHV[0] += 1
                            v.wait_ge(ch_v, CHV[0])
                            v.tensor_tensor(its, aat[:, :], ues, ALU.mult).then_inc(s_it, 1)
                accum(IC - 1)
        nc.all_engine_barrier()

        with (
            nc.sbuf_tensor("x1r" + tag, [128, 2 * D], F32) as x1r,
            nc.sbuf_tensor("rr" + tag, [128, 2 * D], F32) as rr,
            nc.sbuf_tensor("ro" + tag, [128, 2 * D], F32) as ro,
            nc.sbuf_tensor("sqR" + tag, [128, D], F32) as sqR,
            nc.sbuf_tensor("stR" + tag, [128, 8], F32) as stR,
            nc.sbuf_tensor("rstdR" + tag, [128, 1], F32) as rstdR,
            nc.semaphore("srl0" + tag) as s_rl0,
            nc.semaphore("srl1" + tag) as s_rl1,
            nc.semaphore("sra" + tag) as s_ra,
            nc.semaphore("srs" + tag) as s_rs,
            nc.semaphore("srq" + tag) as s_rq,
            nc.semaphore("sro" + tag) as s_ro,
            nc.semaphore("sfin" + tag) as s_fin,
            nc.Block() as block,
        ):
            s_rl = [s_rl0, s_rl1]

            @block.sync
            def _(s):
                for tb in range(NTB):
                    if tb >= 2:
                        s.wait_ge(s_ra, tb - 1)
                    s.dma_start(out=x1r[:, (tb % 2) * D:(tb % 2 + 1) * D],
                                in_=x1_dram[tb * 128:(tb + 1) * 128, :]).then_inc(
                                    s_rl[tb % 2], 16)

            @block.vector
            def _(v):
                for tb in range(NTB):
                    v.wait_ge(s_rl[tb % 2], (tb // 2 + 1) * 16)
                    if tb >= 2:
                        v.wait_ge(s_ro, tb * 16)
                    v.tensor_tensor(rr[:, (tb % 2) * D:(tb % 2 + 1) * D],
                                    x1r[:, (tb % 2) * D:(tb % 2 + 1) * D],
                                    ya[:, tb * D:(tb + 1) * D], ALU.add).then_inc(s_ra, 1)
                    if final_ln:
                        rrs = rr[:, (tb % 2) * D:(tb % 2 + 1) * D]
                        if tb > 0:
                            v.wait_ge(s_fin, tb)
                        v.wait_ge(s_ra, tb + 1)
                        v.reduce_sum(stR[:, 0:1], rrs, mybir.AxisListType.X).then_inc(s_rs, 1)
                        v.wait_ge(s_rq, 2 * tb + 1)
                        v.reduce_sum(stR[:, 1:2], sqR[:, :], mybir.AxisListType.X).then_inc(s_rs, 1)
                        v.wait_ge(s_rs, tb * 9 + 2)
                        v.tensor_scalar(stR[:, 2:3], stR[:, 0:1], 1.0 / D, None, ALU.mult).then_inc(s_rs, 1)
                        v.tensor_scalar(stR[:, 3:4], stR[:, 1:2], 1.0 / D, None, ALU.mult).then_inc(s_rs, 1)
                        v.wait_ge(s_rs, tb * 9 + 4)
                        v.tensor_tensor(stR[:, 4:5], stR[:, 2:3], stR[:, 2:3], ALU.mult).then_inc(s_rs, 1)
                        v.wait_ge(s_rs, tb * 9 + 5)
                        v.tensor_tensor(stR[:, 5:6], stR[:, 3:4], stR[:, 4:5], ALU.subtract).then_inc(s_rs, 1)
                        v.wait_ge(s_rs, tb * 9 + 6)
                        v.tensor_scalar(stR[:, 5:6], stR[:, 5:6], 1e-5, None, ALU.add).then_inc(s_rs, 1)
                        v.wait_ge(s_rq, 2 * tb + 2)
                        v.reciprocal(rstdR[:, 0:1], stR[:, 6:7]).then_inc(s_rs, 1)
                        v.wait_ge(s_rs, tb * 9 + 8)
                        v.tensor_scalar(stR[:, 7:8], stR[:, 2:3], -1.0, rstdR[:, 0:1],
                                        ALU.mult, ALU.mult).then_inc(s_rs, 1)

            if final_ln:
                @block.scalar
                def _(sc):
                    for tb in range(NTB):
                        rrs = rr[:, (tb % 2) * D:(tb % 2 + 1) * D]
                        sc.wait_ge(s_ra, tb + 1)
                        if tb > 0:
                            sc.wait_ge(s_rs, tb * 9)
                        sc.activation(sqR[:, :], rrs, AF.Square).then_inc(s_rq, 1)
                        sc.wait_ge(s_rs, tb * 9 + 7)
                        sc.sqrt(stR[:, 6:7], stR[:, 5:6]).then_inc(s_rq, 1)
                        sc.wait_ge(s_rs, tb * 9 + 9)
                        if tb >= 2:
                            sc.wait_ge(s_ro, tb * 16)
                        sc.activation(ro[:, (tb % 2) * D:(tb % 2 + 1) * D], rrs,
                                      AF.Identity, bias=stR[:, 7:8],
                                      scale=rstdR[:, 0:1]).then_inc(s_fin, 1)

            @block.gpsimd
            def _(g):
                for tb in range(NTB):
                    if final_ln:
                        g.wait_ge(s_fin, tb + 1)
                        srcap = ro[:, (tb % 2) * D:(tb % 2 + 1) * D]
                    else:
                        g.wait_ge(s_ra, tb + 1)
                        srcap = rr[:, (tb % 2) * D:(tb % 2 + 1) * D]
                    g.dma_start(out=out_dram[tb * 128:(tb + 1) * 128, :],
                                in_=srcap).then_inc(s_ro, 16)
                g.wait_ge(s_ro, NTB * 16)
        nc.all_engine_barrier()


def build_ffn(layer):
    nc = bass.Bass("TRN2", target_bir_lowering=False)
    I = 8192 if layer == 0 else 16384
    ypre = nc.dram_tensor("ypre", [TC, D], F32, kind="ExternalInput").ap()
    WgT = nc.dram_tensor("WgT", [D, I], BF16, kind="ExternalInput").ap()
    WuT = nc.dram_tensor("WuT", [D, I], BF16, kind="ExternalInput").ap()
    WdT = nc.dram_tensor("WdT", [I, D], BF16, kind="ExternalInput").ap()
    outd = nc.dram_tensor("outd", [TC, D], F32, kind="ExternalOutput").ap()
    x1_dram = nc.dram_tensor("x1_dram", [TC, D], F32, kind="Internal")
    emit_ffn(nc, layer, ypre, WgT, WuT, WdT, outd, I, final_ln=(layer == 1),
             x1_dram=x1_dram.ap())
    return nc



def emit_attn(nc, ypre_in, out_dram):
    tag = "a3"
    NTB = TC // 128
    NK = D // 128
    NSEG = TC // 512
    H = 16
    SC = 0.08838834764831845
    import contextlib
    ctx = contextlib.ExitStack()
    with ctx:
        _e = ctx.enter_context
        x2f = _e(nc.sbuf_tensor("x2f" + tag, [128, NTB * D], F32))
        x2T = _e(nc.sbuf_tensor("x2T" + tag, [128, NK * TC], BF16))
        s_ev = _e(nc.semaphore("sev" + tag))

        perms = {}
        for p_ in range(2):
            for hf in range(2):
                m = np.zeros((128, 128), np.float32)
                for qp in range(64 * hf, 64 * (hf + 1)):
                    m[qp, 2 * (qp - 64 * hf) + p_] = 1.0
                perms[(p_, hf)] = nc.inline_tensor(
                    m.astype(ml_dtypes.bfloat16), f"perm{p_}{hf}" + tag)
        identb = nc.inline_tensor(np.eye(128, dtype=np.float32).astype(
            ml_dtypes.bfloat16), "ida" + tag)

        with (
            nc.sbuf_tensor("yp" + tag, [128, 2 * D], F32) as yp,
            nc.sbuf_tensor("sq" + tag, [128, D], F32) as sq,
            nc.sbuf_tensor("x1b" + tag, [128, D], BF16) as x1b,
            nc.sbuf_tensor("st" + tag, [128, 8], F32) as st,
            nc.sbuf_tensor("rstd" + tag, [128, 1], F32) as rstd,
            nc.sbuf_tensor("idl" + tag, [128, 128], BF16) as idl,
            nc.psum_tensor("pt0" + tag, [128, 512], BF16) as pt0,
            nc.psum_tensor("pt1" + tag, [128, 512], BF16) as pt1,
            nc.semaphore("sl0" + tag) as s_l0,
            nc.semaphore("sl1" + tag) as s_l1,
            nc.semaphore("ssq" + tag) as s_sq,
            nc.semaphore("sst" + tag) as s_st,
            nc.semaphore("sx1" + tag) as s_x1,
            nc.semaphore("sxb" + tag) as s_xb,
            nc.semaphore("stp" + tag) as s_tp,
            nc.semaphore("sidl" + tag) as s_idl,
            nc.Block() as block,
        ):
            s_l = [s_l0, s_l1]

            @block.sync
            def _(s):
                s.dma_start(out=idl[:, :], in_=identb.ap()).then_inc(s_idl, 16)
                for tb in range(NTB):
                    if tb >= 2:
                        s.wait_ge(s_x1, tb - 1)
                    s.dma_start(out=yp[:, (tb % 2) * D:(tb % 2 + 1) * D],
                                in_=ypre_in[tb * 128:(tb + 1) * 128, :]).then_inc(
                                    s_l[tb % 2], 16)

            @block.scalar
            def _(sc):
                for tb in range(NTB):
                    yps = yp[:, (tb % 2) * D:(tb % 2 + 1) * D]
                    sc.wait_ge(s_l[tb % 2], (tb // 2 + 1) * 16)
                    if tb > 0:
                        sc.wait_ge(s_st, tb * 9)
                    sc.activation(sq[:, :], yps, AF.Square).then_inc(s_sq, 1)
                    sc.wait_ge(s_st, tb * 9 + 7)
                    sc.sqrt(st[:, 6:7], st[:, 5:6]).then_inc(s_sq, 1)
                    sc.wait_ge(s_st, tb * 9 + 9)
                    sc.activation(x2f[:, tb * D:(tb + 1) * D], yps,
                                  AF.Identity, bias=st[:, 7:8],
                                  scale=rstd[:, 0:1]).then_inc(s_x1, 1)

            @block.vector
            def _(v):
                for tb in range(NTB):
                    yps = yp[:, (tb % 2) * D:(tb % 2 + 1) * D]
                    v.wait_ge(s_l[tb % 2], (tb // 2 + 1) * 16)
                    if tb > 0:
                        v.wait_ge(s_x1, tb)
                    v.reduce_sum(st[:, 0:1], yps, mybir.AxisListType.X).then_inc(s_st, 1)
                    v.wait_ge(s_sq, 2 * tb + 1)
                    v.reduce_sum(st[:, 1:2], sq[:, :], mybir.AxisListType.X).then_inc(s_st, 1)
                    v.wait_ge(s_st, tb * 9 + 2)
                    v.tensor_scalar(st[:, 2:3], st[:, 0:1], 1.0 / D, None, ALU.mult).then_inc(s_st, 1)
                    v.tensor_scalar(st[:, 3:4], st[:, 1:2], 1.0 / D, None, ALU.mult).then_inc(s_st, 1)
                    v.wait_ge(s_st, tb * 9 + 4)
                    v.tensor_tensor(st[:, 4:5], st[:, 2:3], st[:, 2:3], ALU.mult).then_inc(s_st, 1)
                    v.wait_ge(s_st, tb * 9 + 5)
                    v.tensor_tensor(st[:, 5:6], st[:, 3:4], st[:, 4:5], ALU.subtract).then_inc(s_st, 1)
                    v.wait_ge(s_st, tb * 9 + 6)
                    v.tensor_scalar(st[:, 5:6], st[:, 5:6], 1e-5, None, ALU.add).then_inc(s_st, 1)
                    v.wait_ge(s_sq, 2 * tb + 2)
                    v.reciprocal(rstd[:, 0:1], st[:, 6:7]).then_inc(s_st, 1)
                    v.wait_ge(s_st, tb * 9 + 8)
                    v.tensor_scalar(st[:, 7:8], st[:, 2:3], -1.0, rstd[:, 0:1],
                                    ALU.mult, ALU.mult).then_inc(s_st, 1)
                    v.wait_ge(s_x1, tb + 1)
                    if tb > 0:
                        v.wait_ge(s_tp, tb * NK)
                    v.tensor_copy(x1b[:, :], x2f[:, tb * D:(tb + 1) * D]).then_inc(s_xb, 1)
                    for k in range(NK):
                        cntv = tb * NK + k + 1
                        v.wait_ge(s_tp, cntv)
                        v.tensor_copy(x2T[:, k * TC + tb * 128:k * TC + (tb + 1) * 128],
                                      (pt0 if cntv % 2 == 0 else pt1)[:, 0:128]
                                      ).then_inc(s_ev, 1)

            @block.tensor
            def _(te):
                te.wait_ge(s_idl, 16)
                cnt = 0
                for tb in range(NTB):
                    te.wait_ge(s_xb, tb + 1)
                    for k in range(NK):
                        cnt += 1
                        if cnt > 2:
                            te.wait_ge(s_ev, cnt - 2)
                        te.transpose((pt0 if cnt % 2 == 0 else pt1)[:, 0:128],
                                     x1b[:, k * 128:(k + 1) * 128],
                                     idl[:, :]).then_inc(s_tp, 1)

        nc.all_engine_barrier()

        import contextlib as _ct
        _actx = _ct.ExitStack()
        with _actx:
            _a = _actx.enter_context
            pm = _a(nc.sbuf_tensor("pm" + tag, [128, 4 * 128], BF16))
            ida2 = _a(nc.sbuf_tensor("ida2" + tag, [128, 128], BF16))
            pb = _a(nc.sbuf_tensor("pb" + tag, [128, 2 * 256], BF16))
            atn = _a(nc.sbuf_tensor("atn" + tag, [128, 2 * 256], BF16))
            atT = _a(nc.sbuf_tensor("atT" + tag, [128, 2 * 256], BF16))
            gk = _a(nc.sbuf_tensor("gk" + tag, [128, 2 * 128], BF16))
            ob = _a(nc.sbuf_tensor("ob" + tag, [128, 2 * 128], BF16))
            sm = _a(nc.sbuf_tensor("sm" + tag, [128, 6], F32))
            attsb = _a(nc.sbuf_tensor("att" + tag, [128, NTB * D], F32))
            pss0 = _a(nc.psum_tensor("pss0" + tag, [128, 256], F32))
            pss1 = _a(nc.psum_tensor("pss1" + tag, [128, 256], F32))
            ptr0 = _a(nc.psum_tensor("ptr0" + tag, [128, 128], BF16))
            ptr1 = _a(nc.psum_tensor("ptr1" + tag, [128, 128], BF16))
            pso = _a(nc.psum_tensor("pso" + tag, [128, 128], F32))
            ppm = _a(nc.psum_tensor("ppm" + tag, [128, 128], F32))
            s_pm = _a(nc.semaphore("spm" + tag))
            s_ms = _a(nc.semaphore("sms" + tag))
            s_sx = _a(nc.semaphore("ssx" + tag))
            ch = _a(nc.semaphore("svx" + tag))
            s_tr = _a(nc.semaphore("str" + tag))
            s_te = _a(nc.semaphore("ste" + tag))
            s_mo = _a(nc.semaphore("smo" + tag))
            s_oe = _a(nc.semaphore("soe" + tag))
            s_mp = _a(nc.semaphore("smp" + tag))
            s_pc = _a(nc.semaphore("spc" + tag))
            s_ro = _a(nc.semaphore("sro" + tag))
            s_ra = _a(nc.semaphore("sraa" + tag))
            block = _a(nc.Block())

            @block.sync
            def _(s):
                s.dma_start(out=ida2[:, :], in_=identb.ap()).then_inc(s_pm, 16)
                for i, (p_, hf) in enumerate([(0, 0), (0, 1), (1, 0), (1, 1)]):
                    s.dma_start(out=pm[:, i * 128:(i + 1) * 128],
                                in_=perms[(p_, hf)].ap()).then_inc(s_pm, 16)

            @block.tensor
            def _(te):
                te.wait_ge(s_pm, 80)
                trc = 0
                for s_ in range(NSEG):
                    for h in range(H):
                        idx = s_ * H + h
                        p_ = h % 2
                        base = h * TC + s_ * 512 + p_
                        gTs = [x2T[:, base + 2 * qh * 128:base + 2 * (qh + 1) * 128 - 1:2]
                               for qh in range(2)]
                        gTfull = x2T[:, base:base + 511:2]
                        for qh in range(2):
                            if idx >= 1:
                                te.wait_ge(ch, (idx - 1) * 10 + 5 * qh + 1)
                                te.wait_ge(s_sx, 2 * (idx - 1) + qh + 1)
                            te.matmul((pss0 if qh == 0 else pss1)[:, :], gTs[qh],
                                      gTfull, start=True, stop=True).then_inc(s_ms, 1)
                        te.wait_ge(ch, idx * 10 + 10)
                        for j in range(6):
                            trc += 1
                            if trc > 2:
                                te.wait_ge(s_te, trc - 2)
                            if j < 4:
                                qh, kc = j // 2, j % 2
                                te.transpose((ptr0 if trc % 2 == 0 else ptr1)[:, :],
                                             atn[:, qh * 256 + kc * 128:
                                                 qh * 256 + (kc + 1) * 128],
                                             ida2[:, :]).then_inc(s_tr, 1)
                            else:
                                kc = j - 4
                                te.transpose((ptr0 if trc % 2 == 0 else ptr1)[:, :],
                                             gTs[kc], ida2[:, :]).then_inc(s_tr, 1)
                        te.wait_ge(s_te, idx * 6 + 6)
                        for qh in range(2):
                            te.wait_ge(s_oe, idx * 2 + qh)
                            for kc in range(2):
                                mm = te.matmul(pso[:, :],
                                               atT[:, kc * 256 + qh * 128:
                                                   kc * 256 + (qh + 1) * 128],
                                               gk[:, kc * 128:(kc + 1) * 128],
                                               start=(kc == 0), stop=(kc == 1))
                            mm.then_inc(s_mo, 1)
                        te.wait_ge(s_oe, idx * 2 + 2)
                        for qh in range(2):
                            for hf in range(2):
                                pmi = idx * 4 + 2 * qh + hf
                                te.wait_ge(s_pc, pmi)
                                te.matmul(ppm[:, :],
                                          pm[:, (p_ * 2 + hf) * 128:(p_ * 2 + hf + 1) * 128],
                                          ob[:, qh * 128:(qh + 1) * 128],
                                          start=True, stop=True).then_inc(s_mp, 1)

            @block.scalar
            def _(sc):
                for s_ in range(NSEG):
                    for h in range(H):
                        idx = s_ * H + h
                        for qh in range(2):
                            k_ = idx * 2 + qh
                            sc.wait_ge(s_ms, k_ + 1)
                            sc.wait_ge(ch, idx * 10 + 5 * qh + 2)
                            sc.activation(pb[:, qh * 256:(qh + 1) * 256],
                                          (pss0 if qh == 0 else pss1)[:, :], AF.Exp,
                                          bias=sm[:, 4 + qh:5 + qh],
                                          scale=SC).then_inc(s_sx, 1)

            @block.vector
            def _(v):
                for s_ in range(NSEG):
                    for h in range(H):
                        idx = s_ * H + h
                        for qh in range(2):
                            k_ = idx * 2 + qh
                            pss = pss0 if qh == 0 else pss1
                            v.wait_ge(s_ms, k_ + 1)
                            v.tensor_reduce(sm[:, 0:1], pss[:, :], mybir.AxisListType.X,
                                            ALU.max).then_inc(ch, 1)
                            v.wait_ge(ch, idx * 10 + 5 * qh + 1)
                            if idx >= 1:
                                v.wait_ge(s_sx, 2 * (idx - 1) + qh + 1)
                            v.tensor_scalar(sm[:, 4 + qh:5 + qh], sm[:, 0:1], -SC, None,
                                            ALU.mult).then_inc(ch, 1)
                            v.wait_ge(s_sx, k_ + 1)
                            v.reduce_sum(sm[:, 1:2], pb[:, qh * 256:(qh + 1) * 256],
                                         mybir.AxisListType.X).then_inc(ch, 1)
                            v.wait_ge(ch, idx * 10 + 5 * qh + 3)
                            v.reciprocal(sm[:, 2:3], sm[:, 1:2]).then_inc(ch, 1)
                            v.wait_ge(ch, idx * 10 + 5 * qh + 4)
                            if idx >= 1 and qh == 0:
                                v.wait_ge(s_tr, idx * 6)
                            v.tensor_scalar(atn[:, qh * 256:(qh + 1) * 256],
                                            pb[:, qh * 256:(qh + 1) * 256],
                                            sm[:, 2:3], None, ALU.mult).then_inc(ch, 1)
                        for j in range(6):
                            tcv = idx * 6 + j + 1
                            v.wait_ge(s_tr, tcv)
                            if idx >= 1 and j == 0:
                                v.wait_ge(s_mo, idx * 2)
                            dst = (atT[:, (j % 2) * 256 + (j // 2) * 128:
                                       (j % 2) * 256 + (j // 2) * 128 + 128]
                                   if j < 4 else gk[:, (j - 4) * 128:(j - 3) * 128])
                            v.tensor_copy(dst, (ptr0 if tcv % 2 == 0 else ptr1)[:, :]
                                          ).then_inc(s_te, 1)
                        for qh in range(2):
                            v.wait_ge(s_mo, idx * 2 + qh + 1)
                            if idx >= 1 and qh == 0:
                                v.wait_ge(s_mp, idx * 4)
                            v.tensor_copy(ob[:, qh * 128:(qh + 1) * 128],
                                          pso[:, :]).then_inc(s_oe, 1)
                        for qh in range(2):
                            for hf in range(2):
                                pmi = idx * 4 + 2 * qh + hf
                                v.wait_ge(s_mp, pmi + 1)
                                tb = s_ * 4 + 2 * qh + hf
                                v.tensor_copy(attsb[:, tb * D + h * 128:
                                                    tb * D + (h + 1) * 128],
                                              ppm[:, :]).then_inc(s_pc, 1)

            @block.gpsimd
            def _(g):
                for tb in range(NTB):
                    s_seg = tb // 4
                    g.wait_ge(s_pc, (s_seg + 1) * H * 4)
                    g.tensor_tensor(attsb[:, tb * D:(tb + 1) * D],
                                    attsb[:, tb * D:(tb + 1) * D],
                                    x2f[:, tb * D:(tb + 1) * D], ALU.add).then_inc(s_ra, 1)
                    g.wait_ge(s_ra, tb + 1)
                    g.dma_start(out=out_dram[tb * 128:(tb + 1) * 128, :],
                                in_=attsb[:, tb * D:(tb + 1) * D]).then_inc(s_ro, 16)
                g.wait_ge(s_ro, NTB * 16)
        nc.all_engine_barrier()


def build_attn():
    nc = bass.Bass("TRN2", target_bir_lowering=False)
    ypre = nc.dram_tensor("ypre", [TC, D], F32, kind="ExternalInput").ap()
    outd = nc.dram_tensor("outd", [TC, D], F32, kind="ExternalOutput").ap()
    emit_attn(nc, ypre, outd)
    return nc



def _stage_reset(nc, snap=None, dma_reset=True):
    nc.all_engine_barrier()
    free = sorted(nc.free_semaphores)
    if free:
        with nc.Block() as b:
            @b.gpsimd
            def _(g):
                i = 0
                while i < len(free):
                    j = i
                    while j + 1 < len(free) and free[j + 1] == free[j] + 1:
                        j += 1
                    if dma_reset:
                        g.dma_reset(range(free[i], free[j] + 1))
                    g.sem_clear(range(free[i], free[j] + 1))
                    i = j + 1
    nc.all_engine_barrier()


def emit_t1(nc, y2, ypre1_dram):
    tag = "t1"
    NTB = TC // 128
    NK = D // 128
    identf = nc.inline_tensor(np.eye(128, dtype=np.float32).astype(ml_dtypes.bfloat16), "idt1")
    with (
        nc.sbuf_tensor("tldA" + tag, [128, NK * 128], BF16) as tld0,
        nc.sbuf_tensor("tldB" + tag, [128, NK * 128], BF16) as tld1,
        nc.sbuf_tensor("trow" + tag, [128, 2 * D], F32) as trow,
        nc.sbuf_tensor("tid" + tag, [128, 128], BF16) as tid,
        nc.psum_tensor("tp0" + tag, [128, 128], BF16) as tp0,
        nc.psum_tensor("tp1" + tag, [128, 128], BF16) as tp1,
        nc.semaphore("t1l0") as s_l0,
        nc.semaphore("t1l1") as s_l1,
        nc.semaphore("t1tp") as s_tp,
        nc.semaphore("t1ev") as s_ev,
        nc.semaphore("t1ro") as s_ro,
        nc.semaphore("t1id") as s_id,
        nc.Block() as block,
    ):
        s_l = [s_l0, s_l1]
        tlds = [tld0, tld1]
        agvs = [y2[b].ap().rearrange("(r p) t -> p r t", p=128) for b in range(2)]

        @block.sync
        def _(s):
            s.dma_start(out=tid[:, :], in_=identf.ap()).then_inc(s_id, 16)

        @block.gpsimd
        def _(g):
            for tb in range(NTB):
                if tb >= 2:
                    g.wait_ge(s_tp, (tb - 1) * NK)
                for b in range(2):
                    g.dma_start(
                        out=tlds[tb % 2][:, :].rearrange(
                            "p (r c t) -> p c r t", c=2, t=128)[:, b],
                        in_=agvs[b][:, :, tb * 128:(tb + 1) * 128],
                    ).then_inc(s_l[tb % 2], 16)

        @block.tensor
        def _(te):
            te.wait_ge(s_id, 16)
            cnt = 0
            for tb in range(NTB):
                te.wait_ge(s_l[tb % 2], (tb // 2 + 1) * 32)
                for k in range(NK):
                    cnt += 1
                    if cnt > 2:
                        te.wait_ge(s_ev, cnt - 2)
                    te.transpose((tp0 if cnt % 2 == 0 else tp1)[:, :],
                                 tlds[tb % 2][:, k * 128:(k + 1) * 128],
                                 tid[:, :]).then_inc(s_tp, 1)

        @block.vector
        def _(v):
            cnt = 0
            for tb in range(NTB):
                for k in range(NK):
                    cnt += 1
                    v.wait_ge(s_tp, cnt)
                    if k == 0 and tb >= 2:
                        v.wait_ge(s_ro, tb * 16)
                    v.tensor_copy(trow[:, (tb % 2) * D + k * 128:(tb % 2) * D + (k + 1) * 128],
                                  (tp0 if cnt % 2 == 0 else tp1)[:, :]).then_inc(s_ev, 1)

        @block.scalar
        def _(sc):
            for tb in range(NTB):
                sc.wait_ge(s_ev, (tb + 1) * NK)
                sc.dma_start(out=ypre1_dram[tb * 128:(tb + 1) * 128, :],
                             in_=trow[:, (tb % 2) * D:(tb % 2 + 1) * D]).then_inc(s_ro, 16)
            sc.wait_ge(s_ro, NTB * 16)


def build_fused():
    nc = bass.Bass("TRN2", target_bir_lowering=False)
    IC0 = 8192 // NC
    IC1 = 16384 // NC
    xT_tc = nc.dram_tensor("xT_tc", [D, TC], F32, kind="ExternalInput").ap()
    xT_dc = nc.dram_tensor("xT_dc", [DC, T], F32, kind="ExternalInput").ap()
    WxT = nc.dram_tensor("WxT", [D, 160], F32, kind="ExternalInput").ap()
    WdtT_c = nc.dram_tensor("WdtT_c", [R, DC], F32, kind="ExternalInput").ap()
    bdt_c = nc.dram_tensor("bdt_c", [DC], F32, kind="ExternalInput").ap()
    A_c = nc.dram_tensor("A_c", [DC, N], F32, kind="ExternalInput").ap()
    Dp_c = nc.dram_tensor("Dp_c", [DC], F32, kind="ExternalInput").ap()
    g0w = nc.dram_tensor("g0w", [IC0, D], I32, kind="ExternalInput").ap()
    g0s = nc.dram_tensor("g0s", [IC0], F32, kind="ExternalInput").ap()
    u0w = nc.dram_tensor("u0w", [IC0, D], I32, kind="ExternalInput").ap()
    u0s = nc.dram_tensor("u0s", [IC0], F32, kind="ExternalInput").ap()
    d0w = nc.dram_tensor("d0w", [D, IC0], I32, kind="ExternalInput").ap()
    d0s = nc.dram_tensor("d0s", [D], F32, kind="ExternalInput").ap()
    g1w = nc.dram_tensor("g1w", [IC1, D // 2], I32, kind="ExternalInput").ap()
    g1s = nc.dram_tensor("g1s", [IC1, D // 32], F32, kind="ExternalInput").ap()
    u1w = nc.dram_tensor("u1w", [IC1, D // 2], I32, kind="ExternalInput").ap()
    u1s = nc.dram_tensor("u1s", [IC1, D // 32], F32, kind="ExternalInput").ap()
    d1w = nc.dram_tensor("d1w", [D, IC1 // 2], I32, kind="ExternalInput").ap()
    d1s = nc.dram_tensor("d1s", [D, IC1 // 32], F32, kind="ExternalInput").ap()
    outd = nc.dram_tensor("outd", [TC, D], F32, kind="ExternalOutput").ap()
    ag_dt_in = nc.dram_tensor("ag_dt_in", [R, TC], BF16, kind="Internal")
    ag_dt_out = nc.dram_tensor("ag_dt_out", [NC * R, TC], BF16, kind="Internal",
                               addr_space="Shared")
    ag_bc_in = nc.dram_tensor("ag_bc_in", [2 * N, TC], BF16, kind="Internal")
    ag_bc_out = nc.dram_tensor("ag_bc_out", [NC * 2 * N, TC], BF16, kind="Internal",
                               addr_space="Shared")
    bc_packed = nc.dram_tensor("bc_packed", [2 * N, T], BF16, kind="Internal")
    ya2a_in = [nc.dram_tensor(f"ya2a_in{b}", [NC * 128, TC], BF16, kind="Internal")
               for b in range(2)]
    ya2a_out = [nc.dram_tensor(f"ya2a_out{b}", [NC * 128, TC], BF16, kind="Internal")
                for b in range(2)]
    cw = {k: nc.dram_tensor("c_" + k, shp, BF16, kind="Internal") for k, shp in
          [("w0g", [D, IC0]), ("w0u", [D, IC0]), ("w0d", [IC0, D]),
           ("w1g", [D, IC1]), ("w1u", [D, IC1]), ("w1d", [IC1, D])]}
    aw = {k: nc.dram_tensor("a_" + k, [NC * shp0, shp1], BF16, kind="Internal",
                            addr_space="Shared") for k, (shp0, shp1) in
          [("w0g", (D, IC0)), ("w0u", (D, IC0)), ("w0d", (IC0, D)),
           ("w1g", (D, IC1)), ("w1u", (D, IC1)), ("w1d", (IC1, D))]}
    ypre1_dram = nc.dram_tensor("ypre1_d", [TC, D], F32, kind="Internal")
    ypre2_dram = nc.dram_tensor("ypre2_d", [TC, D], F32, kind="Internal")
    ypre3_dram = nc.dram_tensor("ypre3_d", [TC, D], F32, kind="Internal")
    x1d_f0 = nc.dram_tensor("x1d_f0", [TC, D], F32, kind="Internal")
    x1d_f1 = nc.dram_tensor("x1d_f1", [TC, D], F32, kind="Internal")

    import contextlib
    octx = contextlib.ExitStack()
    with octx:
        s_agw = octx.enter_context(nc.semaphore("g_agw"))
        s_agy = octx.enter_context(nc.semaphore("g_agy"))

        _sc_prep = nc.enter_named_scope("prep", False)
        snap = nc.snapshot_sems()
        ctx = contextlib.ExitStack()
        with ctx:
            _prep_pair(nc, (g0w, g0s, cw["w0g"].ap(), IC0 // 128, D),
                       (g1w, g1s, cw["w1g"].ap(), IC1 // 128, D), tag="pg")
            _prep_pair(nc, (u0w, u0s, cw["w0u"].ap(), IC0 // 128, D),
                       (u1w, u1s, cw["w1u"].ap(), IC1 // 128, D), tag="pu")
            _prep_pair(nc, (d0w, d0s, cw["w0d"].ap(), D // 128, IC0),
                       (d1w, d1s, cw["w1d"].ap(), D // 128, IC1), tag="pd")
        _stage_reset(nc, snap)

        def _w_ag_hook(g):
            for k in ["w0g", "w0u", "w0d", "w1g", "w1u", "w1d"]:
                g.collective_compute("AllGather", ALU.bypass,
                                     replica_groups=[list(range(NC))],
                                     ins=[cw[k].ap().opt()],
                                     outs=[aw[k].ap().opt()]).then_inc(s_agw)
        nc.leave_named_scope("prep", _sc_prep[0], False)

        _sc_s1 = nc.enter_named_scope("ssm", False)
        snap = nc.snapshot_sems()
        emit_stage1(nc, xT_tc, xT_dc, WxT, WdtT_c, bdt_c, A_c, Dp_c, None,
                    ag_dt_in, ag_dt_out, ag_bc_in, ag_bc_out, bc_packed,
                    a2a_y=(ya2a_in, ya2a_out, s_agy), post_ag_hook=_w_ag_hook)
        _stage_reset(nc, snap)
        nc.leave_named_scope("ssm", _sc_s1[0], False)

        _sc_t1 = nc.enter_named_scope("t1", False)
        snap = nc.snapshot_sems()
        with nc.Block() as b:
            @b.gpsimd
            def _(g):
                g.wait_ge(s_agy, 2)
        nc.all_engine_barrier()
        emit_t1(nc, ya2a_out, ypre1_dram.ap())
        _stage_reset(nc, snap)
        nc.leave_named_scope("t1", _sc_t1[0], False)

        def wv0(ic):
            r, lc = (ic * 512) // IC0, (ic * 512) % IC0
            gsrc = aw["w0g"].ap().rearrange("(r k p) i -> p r k i", p=128, k=D // 128)[
                :, r, :, lc:lc + 512]
            usrc = aw["w0u"].ap().rearrange("(r k p) i -> p r k i", p=128, k=D // 128)[
                :, r, :, lc:lc + 512]
            return gsrc, usrc
        _sc_f0 = nc.enter_named_scope("ffn0", False)
        snap = nc.snapshot_sems()
        emit_ffn(nc, 0, ypre1_dram.ap(), None, None, aw["w0d"].ap(), ypre2_dram.ap(),
                 8192, final_ln=False, x1_dram=x1d_f0.ap(), w_views=wv0,
                 w_ready=(s_agw, 6))
        _stage_reset(nc, snap)
        nc.leave_named_scope("ffn0", _sc_f0[0], False)

        _sc_at = nc.enter_named_scope("attn", False)
        snap = nc.snapshot_sems()
        emit_attn(nc, ypre2_dram.ap(), ypre3_dram.ap())
        _stage_reset(nc, snap)
        nc.leave_named_scope("attn", _sc_at[0], False)
        _sc_f1 = nc.enter_named_scope("ffn1", False)

        def wv1(ic):
            r, lc = (ic * 512) // IC1, (ic * 512) % IC1
            gsrc = aw["w1g"].ap().rearrange("(r k p) i -> p r k i", p=128, k=D // 128)[
                :, r, :, lc:lc + 512]
            usrc = aw["w1u"].ap().rearrange("(r k p) i -> p r k i", p=128, k=D // 128)[
                :, r, :, lc:lc + 512]
            return gsrc, usrc
        emit_ffn(nc, 1, ypre3_dram.ap(), None, None, aw["w1d"].ap(), outd,
                 16384, final_ln=True, x1_dram=x1d_f1.ap(), w_views=wv1)
        nc.leave_named_scope("ffn1", _sc_f1[0], False)
    return nc


def fused_inputs(full):
    x = np.ascontiguousarray(np.asarray(full["L0_attn_x"], dtype=np.float32
                                        ).reshape(T, D))
    m1 = stage1_inputs(x, np.asarray(full["L0_attn_W_xproj"]),
                       np.asarray(full["L0_attn_W_dt"]),
                       np.asarray(full["L0_attn_b_dt"]),
                       np.asarray(full["L0_attn_A"]),
                       np.asarray(full["L0_attn_D"]))
    mp = prep_inputs(full)
    return [{**m1[c], **mp[c]} for c in range(NC)]



from concourse.bass_utils import run_bass_kernel_spmd


def kernel(L0_attn_x, L0_attn_W_xproj, L0_attn_W_dt, L0_attn_b_dt, L0_attn_A,
           L0_attn_D, L0_ffn_gate_w, L0_ffn_gate_scale, L0_ffn_up_w,
           L0_ffn_up_scale, L0_ffn_down_w, L0_ffn_down_scale,
           L1_ffn_gate_W_packed, L1_ffn_gate_scales, L1_ffn_up_W_packed,
           L1_ffn_up_scales, L1_ffn_down_W_packed, L1_ffn_down_scales):
    full = {
        "L0_attn_x": np.asarray(L0_attn_x),
        "L0_attn_W_xproj": np.asarray(L0_attn_W_xproj),
        "L0_attn_W_dt": np.asarray(L0_attn_W_dt),
        "L0_attn_b_dt": np.asarray(L0_attn_b_dt),
        "L0_attn_A": np.asarray(L0_attn_A),
        "L0_attn_D": np.asarray(L0_attn_D),
        "L0_ffn_gate_w": np.asarray(L0_ffn_gate_w),
        "L0_ffn_gate_scale": np.asarray(L0_ffn_gate_scale),
        "L0_ffn_up_w": np.asarray(L0_ffn_up_w),
        "L0_ffn_up_scale": np.asarray(L0_ffn_up_scale),
        "L0_ffn_down_w": np.asarray(L0_ffn_down_w),
        "L0_ffn_down_scale": np.asarray(L0_ffn_down_scale),
        "L1_ffn_gate_W_packed": np.asarray(L1_ffn_gate_W_packed),
        "L1_ffn_gate_scales": np.asarray(L1_ffn_gate_scales),
        "L1_ffn_up_W_packed": np.asarray(L1_ffn_up_W_packed),
        "L1_ffn_up_scales": np.asarray(L1_ffn_up_scales),
        "L1_ffn_down_W_packed": np.asarray(L1_ffn_down_W_packed),
        "L1_ffn_down_scales": np.asarray(L1_ffn_down_scales),
    }
    nc = build_fused()
    maps = fused_inputs(full)
    res = run_bass_kernel_spmd(nc, maps, core_ids=list(range(NC))).results
    x4 = np.concatenate([res[c]["outd"] for c in range(NC)], axis=0)
    return np.ascontiguousarray(x4.reshape(1, T, D)).astype(np.float32)


# version 12
# speedup vs baseline: 1.0770x; 1.0770x over previous
import sys
if "/opt/trn_rl_repo" not in sys.path:
    sys.path.insert(0, "/opt/trn_rl_repo")
"""Builders for the mk_seq_postln_d2 megakernel stages (dev version).

T = 8192 tokens, D = 2048 dims, N = 16 ssm state, R = 128 dt_rank
NC = 8 cores; DC = 256 dims/core (stage 1); TC = 1024 tokens/core
"xT" means feature-major [d on partitions, t on free].
"""
import os
import numpy as np
import ml_dtypes
import concourse.bass as bass
import concourse.mybir as mybir

F32 = mybir.dt.float32
BF16 = mybir.dt.bfloat16
I32 = mybir.dt.int32
AF = mybir.ActivationFunctionType
ALU = mybir.AluOpType

T, D, N, R = 8192, 2048, 16, 128
NC = 8
DC = D // NC
TC = T // NC
HALF = T // 2


def emit_stage1(nc, xT_tc, xT_dc, WxT, WdtT_c, bdt_c, A_c, Dp_c, ypreT,
                ag_dt_in, ag_dt_out, ag_bc_in, ag_bc_out, bc_packed,
                sems=None, ag_y=None, a2a_y=None, post_ag_hook=None,
                mid_hook=None):
    TT = 512
    NB = DC // 128
    NT = TC // TT
    NTILE = T // TT

    import contextlib
    ctx = contextlib.ExitStack()
    with ctx:
        wdt = ctx.enter_context(nc.sbuf_tensor("s1_wdt", [128, DC], BF16))
        bdt = ctx.enter_context(nc.sbuf_tensor("s1_bdt", [128, NB], F32))
        acol = ctx.enter_context(nc.sbuf_tensor("s1_acol", [128, NB * N], F32))
        dcol = ctx.enter_context(nc.sbuf_tensor("s1_dcol", [128, NB], F32))
        dp1 = ctx.enter_context(nc.sbuf_tensor("s1_dp1", [128, NB], F32))

        s_ld = ctx.enter_context(nc.semaphore("s1_ld"))
        s_mm = ctx.enter_context(nc.semaphore("s1_mm"))
        s_ev = ctx.enter_context(nc.semaphore("s1_ev"))
        s_ag = ctx.enter_context(nc.semaphore("s1_ag"))
        s_dma = ctx.enter_context(nc.semaphore("s1_dma"))
        s_dts = [ctx.enter_context(nc.semaphore("s1_dts0")),
                 ctx.enter_context(nc.semaphore("s1_dts1"))]
        s_sp = ctx.enter_context(nc.semaphore("s1_sp"))
        s_xb = ctx.enter_context(nc.semaphore("s1_xb"))
        s_dxr = ctx.enter_context(nc.semaphore("s1_dxr"))
        s_bc = [ctx.enter_context(nc.semaphore("s1_bc0")),
                ctx.enter_context(nc.semaphore("s1_bc1"))]
        s_ad = ctx.enter_context(nc.semaphore("s1_ad"))
        s_adc = ctx.enter_context(nc.semaphore("s1_adc"))
        s_tm = ctx.enter_context(nc.semaphore("s1_tm"))
        s_yo = ctx.enter_context(nc.semaphore("s1_yo"))
        ch_v = ctx.enter_context(nc.semaphore("s1_chv"))
        ch_sc = ctx.enter_context(nc.semaphore("s1_chsc"))
        ch_g = ctx.enter_context(nc.semaphore("s1_chg"))
        s_rp = ctx.enter_context(nc.semaphore("s1_rp"))

        with (
            nc.sbuf_tensor("s1_wx", [128, 16 * 160], BF16) as wx,
            nc.sbuf_tensor("s1_xttc", [128, 16 * TC], BF16) as xttc,
            nc.sbuf_tensor("s1_dtT", [128, TC], BF16) as dtT,
            nc.sbuf_tensor("s1_bcs", [32, TC], BF16) as bcs,
            nc.psum_tensor("s1_psdt", [128, 2 * TT], F32) as ps_dt,
            nc.psum_tensor("s1_psbc", [32, 2 * TT], F32) as ps_bc,
            nc.Block() as block,
        ):
            @block.gpsimd
            def _(g):
                g.dma_start(out=wx[:, :].rearrange("p (k e) -> p k e", k=16),
                            in_=WxT.rearrange("(k p) e -> p k e", k=16)).then_inc(s_ld, 16)
                g.dma_start(out=xttc[:, :].rearrange("p (k t) -> p k t", k=16),
                            in_=xT_tc.rearrange("(k p) t -> p k t", k=16)).then_inc(s_ld, 16)
                g.dma_start(out=wdt[:, :], in_=WdtT_c).then_inc(s_ld, 16)

            @block.sync
            def _(s):
                s.dma_start(out=bdt[:, :], in_=bdt_c.rearrange("(b p) -> p b", p=128), allow_slow_non_contiguous=True).then_inc(s_dma, 16)
                s.dma_start(out=acol[:, :].rearrange("p (b n) -> p b n", n=16), in_=A_c.rearrange("(b p) n -> p b n", p=128)).then_inc(s_dma, 16)
                s.dma_start(out=dcol[:, :], in_=Dp_c.rearrange("(b p) -> p b", p=128), allow_slow_non_contiguous=True).then_inc(s_dma, 16)

            @block.tensor
            def _(te):
                te.wait_ge(s_ld, 48)
                for tt in range(NT):
                    for k in range(16):
                        te.matmul(ps_dt[:, tt * TT:(tt + 1) * TT],
                                  wx[:, k * 160:k * 160 + 128],
                                  xttc[:, k * TC + tt * TT:k * TC + (tt + 1) * TT],
                                  start=(k == 0), stop=(k == 15))
                    for k in range(16):
                        mm = te.matmul(ps_bc[:, tt * TT:(tt + 1) * TT],
                                  wx[:, k * 160 + 128:k * 160 + 160],
                                  xttc[:, k * TC + tt * TT:k * TC + (tt + 1) * TT],
                                  start=(k == 0), stop=(k == 15))
                    mm.then_inc(s_mm, 1)

            @block.scalar
            def _(sc):
                for tt in range(NT):
                    sc.wait_ge(s_mm, tt + 1)
                    sc.copy(dtT[:, tt * TT:(tt + 1) * TT],
                            ps_dt[:, tt * TT:(tt + 1) * TT]).then_inc(s_ev, 1)

            @block.vector
            def _(v):
                v.wait_ge(s_dma, 48)
                v.tensor_scalar(dp1[:, :], dcol[:, :], 1.0, None, ALU.add).then_inc(s_ev, 1)
                for tt in range(NT):
                    v.wait_ge(s_mm, tt + 1)
                    v.tensor_copy(bcs[:, tt * TT:(tt + 1) * TT],
                                  ps_bc[:, tt * TT:(tt + 1) * TT]).then_inc(s_ev, 1)

            @block.sync
            def _(s):
                s.wait_ge(s_ev, 1 + 2 * NT)
                s.dma_start(out=ag_dt_in[:, :], in_=dtT[:, :]).then_inc(s_dma, 16)
                s.dma_start(out=ag_bc_in[:, :], in_=bcs[:, :]).then_inc(s_dma, 16)

            @block.gpsimd
            def _(g):
                g.wait_ge(s_dma, 80)
                g.collective_compute("AllGather", ALU.bypass,
                                     replica_groups=[list(range(NC))],
                                     ins=[ag_dt_in.ap().opt()],
                                     outs=[ag_dt_out.ap().opt()]).then_inc(s_ag, 1)
                g.collective_compute("AllGather", ALU.bypass,
                                     replica_groups=[list(range(NC))],
                                     ins=[ag_bc_in.ap().opt()],
                                     outs=[ag_bc_out.ap().opt()]).then_inc(s_ag, 1)
                g.wait_ge(s_ag, 2)
                g.dma_start(
                    out=bc_packed.ap().rearrange("n (r t) -> n r t", r=NC),
                    in_=ag_bc_out.ap().rearrange("(r n) t -> n r t", n=2 * N),
                ).then_inc(s_rp, 16)
                if post_ag_hook is not None:
                    post_ag_hook(g)

        nc.all_engine_barrier()

        if mid_hook is not None:
            mid_hook()

        STOP = os.environ.get("S1_STOP", "full")
        if STOP == "A":
            return
        with (
            nc.sbuf_tensor("s1_deltaT", [128, NB * T], BF16) as deltaT,
            nc.sbuf_tensor("s1_dx", [128, T], BF16) as dx,
            nc.sbuf_tensor("s1_yb", [128, T], F32) as yb,
            nc.sbuf_tensor("s1_xbf", [128, T], F32) as xbf,
            nc.sbuf_tensor("s1_dts", [128, 2 * TT], BF16) as dts,
            nc.sbuf_tensor("s1_bmb", [128, 2 * HALF], BF16) as bmb,
            nc.sbuf_tensor("s1_cmb", [128, 2 * HALF], BF16) as cmb,
            nc.sbuf_tensor("s1_adt", [128, 2 * HALF], BF16) as adt,
            nc.sbuf_tensor("s1_bxt", [128, HALF], BF16) as bxt,
            nc.sbuf_tensor("s1_ht", [128, HALF], BF16) as ht,
            nc.sbuf_tensor("s1_tmp", [128, 2 * HALF], BF16) as tmp,
            nc.sbuf_tensor("s1_carry", [128, 1], F32) as carry,
            nc.psum_tensor("s1_psde", [128, 2 * TT], F32) as ps_de,
            nc.Block() as block,
        ):
            STREAMONLY = STOP == "stream"

            def dts_load(eng, j):
                tt_ = j % NTILE
                eng.dma_start(
                    out=dts[:, (j % 2) * TT:(j % 2 + 1) * TT],
                    in_=ag_dt_out.ap().rearrange("(c p) t -> p c t", p=R)[
                        :, (tt_ * TT) // TC,
                        (tt_ * TT) % TC:(tt_ * TT) % TC + TT],
                ).then_inc(s_dts[j % 2], 16)

            @block.sync
            def _(s):
                dts_load(s, 0)
                dts_load(s, 1)
                for b in range(NB):
                    if STOP in ("delta", "stream", "noscalar"):
                        continue
                    if b == 0:
                        s.wait_ge(s_rp, 16)
                    for n in range(N):
                        for hf in range(2):
                            idx = (b * N + n) * 2 + hf
                            buf = idx % 2
                            if idx >= 2:
                                s.wait_ge(s_tm, idx - 1)
                            src_b = bc_packed.ap()[n:n + 1, hf * HALF:(hf + 1) * HALF
                                                   ].broadcast_to([128, HALF])
                            s.dma_start(out=bmb[:, buf * HALF:(buf + 1) * HALF],
                                        in_=src_b).then_inc(s_bc[buf], 16)

            @block.tensor
            def _(te):
                if STOP == "stream":
                    return
                for b in range(NB):
                    for tt in range(NTILE):
                        j = b * NTILE + tt
                        te.wait_ge(s_dts[j % 2], (j // 2 + 1) * 16)
                        if j >= 2 and STOP != "noscalar":
                            te.wait_ge(s_sp, j - 1)
                        te.matmul(ps_de[:, (j % 2) * TT:(j % 2 + 1) * TT],
                                  wdt[:, b * 128:(b + 1) * 128],
                                  dts[:, (j % 2) * TT:(j % 2 + 1) * TT],
                                  start=True, stop=True).then_inc(s_mm, 1)

            @block.scalar
            def _(sc):
                if STOP in ("stream", "noscalar"):
                    return
                CHSC = [0]

                def sp_tile(b, tt):
                    j = b * NTILE + tt
                    sc.wait_ge(s_mm, NT + j + 1)
                    if j + 2 < NB * NTILE:
                        dts_load(sc, j + 2)
                    sl = slice(b * T + tt * TT, b * T + (tt + 1) * TT)
                    psl = slice((j % 2) * TT, (j % 2 + 1) * TT)
                    sc.activation(deltaT[:, sl], ps_de[:, psl], AF.Exp,
                                  bias=bdt[:, b:b + 1], scale=1.0).then_inc(ch_sc, 1)
                    CHSC[0] += 1
                    sc.wait_ge(ch_sc, CHSC[0])
                    sc.activation(deltaT[:, sl], deltaT[:, sl], AF.Ln,
                                  bias=1.0, scale=1.0).then_inc(s_sp, 1)
                for tt in range(NTILE):
                    sp_tile(0, tt)
                if STOP != "delta":
                    for b in range(NB):
                        sc.wait_ge(s_sp, (b + 1) * NTILE)
                        for n in range(N):
                            for hf in range(2):
                                idx = (b * N + n) * 2 + hf
                                loc = idx - b * N * 2
                                if b + 1 < NB and loc < NTILE:
                                    sp_tile(b + 1, loc)
                                if idx >= 2:
                                    sc.wait_ge(s_adc, idx - 1)
                                sc.activation(adt[:, (idx % 2) * HALF:(idx % 2 + 1) * HALF],
                                              deltaT[:, b * T + hf * HALF:
                                                     b * T + (hf + 1) * HALF], AF.Exp,
                                              scale=acol[:, b * N + n:b * N + n + 1]
                                              ).then_inc(s_ad, 1)

            CHV = [0]
            CHG = [0]

            @block.vector
            def _(v):
                if STOP in ("delta", "stream", "noscalar"):
                    return
                for b in range(NB):
                    v.wait_ge(s_xb, (b + 1) * 16)
                    v.wait_ge(s_sp, (b + 1) * NTILE)
                    if b > 0:
                        v.wait_ge(s_yo, b * 16)
                    v.tensor_scalar(yb[:, :], xbf[:, :], dp1[:, b:b + 1], None, ALU.mult)
                    v.tensor_tensor(dx[:, :], deltaT[:, b * T:(b + 1) * T],
                                    xbf[:, :], ALU.mult).then_inc(s_dxr, 1)
                    v.wait_ge(s_dxr, b + 1)
                    for n in range(N):
                        for hf in range(2):
                            idx = (b * N + n) * 2 + hf
                            hsl = slice(hf * HALF, (hf + 1) * HALF)
                            bsl = slice((idx % 2) * HALF, (idx % 2 + 1) * HALF)
                            v.wait_ge(s_bc[idx % 2], (idx // 2 + 1) * 32)
                            v.wait_ge(s_ad, idx + 1)
                            v.wait_ge(s_adc, idx)
                            v.tensor_tensor(bxt[:, :], dx[:, hsl], bmb[:, bsl],
                                            ALU.mult).then_inc(ch_v, 1)
                            CHV[0] += 1
                            if hf == 1:
                                v.tensor_copy(carry[:, 0:1], ht[:, HALF - 1:HALF]).then_inc(ch_v, 1)
                                CHV[0] += 1
                            v.wait_ge(ch_v, CHV[0])
                            v.wait_ge(s_tm, idx)
                            init = 0.0 if hf == 0 else carry[:, 0:1]
                            v.tensor_tensor_scan(ht[:, :], adt[:, bsl], bxt[:, :], init,
                                                 ALU.mult, ALU.add).then_inc(s_adc, 1)
                            if idx >= 2:
                                v.wait_ge(ch_g, idx - 1)
                            v.wait_ge(s_adc, idx + 1)
                            v.tensor_tensor(tmp[:, bsl], ht[:, :], cmb[:, bsl],
                                            ALU.mult).then_inc(s_tm, 1)

            @block.gpsimd
            def _(g):
                if STOP in ("delta", "stream", "noscalar"):
                    return

                def cmb_load(idx2):
                    buf2 = idx2 % 2
                    hf2 = idx2 % 2
                    n2 = (idx2 // 2) % N
                    src_c = bc_packed.ap()[N + n2:N + n2 + 1,
                                           hf2 * HALF:(hf2 + 1) * HALF
                                           ].broadcast_to([128, HALF])
                    g.dma_start(out=cmb[:, buf2 * HALF:(buf2 + 1) * HALF],
                                in_=src_c).then_inc(s_bc[buf2], 16)
                g.wait_ge(s_rp, 16)
                cmb_load(0)
                cmb_load(1)
                for b in range(NB):
                    if b > 0:
                        g.wait_ge(s_dxr, b)
                    g.dma_start(out=xbf[:, :],
                                in_=xT_dc[b * 128:(b + 1) * 128, :]).then_inc(s_xb, 16)
                    for n in range(N):
                        for hf in range(2):
                            idx = (b * N + n) * 2 + hf
                            hsl = slice(hf * HALF, (hf + 1) * HALF)
                            bsl = slice((idx % 2) * HALF, (idx % 2 + 1) * HALF)
                            g.wait_ge(s_tm, idx + 1)
                            if idx + 2 < NB * N * 2:
                                cmb_load(idx + 2)
                            g.wait_ge(ch_g, CHG[0])
                            g.tensor_tensor(yb[:, hsl], yb[:, hsl], tmp[:, bsl],
                                            ALU.add).then_inc(ch_g, 1)
                            CHG[0] += 1
                    g.wait_ge(ch_g, CHG[0])
                    if a2a_y is not None:
                        a2a_ins, a2a_outs, s_agy = a2a_y
                        g.dma_start(
                            out=a2a_ins[b].ap().rearrange("(j p) t -> p j t", p=128),
                            in_=yb[:, :].rearrange("p (j t) -> p j t", j=NC),
                        ).then_inc(s_yo, 16)
                        g.wait_ge(s_yo, (b + 1) * 16)
                        from concourse.aot_env import allow_ici_all_to_all
                        with allow_ici_all_to_all():
                            g.collective_compute(
                                "AllToAll", ALU.bypass,
                                replica_groups=[list(range(NC))],
                                ins=[a2a_ins[b].ap().opt()],
                                outs=[a2a_outs[b].ap().opt()]).then_inc(s_agy, 1)
                    else:
                        g.dma_start(out=ypreT[b * 128:(b + 1) * 128, :],
                                    in_=yb[:, :]).then_inc(s_yo, 16)
                g.wait_ge(s_yo, NB * 16)
                if ag_y is not None:
                    ag_out_t, s_agy = ag_y
                    g.collective_compute("AllGather", ALU.bypass,
                                         replica_groups=[list(range(NC))],
                                         ins=[ypreT.tensor.ap().opt()],
                                         outs=[ag_out_t.ap().opt()]).then_inc(s_agy)


def build_stage1():
    nc = bass.Bass("TRN2", target_bir_lowering=False)
    xT_tc = nc.dram_tensor("xT_tc", [D, TC], F32, kind="ExternalInput").ap()
    xT_dc = nc.dram_tensor("xT_dc", [DC, T], F32, kind="ExternalInput").ap()
    WxT = nc.dram_tensor("WxT", [D, 160], F32, kind="ExternalInput").ap()
    WdtT_c = nc.dram_tensor("WdtT_c", [R, DC], F32, kind="ExternalInput").ap()
    bdt_c = nc.dram_tensor("bdt_c", [DC], F32, kind="ExternalInput").ap()
    A_c = nc.dram_tensor("A_c", [DC, N], F32, kind="ExternalInput").ap()
    Dp_c = nc.dram_tensor("Dp_c", [DC], F32, kind="ExternalInput").ap()
    ypreT = nc.dram_tensor("ypreT", [DC, T], F32, kind="ExternalOutput").ap()

    ag_dt_in = nc.dram_tensor("ag_dt_in", [R, TC], BF16, kind="Internal")
    ag_dt_out = nc.dram_tensor("ag_dt_out", [NC * R, TC], BF16, kind="Internal",
                               addr_space="Shared")
    ag_bc_in = nc.dram_tensor("ag_bc_in", [2 * N, TC], BF16, kind="Internal")
    ag_bc_out = nc.dram_tensor("ag_bc_out", [NC * 2 * N, TC], BF16, kind="Internal",
                               addr_space="Shared")
    bc_packed = nc.dram_tensor("bc_packed", [2 * N, T], BF16, kind="Internal")

    emit_stage1(nc, xT_tc, xT_dc, WxT, WdtT_c, bdt_c, A_c, Dp_c, ypreT,
                ag_dt_in, ag_dt_out, ag_bc_in, ag_bc_out, bc_packed)
    return nc


def stage1_inputs(x, W_xproj, W_dt, b_dt, A, Dp):
    x = np.ascontiguousarray(x.reshape(T, D))
    xT = np.ascontiguousarray(x.T)
    maps = []
    for c in range(NC):
        dc = slice(c * DC, (c + 1) * DC)
        tc = slice(c * TC, (c + 1) * TC)
        maps.append({
            "xT_tc": np.ascontiguousarray(xT[:, tc]),
            "xT_dc": np.ascontiguousarray(xT[dc, :]),
            "WxT": np.ascontiguousarray(W_xproj.T),
            "WdtT_c": np.ascontiguousarray(W_dt[dc, :].T),
            "bdt_c": np.ascontiguousarray(b_dt[dc]),
            "A_c": np.ascontiguousarray(A[dc, :]),
            "Dp_c": np.ascontiguousarray(Dp[dc]),
        })
    return maps



def _prep_pair(nc, deq, dec, tag=""):
    qsrc, qscol_src, qoutT, QNI, QW = deq
    dsrc, dsc2_src, doutT, DNI, DW = dec
    QNK = QW // 128
    DNK = DW // 128
    DWP = DW // 2
    DNSC = DW // 32
    import contextlib
    ctx = contextlib.ExitStack()
    with ctx:
        _e = ctx.enter_context
        qlw = _e(nc.sbuf_tensor("qlw" + tag, [128, 2 * QW], I32))
        qwq = _e(nc.sbuf_tensor("qwq" + tag, [128, QNI * QW], BF16))
        qscol = _e(nc.sbuf_tensor("qsc" + tag, [128, QNI], F32))
        qbcol = _e(nc.sbuf_tensor("qbc" + tag, [128, QNI], F32))
        qarow = _e(nc.sbuf_tensor("qar" + tag, [128, 2 * QNI * 128], BF16))
        ident = _e(nc.sbuf_tensor("pid" + tag, [128, 128], BF16))
        qp0 = _e(nc.psum_tensor("qp0" + tag, [128, 512], BF16))
        qp1 = _e(nc.psum_tensor("qp1" + tag, [128, 512], BF16))
        dlw = _e(nc.sbuf_tensor("dlw" + tag, [128, 2 * DWP], I32))
        dwq = _e(nc.sbuf_tensor("dwq" + tag, [128, DNI * DW], BF16))
        s1t = _e(nc.sbuf_tensor("ds1" + tag, [128, DNI * DNSC], F32))
        scr = {}
        for en in ("v", "g"):
            scr[en] = {nm: _e(nc.sbuf_tensor(f"d{nm}{en}" + tag, [128, DWP],
                                             I32 if nm in ("lo", "hi", "t0", "ub", "sb")
                                             else F32))
                       for nm in ("lo", "hi", "t0", "ub", "sb", "err", "vm")}
        darow = _e(nc.sbuf_tensor("dar" + tag, [128, 2 * DNI * 128], BF16))
        dp0 = _e(nc.psum_tensor("dp0" + tag, [128, 512], BF16))
        dp1 = _e(nc.psum_tensor("dp1" + tag, [128, 512], BF16))
        sm = {nm: _e(nc.semaphore(nm + tag)) for nm in
              ("ql0", "ql1", "qdq", "qtp", "qev", "qa0", "qa1", "qbi",
               "dl0", "dl1", "chv", "chg", "dtp", "dev", "da0", "da1", "psm")}
        q_l = [sm["ql0"], sm["ql1"]]
        q_ar = [sm["qa0"], sm["qa1"]]
        d_l = [sm["dl0"], sm["dl1"]]
        d_ar = [sm["da0"], sm["da1"]]
        chains = {"v": sm["chv"], "g": sm["chg"]}
        identT = nc.inline_tensor(np.eye(128, dtype=np.float32).astype(
            ml_dtypes.bfloat16), "pidc" + tag)
        eng_of = lambda ib: "v"
        n_v = sum(1 for ib in range(DNI) if eng_of(ib) == "v")
        n_g = DNI - n_v
        done_at = {}
        cnt_e = {"v": 0, "g": 0}
        for ib in range(DNI):
            cnt_e[eng_of(ib)] += 18
            done_at[ib] = cnt_e[eng_of(ib)]

        with nc.Block() as block:
            @block.sync
            def _(s):
                s.dma_start(out=ident[:, :], in_=identT.ap()).then_inc(sm["psm"], 16)
                s.dma_start(out=qscol[:, :],
                            in_=qscol_src.rearrange("(b p) -> p b", p=128),
                            allow_slow_non_contiguous=True).then_inc(sm["psm"], 16)
                s.dma_start(out=s1t[:, :].rearrange("p (b n) -> p b n", n=DNSC),
                            in_=dsc2_src.rearrange("(b p) n -> p b n", p=128)
                            ).then_inc(sm["psm"], 16)
                for ib in range(max(QNI, DNI)):
                    if ib < DNI:
                        if ib >= 2:
                            s.wait_ge(chains[eng_of(ib - 2)], done_at[ib - 2])
                        s.dma_start(out=dlw[:, (ib % 2) * DWP:(ib % 2 + 1) * DWP],
                                    in_=dsrc[ib * 128:(ib + 1) * 128, :]
                                    ).then_inc(d_l[ib % 2], 16)
                    if ib < QNI:
                        if ib >= 2:
                            s.wait_ge(sm["qdq"], ib - 1)
                        s.dma_start(out=qlw[:, (ib % 2) * QW:(ib % 2 + 1) * QW],
                                    in_=qsrc[ib * 128:(ib + 1) * 128, :]
                                    ).then_inc(q_l[ib % 2], 16)
                for k in range(QNK):
                    s.wait_ge(sm["qev"], (k + 1) * QNI)
                    s.dma_start(out=qoutT[k * 128:(k + 1) * 128, :],
                                in_=qarow[:, (k % 2) * QNI * 128:(k % 2 + 1) * QNI * 128]
                                ).then_inc(q_ar[k % 2], 16)
                s.wait_ge(q_ar[0], ((QNK + 1) // 2) * 16)
                s.wait_ge(q_ar[1], (QNK // 2) * 16)

            def decode_tile(en, ib, CH):
                key = eng_of(ib)
                ch = chains[key]
                sc_ = scr[key]
                lo, hi, t0 = sc_["lo"], sc_["hi"], sc_["t0"]
                ub, sb, errt, vm = sc_["ub"], sc_["sb"], sc_["err"], sc_["vm"]
                en.wait_ge(d_l[ib % 2], (ib // 2 + 1) * 16)
                lws = dlw[:, (ib % 2) * DWP:(ib % 2 + 1) * DWP]
                en.tensor_scalar(lo[:, :], lws, 15, None, ALU.bitwise_and).then_inc(ch, 1)
                en.tensor_scalar(hi[:, :], lws, 4, 15, ALU.logical_shift_right,
                                 ALU.bitwise_and).then_inc(ch, 1)
                CH[0] += 2
                for pl, csrc in ((0, lo), (1, hi)):
                    en.wait_ge(ch, CH[0])
                    en.tensor_scalar(t0[:, :], csrc[:, :], 7, None,
                                     ALU.bitwise_and).then_inc(ch, 1)
                    CH[0] += 1
                    en.wait_ge(ch, CH[0])
                    en.tensor_scalar(ub[:, :], t0[:, :], 252, 4194304, ALU.add,
                                     ALU.mult).then_inc(ch, 1)
                    en.tensor_scalar(sb[:, :], csrc[:, :], 8, 28, ALU.bitwise_and,
                                     ALU.logical_shift_left).then_inc(ch, 1)
                    CH[0] += 2
                    en.wait_ge(ch, CH[0])
                    en.tensor_scalar(errt[:, :], t0[:, :], -0.25, 0.5, ALU.mult,
                                     ALU.add).then_inc(ch, 1)
                    CH[0] += 1
                    en.wait_ge(ch, CH[0])
                    en.tensor_scalar(errt[:, :], errt[:, :], 0.0, None,
                                     ALU.max).then_inc(ch, 1)
                    CH[0] += 1
                    en.wait_ge(ch, CH[0])
                    en.tensor_tensor(vm[:, :], ub[:, :].bitcast(F32), errt[:, :],
                                     ALU.subtract).then_inc(ch, 1)
                    CH[0] += 1
                    en.wait_ge(ch, CH[0])
                    en.tensor_tensor(vm[:, :].bitcast(I32), vm[:, :].bitcast(I32),
                                     sb[:, :], ALU.bitwise_or).then_inc(ch, 1)
                    CH[0] += 1
                    en.wait_ge(ch, CH[0])
                    sbc = s1t[:, ib * DNSC:(ib + 1) * DNSC].rearrange(
                        "p (n o) -> p n o", o=1).broadcast_to([128, DNSC, 16])
                    ow = dwq[:, ib * DW + pl:(ib + 1) * DW:2].rearrange(
                        "p (n o) -> p n o", o=16)
                    en.tensor_tensor(ow, vm[:, :].rearrange("p (n o) -> p n o", o=16),
                                     sbc, ALU.mult).then_inc(ch, 1)
                    CH[0] += 1

            @block.vector
            def _(v):
                v.wait_ge(sm["psm"], 48)
                v.tensor_scalar(qbcol[:, :], qscol[:, :], -128.0, None,
                                ALU.mult).then_inc(sm["qbi"], 1)
                CH = [0]
                for ib in range(DNI):
                    if eng_of(ib) == "v":
                        decode_tile(v, ib, CH)

            @block.gpsimd
            def _(g):
                CH = [0]
                if n_g:
                    g.wait_ge(sm["psm"], 48)
                for ib in range(DNI):
                    if eng_of(ib) == "g":
                        decode_tile(g, ib, CH)
                for ib in range(DNI):
                    g.wait_ge(sm["dev"], (ib + 1) * DNK)
                    g.dma_start(
                        out=doutT.rearrange("(k p) i -> p k i", p=128)[
                            :, :, ib * 128:(ib + 1) * 128],
                        in_=darow[:, (ib % 2) * DNK * 128:(ib % 2 + 1) * DNK * 128
                                  ].rearrange("p (k i) -> p k i", k=DNK)
                    ).then_inc(d_ar[ib % 2], 16)
                g.wait_ge(d_ar[0], ((DNI + 1) // 2) * 16)
                g.wait_ge(d_ar[1], (DNI // 2) * 16)

            @block.scalar
            def _(sc):
                sc.wait_ge(sm["qbi"], 1)
                for ib in range(QNI):
                    sc.wait_ge(q_l[ib % 2], (ib // 2 + 1) * 16)
                    sc.activation(qwq[:, ib * QW:(ib + 1) * QW],
                                  qlw[:, (ib % 2) * QW:(ib % 2 + 1) * QW], AF.Identity,
                                  bias=qbcol[:, ib:ib + 1],
                                  scale=qscol[:, ib:ib + 1]).then_inc(sm["qdq"], 1)
                cnt = 0
                for k in range(QNK):
                    for ib in range(QNI):
                        cnt += 1
                        if ib == 0 and k >= 2:
                            sc.wait_ge(q_ar[k % 2], (k // 2) * 16)
                        sc.wait_ge(sm["qtp"], cnt)
                        sc.copy(qarow[:, (k % 2) * QNI * 128 + ib * 128:
                                      (k % 2) * QNI * 128 + (ib + 1) * 128],
                                (qp0 if cnt % 2 == 0 else qp1)[:, 0:128]
                                ).then_inc(sm["qev"], 1)
                cnt = 0
                for ib in range(DNI):
                    for k in range(DNK):
                        cnt += 1
                        if k == 0 and ib >= 2:
                            sc.wait_ge(d_ar[ib % 2], (ib // 2) * 16)
                        sc.wait_ge(sm["dtp"], cnt)
                        sc.copy(darow[:, (ib % 2) * DNK * 128 + k * 128:
                                      (ib % 2) * DNK * 128 + (k + 1) * 128],
                                (dp0 if cnt % 2 == 0 else dp1)[:, 0:128]
                                ).then_inc(sm["dev"], 1)

            @block.tensor
            def _(te):
                te.wait_ge(sm["qdq"], QNI)
                cnt = 0
                for k in range(QNK):
                    for ib in range(QNI):
                        cnt += 1
                        if cnt > 2:
                            te.wait_ge(sm["qev"], cnt - 2)
                        te.transpose((qp0 if cnt % 2 == 0 else qp1)[:, 0:128],
                                     qwq[:, ib * QW + k * 128:ib * QW + (k + 1) * 128],
                                     ident[:, :]).then_inc(sm["qtp"], 1)
                cnt = 0
                for ib in range(DNI):
                    te.wait_ge(chains[eng_of(ib)], done_at[ib])
                    for k in range(DNK):
                        cnt += 1
                        if cnt > 2:
                            te.wait_ge(sm["dev"], cnt - 2)
                        te.transpose((dp0 if cnt % 2 == 0 else dp1)[:, 0:128],
                                     dwq[:, ib * DW + k * 128:ib * DW + (k + 1) * 128],
                                     ident[:, :]).then_inc(sm["dtp"], 1)
    nc.all_engine_barrier()


def _prep_dequant_matrix(nc, ctx, sems, src, scol_src, outT, NI, W, tag=""):
    import contextlib
    _mctx = contextlib.ExitStack()
    with _mctx:
        sems = [_mctx.enter_context(nc.semaphore(f"pm{tag}_{i}")) for i in range(9)]
        _prep_dequant_body(nc, sems, src, scol_src, outT, NI, W, tag)
    return


def _prep_dequant_body(nc, sems, src, scol_src, outT, NI, W, tag=""):
    s_l0, s_dq, s_tp, s_ev, s_ar0, s_bi, s_sm, s_l1, s_ar1 = sems
    s_l = [s_l0, s_l1]
    s_ar = [s_ar0, s_ar1]
    NK = W // 128
    with (
        nc.sbuf_tensor("p_lw"+tag, [128, 2 * W], I32) as lw,
        nc.sbuf_tensor("p_wq"+tag, [128, NI * W], BF16) as wq,
        nc.sbuf_tensor("p_scol"+tag, [128, NI], F32) as scol,
        nc.sbuf_tensor("p_bcol"+tag, [128, NI], F32) as bcol,
        nc.sbuf_tensor("p_arow"+tag, [128, 2 * NI * 128], BF16) as arow,
        nc.sbuf_tensor("p_ident"+tag, [128, 128], BF16) as ident,
        nc.psum_tensor("p_pst0"+tag, [128, 512], BF16) as pst0,
        nc.psum_tensor("p_pst1"+tag, [128, 512], BF16) as pst1,
        nc.Block() as block,
    ):
        identT = nc.inline_tensor(np.eye(128, dtype=np.float32).astype(
            ml_dtypes.bfloat16), "p_identc"+tag)

        @block.sync
        def _(s):
            s.dma_start(out=ident[:, :], in_=identT.ap()).then_inc(s_sm, 16)
            s.dma_start(out=scol[:, :], in_=scol_src.rearrange("(b p) -> p b", p=128),
                        allow_slow_non_contiguous=True).then_inc(s_sm, 16)
            for ib in range(NI):
                if ib >= 2:
                    s.wait_ge(s_dq, ib - 1)
                s.dma_start(out=lw[:, (ib % 2) * W:(ib % 2 + 1) * W],
                            in_=src[ib * 128:(ib + 1) * 128, :]).then_inc(s_l[ib % 2], 16)

        @block.vector
        def _(v):
            v.wait_ge(s_sm, 32)
            v.tensor_scalar(bcol[:, :], scol[:, :], -128.0, None, ALU.mult).then_inc(s_bi, 1)

        @block.scalar
        def _(sc):
            sc.wait_ge(s_bi, 1)
            for ib in range(NI):
                sc.wait_ge(s_l[ib % 2], (ib // 2 + 1) * 16)
                sc.activation(wq[:, ib * W:(ib + 1) * W],
                              lw[:, (ib % 2) * W:(ib % 2 + 1) * W], AF.Identity,
                              bias=bcol[:, ib:ib + 1],
                              scale=scol[:, ib:ib + 1]).then_inc(s_dq, 1)
            cnt = 0
            for k in range(NK):
                for ib in range(NI):
                    cnt += 1
                    if ib == 0 and k >= 2:
                        sc.wait_ge(s_ar[k % 2], (k // 2) * 16)
                    sc.wait_ge(s_tp, cnt)
                    sc.copy(arow[:, (k % 2) * NI * 128 + ib * 128:
                                 (k % 2) * NI * 128 + (ib + 1) * 128],
                            (pst0 if cnt % 2 == 0 else pst1)[:, 0:128]).then_inc(s_ev, 1)

        @block.tensor
        def _(te):
            te.wait_ge(s_dq, NI)
            cnt = 0
            for k in range(NK):
                for ib in range(NI):
                    cnt += 1
                    if cnt > 2:
                        te.wait_ge(s_ev, cnt - 2)
                    if k >= 2 and ib == 0:
                        pass
                    te.transpose((pst0 if cnt % 2 == 0 else pst1)[:, 0:128],
                                 wq[:, ib * W + k * 128:ib * W + (k + 1) * 128],
                                 ident[:, :]).then_inc(s_tp, 1)

        @block.gpsimd
        def _(g):
            for k in range(NK):
                g.wait_ge(s_ev, (k + 1) * NI)
                g.dma_start(out=outT[k * 128:(k + 1) * 128, :],
                            in_=arow[:, (k % 2) * NI * 128:(k % 2 + 1) * NI * 128]
                            ).then_inc(s_ar[k % 2], 16)
            g.wait_ge(s_ar[0], (NK // 2) * 16)
            g.wait_ge(s_ar[1], (NK // 2) * 16)

    nc.all_engine_barrier()


def _prep_decode_matrix(nc, ctx, sems, src, sc2_src, outT, NI=16, W=2048, tag=""):
    import contextlib
    _mctx = contextlib.ExitStack()
    with _mctx:
        sems = [_mctx.enter_context(nc.semaphore(f"pq{tag}_{i}")) for i in range(9)]
        _prep_decode_body(nc, sems, src, sc2_src, outT, NI, W, tag)
    return


def _prep_decode_body(nc, sems, src, sc2_src, outT, NI=16, W=2048, tag=""):
    s_l0, s_dq, s_tp, s_ev, s_ar0, s_er, s_sm, s_l1, s_ar1 = sems
    s_l = [s_l0, s_l1]
    s_ar = [s_ar0, s_ar1]
    NK = W // 128
    WP = W // 2
    NSC = W // 32
    with (
        nc.sbuf_tensor("q_lw"+tag, [128, 2 * WP], I32) as lw,
        nc.sbuf_tensor("q_wq"+tag, [128, NI * W], BF16) as wq,
        nc.sbuf_tensor("q_s1t"+tag, [128, NI * NSC], F32) as s1t,
        nc.sbuf_tensor("q_lo"+tag, [128, WP], I32) as lo,
        nc.sbuf_tensor("q_hi"+tag, [128, WP], I32) as hi,
        nc.sbuf_tensor("q_t0"+tag, [128, WP], I32) as t0,
        nc.sbuf_tensor("q_ub"+tag, [128, WP], I32) as ub,
        nc.sbuf_tensor("q_sb"+tag, [128, WP], I32) as sb,
        nc.sbuf_tensor("q_err"+tag, [128, WP], F32) as errt,
        nc.sbuf_tensor("q_vm"+tag, [128, WP], F32) as vm,
        nc.sbuf_tensor("q_arow"+tag, [128, 2 * NI * 128], BF16) as arow,
        nc.sbuf_tensor("q_ident"+tag, [128, 128], BF16) as ident,
        nc.psum_tensor("q_pst0"+tag, [128, 512], BF16) as pst0,
        nc.psum_tensor("q_pst1"+tag, [128, 512], BF16) as pst1,
        nc.Block() as block,
    ):
        identT = nc.inline_tensor(np.eye(128, dtype=np.float32).astype(
            ml_dtypes.bfloat16), "q_identc"+tag)
        CH = [0]
        ch = s_dq

        @block.sync
        def _(s):
            s.dma_start(out=ident[:, :], in_=identT.ap()).then_inc(s_sm, 16)
            s.dma_start(out=s1t[:, :].rearrange("p (b n) -> p b n", n=NSC),
                        in_=sc2_src.rearrange("(b p) n -> p b n", p=128)).then_inc(s_sm, 16)
            for ib in range(NI):
                if ib >= 2:
                    s.wait_ge(ch, (ib - 1) * 18)
                s.dma_start(out=lw[:, (ib % 2) * WP:(ib % 2 + 1) * WP],
                            in_=src[ib * 128:(ib + 1) * 128, :]).then_inc(s_l[ib % 2], 16)

        @block.vector
        def _(v):
            v.wait_ge(s_sm, 32)
            for ib in range(NI):
                v.wait_ge(s_l[ib % 2], (ib // 2 + 1) * 16)
                lws = lw[:, (ib % 2) * WP:(ib % 2 + 1) * WP]
                v.tensor_scalar(lo[:, :], lws, 15, None, ALU.bitwise_and).then_inc(ch, 1)
                v.tensor_scalar(hi[:, :], lws, 4, 15, ALU.logical_shift_right,
                                ALU.bitwise_and).then_inc(ch, 1)
                CH[0] += 2
                for pl, csrc in ((0, lo), (1, hi)):
                    v.wait_ge(ch, CH[0])
                    v.tensor_scalar(t0[:, :], csrc[:, :], 7, None,
                                    ALU.bitwise_and).then_inc(ch, 1)
                    CH[0] += 1
                    v.wait_ge(ch, CH[0])
                    v.tensor_scalar(ub[:, :], t0[:, :], 252, 4194304, ALU.add,
                                    ALU.mult).then_inc(ch, 1)
                    v.tensor_scalar(sb[:, :], csrc[:, :], 8, 28, ALU.bitwise_and,
                                    ALU.logical_shift_left).then_inc(ch, 1)
                    CH[0] += 2
                    v.wait_ge(ch, CH[0])
                    v.tensor_scalar(errt[:, :], t0[:, :], -0.25, 0.5, ALU.mult,
                                    ALU.add).then_inc(ch, 1)
                    CH[0] += 1
                    v.wait_ge(ch, CH[0])
                    v.tensor_scalar(errt[:, :], errt[:, :], 0.0, None,
                                    ALU.max).then_inc(ch, 1)
                    CH[0] += 1
                    v.wait_ge(ch, CH[0])
                    v.tensor_tensor(vm[:, :], ub[:, :].bitcast(F32), errt[:, :],
                                    ALU.subtract).then_inc(ch, 1)
                    CH[0] += 1
                    v.wait_ge(ch, CH[0])
                    v.tensor_tensor(vm[:, :].bitcast(I32), vm[:, :].bitcast(I32),
                                    sb[:, :], ALU.bitwise_or).then_inc(ch, 1)
                    CH[0] += 1
                    v.wait_ge(ch, CH[0])
                    sbc = s1t[:, ib * NSC:(ib + 1) * NSC].rearrange(
                        "p (n o) -> p n o", o=1).broadcast_to([128, NSC, 16])
                    ow = wq[:, ib * W + pl:(ib + 1) * W:2].rearrange(
                        "p (n o) -> p n o", o=16)
                    v.tensor_tensor(ow, vm[:, :].rearrange("p (n o) -> p n o", o=16),
                                    sbc, ALU.mult).then_inc(ch, 1)
                    CH[0] += 1

        @block.scalar
        def _(sc):
            tcnt = 0
            for k in range(NK):
                for ib in range(NI):
                    tcnt += 1
                    if ib == 0 and k >= 2:
                        sc.wait_ge(s_ar[k % 2], (k // 2) * 16)
                    sc.wait_ge(s_tp, tcnt)
                    sc.copy(arow[:, (k % 2) * NI * 128 + ib * 128:
                                 (k % 2) * NI * 128 + (ib + 1) * 128],
                            (pst0 if tcnt % 2 == 0 else pst1)[:, 0:128]).then_inc(s_ev, 1)

        @block.tensor
        def _(te):
            te.wait_ge(ch, NI * 18)
            tcnt = 0
            for k in range(NK):
                for ib in range(NI):
                    tcnt += 1
                    if tcnt > 2:
                        te.wait_ge(s_ev, tcnt - 2)
                    te.transpose((pst0 if tcnt % 2 == 0 else pst1)[:, 0:128],
                                 wq[:, ib * W + k * 128:ib * W + (k + 1) * 128],
                                 ident[:, :]).then_inc(s_tp, 1)

        @block.gpsimd
        def _(g):
            for k in range(NK):
                g.wait_ge(s_ev, (k + 1) * NI)
                g.dma_start(out=outT[k * 128:(k + 1) * 128, :],
                            in_=arow[:, (k % 2) * NI * 128:(k % 2 + 1) * NI * 128]
                            ).then_inc(s_ar[k % 2], 16)
            g.wait_ge(s_ar[0], (NK // 2) * 16)
            g.wait_ge(s_ar[1], (NK // 2) * 16)
    nc.all_engine_barrier()


def build_prep():
    nc = bass.Bass("TRN2", target_bir_lowering=False)
    IC0 = 8192 // NC
    IC1 = 16384 // NC
    g0w = nc.dram_tensor("g0w", [IC0, D], I32, kind="ExternalInput").ap()
    g0s = nc.dram_tensor("g0s", [IC0], F32, kind="ExternalInput").ap()
    u0w = nc.dram_tensor("u0w", [IC0, D], I32, kind="ExternalInput").ap()
    u0s = nc.dram_tensor("u0s", [IC0], F32, kind="ExternalInput").ap()
    d0w = nc.dram_tensor("d0w", [D, IC0], I32, kind="ExternalInput").ap()
    d0s = nc.dram_tensor("d0s", [D], F32, kind="ExternalInput").ap()
    g1w = nc.dram_tensor("g1w", [IC1, D // 2], I32, kind="ExternalInput").ap()
    g1s = nc.dram_tensor("g1s", [IC1, D // 32], F32, kind="ExternalInput").ap()
    u1w = nc.dram_tensor("u1w", [IC1, D // 2], I32, kind="ExternalInput").ap()
    u1s = nc.dram_tensor("u1s", [IC1, D // 32], F32, kind="ExternalInput").ap()
    d1w = nc.dram_tensor("d1w", [D, IC1 // 2], I32, kind="ExternalInput").ap()
    d1s = nc.dram_tensor("d1s", [D, IC1 // 32], F32, kind="ExternalInput").ap()
    W0gT = nc.dram_tensor("W0gT", [D, IC0], BF16, kind="ExternalOutput").ap()
    W0uT = nc.dram_tensor("W0uT", [D, IC0], BF16, kind="ExternalOutput").ap()
    W0dT = nc.dram_tensor("W0dT", [IC0, D], BF16, kind="ExternalOutput").ap()
    W1gT = nc.dram_tensor("W1gT", [D, IC1], BF16, kind="ExternalOutput").ap()
    W1uT = nc.dram_tensor("W1uT", [D, IC1], BF16, kind="ExternalOutput").ap()
    W1dT = nc.dram_tensor("W1dT", [IC1, D], BF16, kind="ExternalOutput").ap()

    import contextlib
    ctx = contextlib.ExitStack()
    with ctx:
        sems = [ctx.enter_context(nc.semaphore(f"p_s{i}")) for i in range(6)]
        only = os.environ.get("S0_ONLY", "")
        def want(t):
            return (not only) or t in only.split(",")
        if want("g0"):
            _prep_dequant_matrix(nc, ctx, sems, g0w, g0s, W0gT, NI=IC0 // 128, W=D, tag="g0")
        if want("u0"):
            _prep_dequant_matrix(nc, ctx, sems, u0w, u0s, W0uT, NI=IC0 // 128, W=D, tag="u0")
        if want("d0"):
            _prep_dequant_matrix(nc, ctx, sems, d0w, d0s, W0dT, NI=D // 128, W=IC0, tag="d0")
        if want("g1"):
            _prep_decode_matrix(nc, ctx, sems, g1w, g1s, W1gT, NI=IC1 // 128, W=D, tag="g1")
        if want("u1"):
            _prep_decode_matrix(nc, ctx, sems, u1w, u1s, W1uT, NI=IC1 // 128, W=D, tag="u1")
        if want("d1"):
            _prep_decode_matrix(nc, ctx, sems, d1w, d1s, W1dT, NI=D // 128, W=IC1, tag="d1")
    return nc


def prep_inputs(inp):
    IC0 = 8192 // NC
    IC1 = 16384 // NC
    maps = []
    for c in range(NC):
        i0 = slice(c * IC0, (c + 1) * IC0)
        i1 = slice(c * IC1, (c + 1) * IC1)
        pk1 = slice(c * IC1 // 2, (c + 1) * IC1 // 2)
        sc1 = slice(c * IC1 // 32, (c + 1) * IC1 // 32)
        maps.append({
            "g0w": np.ascontiguousarray(inp["L0_ffn_gate_w"][i0, :]),
            "g0s": np.ascontiguousarray(inp["L0_ffn_gate_scale"][i0]),
            "u0w": np.ascontiguousarray(inp["L0_ffn_up_w"][i0, :]),
            "u0s": np.ascontiguousarray(inp["L0_ffn_up_scale"][i0]),
            "d0w": np.ascontiguousarray(inp["L0_ffn_down_w"][:, i0]),
            "d0s": np.ascontiguousarray(inp["L0_ffn_down_scale"]),
            "g1w": np.ascontiguousarray(inp["L1_ffn_gate_W_packed"][i1, :]),
            "g1s": np.ascontiguousarray(inp["L1_ffn_gate_scales"][i1, :]),
            "u1w": np.ascontiguousarray(inp["L1_ffn_up_W_packed"][i1, :]),
            "u1s": np.ascontiguousarray(inp["L1_ffn_up_scales"][i1, :]),
            "d1w": np.ascontiguousarray(inp["L1_ffn_down_W_packed"][:, pk1]),
            "d1s": np.ascontiguousarray(inp["L1_ffn_down_scales"][:, sc1]),
        })
    return maps



def emit_ffn(nc, layer, ypre_in, WgT, WuT, WdT, out_dram, I, final_ln, x1_dram,
             w_views=None, w_ready=None):
    tag = f"f{layer}"
    NTB = TC // 128
    NK = D // 128
    IC = I // 512
    import contextlib
    ctx = contextlib.ExitStack()
    with ctx:
        x1T = ctx.enter_context(nc.sbuf_tensor("x1T" + tag, [128, NK * TC], BF16))
        s_ev = ctx.enter_context(nc.semaphore("sev" + tag))
        s_ya = ctx.enter_context(nc.semaphore("sya" + tag))
        ya = ctx.enter_context(nc.sbuf_tensor("ya" + tag, [128, NTB * D], F32))
        identb = nc.inline_tensor(np.eye(128, dtype=np.float32).astype(
            ml_dtypes.bfloat16), "idb" + tag)

        with (
            nc.sbuf_tensor("yp" + tag, [128, 2 * D], F32) as yp,
            nc.sbuf_tensor("sq" + tag, [128, D], F32) as sq,
            nc.sbuf_tensor("x1f" + tag, [128, 2 * D], F32) as x1f,
            nc.sbuf_tensor("x1b" + tag, [128, D], BF16) as x1b,
            nc.sbuf_tensor("st" + tag, [128, 8], F32) as st,
            nc.sbuf_tensor("rstd" + tag, [128, 1], F32) as rstd,
            nc.sbuf_tensor("idl" + tag, [128, 128], BF16) as idl,
            nc.psum_tensor("pt0" + tag, [128, 512], BF16) as pt0,
            nc.psum_tensor("pt1" + tag, [128, 512], BF16) as pt1,
            nc.semaphore("sl0" + tag) as s_l0,
            nc.semaphore("sl1" + tag) as s_l1,
            nc.semaphore("ssq" + tag) as s_sq,
            nc.semaphore("sst" + tag) as s_st,
            nc.semaphore("sx1" + tag) as s_x1,
            nc.semaphore("sxb" + tag) as s_xb,
            nc.semaphore("stp" + tag) as s_tp,
            nc.semaphore("sxd" + tag) as s_xd,
            nc.semaphore("sidl" + tag) as s_idl,
            nc.Block() as block,
        ):
            s_l = [s_l0, s_l1]

            @block.sync
            def _(s):
                s.dma_start(out=idl[:, :], in_=identb.ap()).then_inc(s_idl, 16)
                for tb in range(NTB):
                    if tb >= 2:
                        s.wait_ge(s_x1, tb - 1)
                    s.dma_start(out=yp[:, (tb % 2) * D:(tb % 2 + 1) * D],
                                in_=ypre_in[tb * 128:(tb + 1) * 128, :]).then_inc(
                                    s_l[tb % 2], 16)

            @block.scalar
            def _(sc):
                for tb in range(NTB):
                    yps = yp[:, (tb % 2) * D:(tb % 2 + 1) * D]
                    sc.wait_ge(s_l[tb % 2], (tb // 2 + 1) * 16)
                    if tb > 0:
                        sc.wait_ge(s_st, tb * 9)
                    sc.activation(sq[:, :], yps, AF.Square).then_inc(s_sq, 1)
                    sc.wait_ge(s_st, tb * 9 + 7)
                    sc.sqrt(st[:, 6:7], st[:, 5:6]).then_inc(s_sq, 1)
                    sc.wait_ge(s_st, tb * 9 + 9)
                    if tb >= 2:
                        sc.wait_ge(s_xd, tb * 16)
                    sc.activation(x1f[:, (tb % 2) * D:(tb % 2 + 1) * D], yps,
                                  AF.Identity, bias=st[:, 7:8],
                                  scale=rstd[:, 0:1]).then_inc(s_x1, 1)

            @block.vector
            def _(v):
                for tb in range(NTB):
                    yps = yp[:, (tb % 2) * D:(tb % 2 + 1) * D]
                    v.wait_ge(s_l[tb % 2], (tb // 2 + 1) * 16)
                    if tb > 0:
                        v.wait_ge(s_x1, tb)
                    v.reduce_sum(st[:, 0:1], yps, mybir.AxisListType.X).then_inc(s_st, 1)
                    v.wait_ge(s_sq, 2 * tb + 1)
                    v.reduce_sum(st[:, 1:2], sq[:, :], mybir.AxisListType.X).then_inc(s_st, 1)
                    v.wait_ge(s_st, tb * 9 + 2)
                    v.tensor_scalar(st[:, 2:3], st[:, 0:1], 1.0 / D, None, ALU.mult).then_inc(s_st, 1)
                    v.tensor_scalar(st[:, 3:4], st[:, 1:2], 1.0 / D, None, ALU.mult).then_inc(s_st, 1)
                    v.wait_ge(s_st, tb * 9 + 4)
                    v.tensor_tensor(st[:, 4:5], st[:, 2:3], st[:, 2:3], ALU.mult).then_inc(s_st, 1)
                    v.wait_ge(s_st, tb * 9 + 5)
                    v.tensor_tensor(st[:, 5:6], st[:, 3:4], st[:, 4:5], ALU.subtract).then_inc(s_st, 1)
                    v.wait_ge(s_st, tb * 9 + 6)
                    v.tensor_scalar(st[:, 5:6], st[:, 5:6], 1e-5, None, ALU.add).then_inc(s_st, 1)
                    v.wait_ge(s_sq, 2 * tb + 2)
                    v.reciprocal(rstd[:, 0:1], st[:, 6:7]).then_inc(s_st, 1)
                    v.wait_ge(s_st, tb * 9 + 8)
                    v.tensor_scalar(st[:, 7:8], st[:, 2:3], -1.0, rstd[:, 0:1],
                                    ALU.mult, ALU.mult).then_inc(s_st, 1)
                    v.wait_ge(s_x1, tb + 1)
                    if tb > 0:
                        v.wait_ge(s_tp, tb * NK)
                    v.tensor_copy(x1b[:, :], x1f[:, (tb % 2) * D:(tb % 2 + 1) * D]
                                  ).then_inc(s_xb, 1)
                    for k in range(NK):
                        cntv = tb * NK + k + 1
                        v.wait_ge(s_tp, cntv)
                        v.tensor_copy(x1T[:, k * TC + tb * 128:k * TC + (tb + 1) * 128],
                                      (pt0 if cntv % 2 == 0 else pt1)[:, 0:128]
                                      ).then_inc(s_ev, 1)

            @block.gpsimd
            def _(g):
                for tb in range(NTB):
                    g.wait_ge(s_x1, tb + 1)
                    g.dma_start(out=x1_dram[tb * 128:(tb + 1) * 128, :],
                                in_=x1f[:, (tb % 2) * D:(tb % 2 + 1) * D]).then_inc(s_xd, 16)
                g.wait_ge(s_xd, NTB * 16)

            @block.tensor
            def _(te):
                te.wait_ge(s_idl, 16)
                cnt = 0
                for tb in range(NTB):
                    te.wait_ge(s_xb, tb + 1)
                    for k in range(NK):
                        cnt += 1
                        if cnt > 2:
                            te.wait_ge(s_ev, cnt - 2)
                        te.transpose((pt0 if cnt % 2 == 0 else pt1)[:, 0:128],
                                     x1b[:, k * 128:(k + 1) * 128],
                                     idl[:, :]).then_inc(s_tp, 1)

        nc.all_engine_barrier()

        NSUB = 4
        TH = TC // 512
        import contextlib as _ct
        _fctx = _ct.ExitStack()
        with _fctx:
            _e = _fctx.enter_context
            wg = _e(nc.sbuf_tensor("wg" + tag, [128, 2 * NK * 512], BF16))
            wu = _e(nc.sbuf_tensor("wu" + tag, [128, 2 * NK * 512], BF16))
            wd = _e(nc.sbuf_tensor("wd" + tag, [128, 4 * D], BF16))
            sgt = _e(nc.sbuf_tensor("sg" + tag, [128, NSUB * TC], BF16))
            uet = _e(nc.sbuf_tensor("ue" + tag, [128, NSUB * TC], BF16))
            itT = _e(nc.sbuf_tensor("itT" + tag, [128, NSUB * TC], BF16))
            if layer == 1:
                aat = _e(nc.sbuf_tensor("aa" + tag, [128, TC], BF16))
                bbt = _e(nc.sbuf_tensor("bb" + tag, [128, TC], BF16))
            pg = _e(nc.psum_tensor("pg" + tag, [128, TC], F32))
            pu = _e(nc.psum_tensor("pu" + tag, [128, TC], F32))
            py0 = _e(nc.psum_tensor("py0" + tag, [128, D // 2], F32))
            py1 = _e(nc.psum_tensor("py1" + tag, [128, D // 2], F32))
            pys = [py0, py1]
            s_w0 = _e(nc.semaphore("sw0" + tag))
            s_w1 = _e(nc.semaphore("sw1" + tag))
            s_wd = _e(nc.semaphore("swd" + tag))
            s_mg = _e(nc.semaphore("smg" + tag))
            s_mu = _e(nc.semaphore("smu" + tag))
            s_my = _e(nc.semaphore("smy" + tag))
            s_ge = _e(nc.semaphore("sge" + tag))
            s_ue = _e(nc.semaphore("sue" + tag))
            s_it = _e(nc.semaphore("sit" + tag))
            s_qq = _e(nc.semaphore("sqq" + tag))
            s_sg = _e(nc.semaphore("ssg" + tag))
            ch_v = _e(nc.semaphore("svch" + tag))
            block = _e(nc.Block())
            s_w = [s_w0, s_w1]

            @block.sync
            def _(s):
                if w_ready is not None:
                    s.wait_ge(w_ready[0], w_ready[1])
                for ic in range(IC):
                    if ic >= 2:
                        s.wait_ge(s_mg, (ic - 1) * NSUB * TH)
                        s.wait_ge(s_mu, (ic - 1) * NSUB * TH)
                    if w_views is None:
                        gsrc = WgT.rearrange("(k p) i -> p k i", p=128)[
                            :, :, ic * 512:(ic + 1) * 512]
                        usrc = WuT.rearrange("(k p) i -> p k i", p=128)[
                            :, :, ic * 512:(ic + 1) * 512]
                    else:
                        gsrc, usrc = w_views(ic)
                    s.dma_start(
                        out=wg[:, (ic % 2) * NK * 512:(ic % 2 + 1) * NK * 512].rearrange(
                            "p (k i) -> p k i", k=NK),
                        in_=gsrc).then_inc(s_w[ic % 2], 16)
                    s.dma_start(
                        out=wu[:, (ic % 2) * NK * 512:(ic % 2 + 1) * NK * 512].rearrange(
                            "p (k i) -> p k i", k=NK),
                        in_=usrc).then_inc(s_w[ic % 2], 16)

            @block.gpsimd
            def _(g):
                for ic in range(IC):
                    if ic >= 1:
                        g.wait_ge(s_my, ic * NTB * 2)
                    g.dma_start(
                        out=wd[:, :].rearrange("p (r h) -> p r h", r=4),
                        in_=WdT[ic * 512:(ic + 1) * 512, :].rearrange(
                            "(r p) h -> p r h", p=128)).then_inc(s_wd, 16)

            @block.tensor
            def _(te):
                def down(j):
                    te.wait_ge(s_it, (j + 1) * NSUB)
                    te.wait_ge(s_wd, (j + 1) * 16)
                    for tb in range(NTB):
                        for dh in range(2):
                            g = (j * NTB + tb) * 2 + dh
                            if g > 1:
                                te.wait_ge(s_ya, g - 1)
                            for isub in range(NSUB):
                                for hc in range(2):
                                    mm = te.matmul(
                                        pys[g % 2][:, hc * 512:(hc + 1) * 512],
                                        itT[:, isub * TC + tb * 128:
                                            isub * TC + (tb + 1) * 128],
                                        wd[:, isub * D + dh * 1024 + hc * 512:
                                           isub * D + dh * 1024 + (hc + 1) * 512],
                                        start=(isub == 0), stop=(isub == NSUB - 1))
                            mm.then_inc(s_my, 1)
                for ic in range(IC):
                    for isub in range(NSUB):
                        g_ = ic * NSUB + isub
                        if isub == 0:
                            te.wait_ge(s_w[ic % 2], (ic // 2 + 1) * 32)
                        if g_ == 0:
                            te.wait_ge(s_ev, NTB * NK)
                        if g_ > 0:
                            te.wait_ge(s_ge, g_)
                        wslot = (ic % 2) * NK * 512
                        for k in range(NK):
                            for th in range(TH):
                                mm = te.matmul(
                                    pg[:, th * 512:(th + 1) * 512],
                                    wg[:, wslot + k * 512 + isub * 128:
                                       wslot + k * 512 + (isub + 1) * 128],
                                    x1T[:, k * TC + th * 512:k * TC + (th + 1) * 512],
                                    start=(k == 0), stop=(k == NK - 1))
                                if k == NK - 1:
                                    mm.then_inc(s_mg, 1)
                        if g_ > 0:
                            te.wait_ge(s_ue, g_)
                        for k in range(NK):
                            for th in range(TH):
                                mm = te.matmul(
                                    pu[:, th * 512:(th + 1) * 512],
                                    wu[:, wslot + k * 512 + isub * 128:
                                       wslot + k * 512 + (isub + 1) * 128],
                                    x1T[:, k * TC + th * 512:k * TC + (th + 1) * 512],
                                    start=(k == 0), stop=(k == NK - 1))
                                if k == NK - 1:
                                    mm.then_inc(s_mu, 1)
                    if ic >= 1:
                        down(ic - 1)
                down(IC - 1)

            @block.scalar
            def _(sc):
                def sig(j, isub):
                    sc.wait_ge(s_qq, j * NSUB + isub + 1)
                    sc.activation(bbt[:, :], aat[:, :],
                                  AF.Sigmoid).then_inc(s_sg, 1)

                def evac(ic, isub):
                    g_ = ic * NSUB + isub
                    sc.wait_ge(s_mg, g_ * TH + TH)
                    if ic >= 1:
                        sc.wait_ge(s_it, (ic - 1) * NSUB + isub + 1)
                    if layer == 0:
                        sc.activation(sgt[:, isub * TC:(isub + 1) * TC],
                                      pg[:, :], AF.Silu).then_inc(s_ge, 1)
                    else:
                        sc.copy(sgt[:, isub * TC:(isub + 1) * TC],
                                pg[:, :]).then_inc(s_ge, 1)
                for ic in range(IC):
                    if layer == 1 and ic >= 1:
                        sig(ic - 1, 0)
                    evac(ic, 0)
                    if layer == 1 and ic >= 1:
                        sig(ic - 1, 1)
                        sig(ic - 1, 2)
                        sig(ic - 1, 3)
                    evac(ic, 1)
                    evac(ic, 2)
                    evac(ic, 3)
                if layer == 1:
                    for isub in range(NSUB):
                        sig(IC - 1, isub)

            CHV = [0]

            @block.vector
            def _(v):
                def accum(j):
                    for tb in range(NTB):
                        for dh in range(2):
                            g = (j * NTB + tb) * 2 + dh
                            v.wait_ge(s_my, g + 1)
                            if j >= 1:
                                v.wait_ge(s_ya, g - NTB * 2 + 1)
                            ysl = ya[:, tb * D + dh * 1024:tb * D + (dh + 1) * 1024]
                            if j == 0:
                                v.tensor_copy(ysl, pys[g % 2][:, :]).then_inc(s_ya, 1)
                            else:
                                v.tensor_tensor(ysl, ysl, pys[g % 2][:, :],
                                                ALU.add).then_inc(s_ya, 1)
                for ic in range(IC):
                    for isub in range(NSUB):
                        g_ = ic * NSUB + isub
                        v.wait_ge(s_mu, g_ * TH + TH)
                        if ic >= 1:
                            v.wait_ge(s_it, (ic - 1) * NSUB + isub + 1)
                        v.tensor_copy(uet[:, isub * TC:(isub + 1) * TC],
                                      pu[:, :]).then_inc(s_ue, 1)
                    if ic >= 1:
                        accum(ic - 1)
                    v.wait_ge(s_my, ic * NTB * 2)
                    for isub in range(NSUB):
                        g_ = ic * NSUB + isub
                        sgs = sgt[:, isub * TC:(isub + 1) * TC]
                        ues = uet[:, isub * TC:(isub + 1) * TC]
                        its = itT[:, isub * TC:(isub + 1) * TC]
                        if layer == 0:
                            v.wait_ge(s_ge, g_ + 1)
                            v.tensor_tensor(its, sgs, ues, ALU.mult).then_inc(s_it, 1)
                        else:
                            v.wait_ge(s_ge, g_ + 1)
                            v.tensor_tensor(aat[:, :], sgs, sgs, ALU.mult).then_inc(ch_v, 1)
                            CHV[0] += 1
                            v.wait_ge(ch_v, CHV[0])
                            v.tensor_scalar(bbt[:, :], aat[:, :], 0.07135481627,
                                            1.595769122, ALU.mult, ALU.add).then_inc(ch_v, 1)
                            CHV[0] += 1
                            v.wait_ge(ch_v, CHV[0])
                            v.tensor_tensor(aat[:, :], sgs, bbt[:, :],
                                            ALU.mult).then_inc(s_qq, 1)
                            v.wait_ge(s_sg, g_ + 1)
                            v.tensor_tensor(aat[:, :], sgs, bbt[:, :],
                                            ALU.mult).then_inc(ch_v, 1)
                            CHV[0] += 1
                            v.wait_ge(ch_v, CHV[0])
                            v.tensor_tensor(its, aat[:, :], ues, ALU.mult).then_inc(s_it, 1)
                accum(IC - 1)
        nc.all_engine_barrier()

        with (
            nc.sbuf_tensor("x1r" + tag, [128, 2 * D], F32) as x1r,
            nc.sbuf_tensor("rr" + tag, [128, 2 * D], F32) as rr,
            nc.sbuf_tensor("ro" + tag, [128, 2 * D], F32) as ro,
            nc.sbuf_tensor("sqR" + tag, [128, D], F32) as sqR,
            nc.sbuf_tensor("stR" + tag, [128, 8], F32) as stR,
            nc.sbuf_tensor("rstdR" + tag, [128, 1], F32) as rstdR,
            nc.semaphore("srl0" + tag) as s_rl0,
            nc.semaphore("srl1" + tag) as s_rl1,
            nc.semaphore("sra" + tag) as s_ra,
            nc.semaphore("srs" + tag) as s_rs,
            nc.semaphore("srq" + tag) as s_rq,
            nc.semaphore("sro" + tag) as s_ro,
            nc.semaphore("sfin" + tag) as s_fin,
            nc.Block() as block,
        ):
            s_rl = [s_rl0, s_rl1]

            @block.sync
            def _(s):
                for tb in range(NTB):
                    if tb >= 2:
                        s.wait_ge(s_ra, tb - 1)
                    s.dma_start(out=x1r[:, (tb % 2) * D:(tb % 2 + 1) * D],
                                in_=x1_dram[tb * 128:(tb + 1) * 128, :]).then_inc(
                                    s_rl[tb % 2], 16)

            @block.vector
            def _(v):
                for tb in range(NTB):
                    v.wait_ge(s_rl[tb % 2], (tb // 2 + 1) * 16)
                    if tb >= 2:
                        v.wait_ge(s_ro, tb * 16)
                    v.tensor_tensor(rr[:, (tb % 2) * D:(tb % 2 + 1) * D],
                                    x1r[:, (tb % 2) * D:(tb % 2 + 1) * D],
                                    ya[:, tb * D:(tb + 1) * D], ALU.add).then_inc(s_ra, 1)
                    if final_ln:
                        rrs = rr[:, (tb % 2) * D:(tb % 2 + 1) * D]
                        if tb > 0:
                            v.wait_ge(s_fin, tb)
                        v.wait_ge(s_ra, tb + 1)
                        v.reduce_sum(stR[:, 0:1], rrs, mybir.AxisListType.X).then_inc(s_rs, 1)
                        v.wait_ge(s_rq, 2 * tb + 1)
                        v.reduce_sum(stR[:, 1:2], sqR[:, :], mybir.AxisListType.X).then_inc(s_rs, 1)
                        v.wait_ge(s_rs, tb * 9 + 2)
                        v.tensor_scalar(stR[:, 2:3], stR[:, 0:1], 1.0 / D, None, ALU.mult).then_inc(s_rs, 1)
                        v.tensor_scalar(stR[:, 3:4], stR[:, 1:2], 1.0 / D, None, ALU.mult).then_inc(s_rs, 1)
                        v.wait_ge(s_rs, tb * 9 + 4)
                        v.tensor_tensor(stR[:, 4:5], stR[:, 2:3], stR[:, 2:3], ALU.mult).then_inc(s_rs, 1)
                        v.wait_ge(s_rs, tb * 9 + 5)
                        v.tensor_tensor(stR[:, 5:6], stR[:, 3:4], stR[:, 4:5], ALU.subtract).then_inc(s_rs, 1)
                        v.wait_ge(s_rs, tb * 9 + 6)
                        v.tensor_scalar(stR[:, 5:6], stR[:, 5:6], 1e-5, None, ALU.add).then_inc(s_rs, 1)
                        v.wait_ge(s_rq, 2 * tb + 2)
                        v.reciprocal(rstdR[:, 0:1], stR[:, 6:7]).then_inc(s_rs, 1)
                        v.wait_ge(s_rs, tb * 9 + 8)
                        v.tensor_scalar(stR[:, 7:8], stR[:, 2:3], -1.0, rstdR[:, 0:1],
                                        ALU.mult, ALU.mult).then_inc(s_rs, 1)

            if final_ln:
                @block.scalar
                def _(sc):
                    for tb in range(NTB):
                        rrs = rr[:, (tb % 2) * D:(tb % 2 + 1) * D]
                        sc.wait_ge(s_ra, tb + 1)
                        if tb > 0:
                            sc.wait_ge(s_rs, tb * 9)
                        sc.activation(sqR[:, :], rrs, AF.Square).then_inc(s_rq, 1)
                        sc.wait_ge(s_rs, tb * 9 + 7)
                        sc.sqrt(stR[:, 6:7], stR[:, 5:6]).then_inc(s_rq, 1)
                        sc.wait_ge(s_rs, tb * 9 + 9)
                        if tb >= 2:
                            sc.wait_ge(s_ro, tb * 16)
                        sc.activation(ro[:, (tb % 2) * D:(tb % 2 + 1) * D], rrs,
                                      AF.Identity, bias=stR[:, 7:8],
                                      scale=rstdR[:, 0:1]).then_inc(s_fin, 1)

            @block.gpsimd
            def _(g):
                for tb in range(NTB):
                    if final_ln:
                        g.wait_ge(s_fin, tb + 1)
                        srcap = ro[:, (tb % 2) * D:(tb % 2 + 1) * D]
                    else:
                        g.wait_ge(s_ra, tb + 1)
                        srcap = rr[:, (tb % 2) * D:(tb % 2 + 1) * D]
                    g.dma_start(out=out_dram[tb * 128:(tb + 1) * 128, :],
                                in_=srcap).then_inc(s_ro, 16)
                g.wait_ge(s_ro, NTB * 16)
        nc.all_engine_barrier()


def build_ffn(layer):
    nc = bass.Bass("TRN2", target_bir_lowering=False)
    I = 8192 if layer == 0 else 16384
    ypre = nc.dram_tensor("ypre", [TC, D], F32, kind="ExternalInput").ap()
    WgT = nc.dram_tensor("WgT", [D, I], BF16, kind="ExternalInput").ap()
    WuT = nc.dram_tensor("WuT", [D, I], BF16, kind="ExternalInput").ap()
    WdT = nc.dram_tensor("WdT", [I, D], BF16, kind="ExternalInput").ap()
    outd = nc.dram_tensor("outd", [TC, D], F32, kind="ExternalOutput").ap()
    x1_dram = nc.dram_tensor("x1_dram", [TC, D], F32, kind="Internal")
    emit_ffn(nc, layer, ypre, WgT, WuT, WdT, outd, I, final_ln=(layer == 1),
             x1_dram=x1_dram.ap())
    return nc



def emit_attn(nc, ypre_in, out_dram):
    tag = "a3"
    NTB = TC // 128
    NK = D // 128
    NSEG = TC // 512
    H = 16
    SC = 0.08838834764831845
    import contextlib
    ctx = contextlib.ExitStack()
    with ctx:
        _e = ctx.enter_context
        x2f = _e(nc.sbuf_tensor("x2f" + tag, [128, NTB * D], F32))
        x2T = _e(nc.sbuf_tensor("x2T" + tag, [128, NK * TC], BF16))
        s_ev = _e(nc.semaphore("sev" + tag))

        perms = {}
        for p_ in range(2):
            for hf in range(2):
                m = np.zeros((128, 128), np.float32)
                for qp in range(64 * hf, 64 * (hf + 1)):
                    m[qp, 2 * (qp - 64 * hf) + p_] = 1.0
                perms[(p_, hf)] = nc.inline_tensor(
                    m.astype(ml_dtypes.bfloat16), f"perm{p_}{hf}" + tag)
        identb = nc.inline_tensor(np.eye(128, dtype=np.float32).astype(
            ml_dtypes.bfloat16), "ida" + tag)

        with (
            nc.sbuf_tensor("yp" + tag, [128, 2 * D], F32) as yp,
            nc.sbuf_tensor("sq" + tag, [128, D], F32) as sq,
            nc.sbuf_tensor("x1b" + tag, [128, D], BF16) as x1b,
            nc.sbuf_tensor("st" + tag, [128, 8], F32) as st,
            nc.sbuf_tensor("rstd" + tag, [128, 1], F32) as rstd,
            nc.sbuf_tensor("idl" + tag, [128, 128], BF16) as idl,
            nc.psum_tensor("pt0" + tag, [128, 512], BF16) as pt0,
            nc.psum_tensor("pt1" + tag, [128, 512], BF16) as pt1,
            nc.semaphore("sl0" + tag) as s_l0,
            nc.semaphore("sl1" + tag) as s_l1,
            nc.semaphore("ssq" + tag) as s_sq,
            nc.semaphore("sst" + tag) as s_st,
            nc.semaphore("sx1" + tag) as s_x1,
            nc.semaphore("sxb" + tag) as s_xb,
            nc.semaphore("stp" + tag) as s_tp,
            nc.semaphore("sidl" + tag) as s_idl,
            nc.Block() as block,
        ):
            s_l = [s_l0, s_l1]

            @block.sync
            def _(s):
                s.dma_start(out=idl[:, :], in_=identb.ap()).then_inc(s_idl, 16)
                for tb in range(NTB):
                    if tb >= 2:
                        s.wait_ge(s_x1, tb - 1)
                    s.dma_start(out=yp[:, (tb % 2) * D:(tb % 2 + 1) * D],
                                in_=ypre_in[tb * 128:(tb + 1) * 128, :]).then_inc(
                                    s_l[tb % 2], 16)

            @block.scalar
            def _(sc):
                for tb in range(NTB):
                    yps = yp[:, (tb % 2) * D:(tb % 2 + 1) * D]
                    sc.wait_ge(s_l[tb % 2], (tb // 2 + 1) * 16)
                    if tb > 0:
                        sc.wait_ge(s_st, tb * 9)
                    sc.activation(sq[:, :], yps, AF.Square).then_inc(s_sq, 1)
                    sc.wait_ge(s_st, tb * 9 + 7)
                    sc.sqrt(st[:, 6:7], st[:, 5:6]).then_inc(s_sq, 1)
                    sc.wait_ge(s_st, tb * 9 + 9)
                    sc.activation(x2f[:, tb * D:(tb + 1) * D], yps,
                                  AF.Identity, bias=st[:, 7:8],
                                  scale=rstd[:, 0:1]).then_inc(s_x1, 1)

            @block.vector
            def _(v):
                for tb in range(NTB):
                    yps = yp[:, (tb % 2) * D:(tb % 2 + 1) * D]
                    v.wait_ge(s_l[tb % 2], (tb // 2 + 1) * 16)
                    if tb > 0:
                        v.wait_ge(s_x1, tb)
                    v.reduce_sum(st[:, 0:1], yps, mybir.AxisListType.X).then_inc(s_st, 1)
                    v.wait_ge(s_sq, 2 * tb + 1)
                    v.reduce_sum(st[:, 1:2], sq[:, :], mybir.AxisListType.X).then_inc(s_st, 1)
                    v.wait_ge(s_st, tb * 9 + 2)
                    v.tensor_scalar(st[:, 2:3], st[:, 0:1], 1.0 / D, None, ALU.mult).then_inc(s_st, 1)
                    v.tensor_scalar(st[:, 3:4], st[:, 1:2], 1.0 / D, None, ALU.mult).then_inc(s_st, 1)
                    v.wait_ge(s_st, tb * 9 + 4)
                    v.tensor_tensor(st[:, 4:5], st[:, 2:3], st[:, 2:3], ALU.mult).then_inc(s_st, 1)
                    v.wait_ge(s_st, tb * 9 + 5)
                    v.tensor_tensor(st[:, 5:6], st[:, 3:4], st[:, 4:5], ALU.subtract).then_inc(s_st, 1)
                    v.wait_ge(s_st, tb * 9 + 6)
                    v.tensor_scalar(st[:, 5:6], st[:, 5:6], 1e-5, None, ALU.add).then_inc(s_st, 1)
                    v.wait_ge(s_sq, 2 * tb + 2)
                    v.reciprocal(rstd[:, 0:1], st[:, 6:7]).then_inc(s_st, 1)
                    v.wait_ge(s_st, tb * 9 + 8)
                    v.tensor_scalar(st[:, 7:8], st[:, 2:3], -1.0, rstd[:, 0:1],
                                    ALU.mult, ALU.mult).then_inc(s_st, 1)
                    v.wait_ge(s_x1, tb + 1)
                    if tb > 0:
                        v.wait_ge(s_tp, tb * NK)
                    v.tensor_copy(x1b[:, :], x2f[:, tb * D:(tb + 1) * D]).then_inc(s_xb, 1)
                    for k in range(NK):
                        cntv = tb * NK + k + 1
                        v.wait_ge(s_tp, cntv)
                        v.tensor_copy(x2T[:, k * TC + tb * 128:k * TC + (tb + 1) * 128],
                                      (pt0 if cntv % 2 == 0 else pt1)[:, 0:128]
                                      ).then_inc(s_ev, 1)

            @block.tensor
            def _(te):
                te.wait_ge(s_idl, 16)
                cnt = 0
                for tb in range(NTB):
                    te.wait_ge(s_xb, tb + 1)
                    for k in range(NK):
                        cnt += 1
                        if cnt > 2:
                            te.wait_ge(s_ev, cnt - 2)
                        te.transpose((pt0 if cnt % 2 == 0 else pt1)[:, 0:128],
                                     x1b[:, k * 128:(k + 1) * 128],
                                     idl[:, :]).then_inc(s_tp, 1)

        nc.all_engine_barrier()

        import contextlib as _ct
        _actx = _ct.ExitStack()
        with _actx:
            _a = _actx.enter_context
            pm = _a(nc.sbuf_tensor("pm" + tag, [128, 4 * 128], BF16))
            ida2 = _a(nc.sbuf_tensor("ida2" + tag, [128, 128], BF16))
            pb = _a(nc.sbuf_tensor("pb" + tag, [128, 2 * 256], BF16))
            atn = _a(nc.sbuf_tensor("atn" + tag, [128, 2 * 256], BF16))
            atT = _a(nc.sbuf_tensor("atT" + tag, [128, 2 * 256], BF16))
            gk = _a(nc.sbuf_tensor("gk" + tag, [128, 2 * 128], BF16))
            ob = _a(nc.sbuf_tensor("ob" + tag, [128, 2 * 128], BF16))
            sm = _a(nc.sbuf_tensor("sm" + tag, [128, 6], F32))
            attsb = _a(nc.sbuf_tensor("att" + tag, [128, NTB * D], F32))
            pss0 = _a(nc.psum_tensor("pss0" + tag, [128, 256], F32))
            pss1 = _a(nc.psum_tensor("pss1" + tag, [128, 256], F32))
            ptr0 = _a(nc.psum_tensor("ptr0" + tag, [128, 128], BF16))
            ptr1 = _a(nc.psum_tensor("ptr1" + tag, [128, 128], BF16))
            pso = _a(nc.psum_tensor("pso" + tag, [128, 128], F32))
            ppm = _a(nc.psum_tensor("ppm" + tag, [128, 128], F32))
            s_pm = _a(nc.semaphore("spm" + tag))
            s_ms = _a(nc.semaphore("sms" + tag))
            s_sx = _a(nc.semaphore("ssx" + tag))
            ch = _a(nc.semaphore("svx" + tag))
            s_tr = _a(nc.semaphore("str" + tag))
            s_te = _a(nc.semaphore("ste" + tag))
            s_mo = _a(nc.semaphore("smo" + tag))
            s_oe = _a(nc.semaphore("soe" + tag))
            s_mp = _a(nc.semaphore("smp" + tag))
            s_pc = _a(nc.semaphore("spc" + tag))
            s_ro = _a(nc.semaphore("sro" + tag))
            s_ra = _a(nc.semaphore("sraa" + tag))
            block = _a(nc.Block())

            @block.sync
            def _(s):
                s.dma_start(out=ida2[:, :], in_=identb.ap()).then_inc(s_pm, 16)
                for i, (p_, hf) in enumerate([(0, 0), (0, 1), (1, 0), (1, 1)]):
                    s.dma_start(out=pm[:, i * 128:(i + 1) * 128],
                                in_=perms[(p_, hf)].ap()).then_inc(s_pm, 16)

            @block.tensor
            def _(te):
                te.wait_ge(s_pm, 80)
                trc = 0
                for s_ in range(NSEG):
                    for h in range(H):
                        idx = s_ * H + h
                        p_ = h % 2
                        base = h * TC + s_ * 512 + p_
                        gTs = [x2T[:, base + 2 * qh * 128:base + 2 * (qh + 1) * 128 - 1:2]
                               for qh in range(2)]
                        gTfull = x2T[:, base:base + 511:2]
                        for qh in range(2):
                            if idx >= 1:
                                te.wait_ge(ch, (idx - 1) * 10 + 5 * qh + 1)
                                te.wait_ge(s_sx, 2 * (idx - 1) + qh + 1)
                            te.matmul((pss0 if qh == 0 else pss1)[:, :], gTs[qh],
                                      gTfull, start=True, stop=True).then_inc(s_ms, 1)
                        te.wait_ge(ch, idx * 10 + 10)
                        for j in range(6):
                            trc += 1
                            if trc > 2:
                                te.wait_ge(s_te, trc - 2)
                            if j < 4:
                                qh, kc = j // 2, j % 2
                                te.transpose((ptr0 if trc % 2 == 0 else ptr1)[:, :],
                                             atn[:, qh * 256 + kc * 128:
                                                 qh * 256 + (kc + 1) * 128],
                                             ida2[:, :]).then_inc(s_tr, 1)
                            else:
                                kc = j - 4
                                te.transpose((ptr0 if trc % 2 == 0 else ptr1)[:, :],
                                             gTs[kc], ida2[:, :]).then_inc(s_tr, 1)
                        te.wait_ge(s_te, idx * 6 + 6)
                        for qh in range(2):
                            te.wait_ge(s_oe, idx * 2 + qh)
                            for kc in range(2):
                                mm = te.matmul(pso[:, :],
                                               atT[:, kc * 256 + qh * 128:
                                                   kc * 256 + (qh + 1) * 128],
                                               gk[:, kc * 128:(kc + 1) * 128],
                                               start=(kc == 0), stop=(kc == 1))
                            mm.then_inc(s_mo, 1)
                        te.wait_ge(s_oe, idx * 2 + 2)
                        for qh in range(2):
                            for hf in range(2):
                                pmi = idx * 4 + 2 * qh + hf
                                te.wait_ge(s_pc, pmi)
                                te.matmul(ppm[:, :],
                                          pm[:, (p_ * 2 + hf) * 128:(p_ * 2 + hf + 1) * 128],
                                          ob[:, qh * 128:(qh + 1) * 128],
                                          start=True, stop=True).then_inc(s_mp, 1)

            @block.scalar
            def _(sc):
                for s_ in range(NSEG):
                    for h in range(H):
                        idx = s_ * H + h
                        for qh in range(2):
                            k_ = idx * 2 + qh
                            sc.wait_ge(s_ms, k_ + 1)
                            sc.wait_ge(ch, idx * 10 + 5 * qh + 2)
                            sc.activation(pb[:, qh * 256:(qh + 1) * 256],
                                          (pss0 if qh == 0 else pss1)[:, :], AF.Exp,
                                          bias=sm[:, 4 + qh:5 + qh],
                                          scale=SC).then_inc(s_sx, 1)

            @block.vector
            def _(v):
                for s_ in range(NSEG):
                    for h in range(H):
                        idx = s_ * H + h
                        for qh in range(2):
                            k_ = idx * 2 + qh
                            pss = pss0 if qh == 0 else pss1
                            v.wait_ge(s_ms, k_ + 1)
                            v.tensor_reduce(sm[:, 0:1], pss[:, :], mybir.AxisListType.X,
                                            ALU.max).then_inc(ch, 1)
                            v.wait_ge(ch, idx * 10 + 5 * qh + 1)
                            if idx >= 1:
                                v.wait_ge(s_sx, 2 * (idx - 1) + qh + 1)
                            v.tensor_scalar(sm[:, 4 + qh:5 + qh], sm[:, 0:1], -SC, None,
                                            ALU.mult).then_inc(ch, 1)
                            v.wait_ge(s_sx, k_ + 1)
                            v.reduce_sum(sm[:, 1:2], pb[:, qh * 256:(qh + 1) * 256],
                                         mybir.AxisListType.X).then_inc(ch, 1)
                            v.wait_ge(ch, idx * 10 + 5 * qh + 3)
                            v.reciprocal(sm[:, 2:3], sm[:, 1:2]).then_inc(ch, 1)
                            v.wait_ge(ch, idx * 10 + 5 * qh + 4)
                            if idx >= 1 and qh == 0:
                                v.wait_ge(s_tr, idx * 6)
                            v.tensor_scalar(atn[:, qh * 256:(qh + 1) * 256],
                                            pb[:, qh * 256:(qh + 1) * 256],
                                            sm[:, 2:3], None, ALU.mult).then_inc(ch, 1)
                        for j in range(6):
                            tcv = idx * 6 + j + 1
                            v.wait_ge(s_tr, tcv)
                            if idx >= 1 and j == 0:
                                v.wait_ge(s_mo, idx * 2)
                            dst = (atT[:, (j % 2) * 256 + (j // 2) * 128:
                                       (j % 2) * 256 + (j // 2) * 128 + 128]
                                   if j < 4 else gk[:, (j - 4) * 128:(j - 3) * 128])
                            v.tensor_copy(dst, (ptr0 if tcv % 2 == 0 else ptr1)[:, :]
                                          ).then_inc(s_te, 1)
                        for qh in range(2):
                            v.wait_ge(s_mo, idx * 2 + qh + 1)
                            if idx >= 1 and qh == 0:
                                v.wait_ge(s_mp, idx * 4)
                            v.tensor_copy(ob[:, qh * 128:(qh + 1) * 128],
                                          pso[:, :]).then_inc(s_oe, 1)
                        for qh in range(2):
                            for hf in range(2):
                                pmi = idx * 4 + 2 * qh + hf
                                v.wait_ge(s_mp, pmi + 1)
                                tb = s_ * 4 + 2 * qh + hf
                                v.tensor_copy(attsb[:, tb * D + h * 128:
                                                    tb * D + (h + 1) * 128],
                                              ppm[:, :]).then_inc(s_pc, 1)

            @block.gpsimd
            def _(g):
                for tb in range(NTB):
                    s_seg = tb // 4
                    g.wait_ge(s_pc, (s_seg + 1) * H * 4)
                    g.tensor_tensor(attsb[:, tb * D:(tb + 1) * D],
                                    attsb[:, tb * D:(tb + 1) * D],
                                    x2f[:, tb * D:(tb + 1) * D], ALU.add).then_inc(s_ra, 1)
                    g.wait_ge(s_ra, tb + 1)
                    g.dma_start(out=out_dram[tb * 128:(tb + 1) * 128, :],
                                in_=attsb[:, tb * D:(tb + 1) * D]).then_inc(s_ro, 16)
                g.wait_ge(s_ro, NTB * 16)
        nc.all_engine_barrier()


def build_attn():
    nc = bass.Bass("TRN2", target_bir_lowering=False)
    ypre = nc.dram_tensor("ypre", [TC, D], F32, kind="ExternalInput").ap()
    outd = nc.dram_tensor("outd", [TC, D], F32, kind="ExternalOutput").ap()
    emit_attn(nc, ypre, outd)
    return nc



def _stage_reset(nc, snap=None, dma_reset=True):
    nc.all_engine_barrier()
    free = sorted(nc.free_semaphores)
    if free:
        with nc.Block() as b:
            @b.gpsimd
            def _(g):
                i = 0
                while i < len(free):
                    j = i
                    while j + 1 < len(free) and free[j + 1] == free[j] + 1:
                        j += 1
                    if dma_reset:
                        g.dma_reset(range(free[i], free[j] + 1))
                    g.sem_clear(range(free[i], free[j] + 1))
                    i = j + 1
    nc.all_engine_barrier()


def emit_t1(nc, y2, ypre1_dram):
    tag = "t1"
    NTB = TC // 128
    NK = D // 128
    identf = nc.inline_tensor(np.eye(128, dtype=np.float32).astype(ml_dtypes.bfloat16), "idt1")
    with (
        nc.sbuf_tensor("tldA" + tag, [128, NK * 128], BF16) as tld0,
        nc.sbuf_tensor("tldB" + tag, [128, NK * 128], BF16) as tld1,
        nc.sbuf_tensor("trow" + tag, [128, 2 * D], F32) as trow,
        nc.sbuf_tensor("tid" + tag, [128, 128], BF16) as tid,
        nc.psum_tensor("tp0" + tag, [128, 128], BF16) as tp0,
        nc.psum_tensor("tp1" + tag, [128, 128], BF16) as tp1,
        nc.semaphore("t1l0") as s_l0,
        nc.semaphore("t1l1") as s_l1,
        nc.semaphore("t1tp") as s_tp,
        nc.semaphore("t1ev") as s_ev,
        nc.semaphore("t1ro") as s_ro,
        nc.semaphore("t1id") as s_id,
        nc.Block() as block,
    ):
        s_l = [s_l0, s_l1]
        tlds = [tld0, tld1]
        agvs = [y2[b].ap().rearrange("(r p) t -> p r t", p=128) for b in range(2)]

        @block.sync
        def _(s):
            s.dma_start(out=tid[:, :], in_=identf.ap()).then_inc(s_id, 16)

        @block.gpsimd
        def _(g):
            for tb in range(NTB):
                if tb >= 2:
                    g.wait_ge(s_tp, (tb - 1) * NK)
                for b in range(2):
                    g.dma_start(
                        out=tlds[tb % 2][:, :].rearrange(
                            "p (r c t) -> p c r t", c=2, t=128)[:, b],
                        in_=agvs[b][:, :, tb * 128:(tb + 1) * 128],
                    ).then_inc(s_l[tb % 2], 16)

        @block.tensor
        def _(te):
            te.wait_ge(s_id, 16)
            cnt = 0
            for tb in range(NTB):
                te.wait_ge(s_l[tb % 2], (tb // 2 + 1) * 32)
                for k in range(NK):
                    cnt += 1
                    if cnt > 2:
                        te.wait_ge(s_ev, cnt - 2)
                    te.transpose((tp0 if cnt % 2 == 0 else tp1)[:, :],
                                 tlds[tb % 2][:, k * 128:(k + 1) * 128],
                                 tid[:, :]).then_inc(s_tp, 1)

        @block.vector
        def _(v):
            cnt = 0
            for tb in range(NTB):
                for k in range(NK):
                    cnt += 1
                    v.wait_ge(s_tp, cnt)
                    if k == 0 and tb >= 2:
                        v.wait_ge(s_ro, tb * 16)
                    v.tensor_copy(trow[:, (tb % 2) * D + k * 128:(tb % 2) * D + (k + 1) * 128],
                                  (tp0 if cnt % 2 == 0 else tp1)[:, :]).then_inc(s_ev, 1)

        @block.scalar
        def _(sc):
            for tb in range(NTB):
                sc.wait_ge(s_ev, (tb + 1) * NK)
                sc.dma_start(out=ypre1_dram[tb * 128:(tb + 1) * 128, :],
                             in_=trow[:, (tb % 2) * D:(tb % 2 + 1) * D]).then_inc(s_ro, 16)
            sc.wait_ge(s_ro, NTB * 16)


def build_fused():
    nc = bass.Bass("TRN2", target_bir_lowering=False)
    IC0 = 8192 // NC
    IC1 = 16384 // NC
    xT_tc = nc.dram_tensor("xT_tc", [D, TC], F32, kind="ExternalInput").ap()
    xT_dc = nc.dram_tensor("xT_dc", [DC, T], F32, kind="ExternalInput").ap()
    WxT = nc.dram_tensor("WxT", [D, 160], F32, kind="ExternalInput").ap()
    WdtT_c = nc.dram_tensor("WdtT_c", [R, DC], F32, kind="ExternalInput").ap()
    bdt_c = nc.dram_tensor("bdt_c", [DC], F32, kind="ExternalInput").ap()
    A_c = nc.dram_tensor("A_c", [DC, N], F32, kind="ExternalInput").ap()
    Dp_c = nc.dram_tensor("Dp_c", [DC], F32, kind="ExternalInput").ap()
    g0w = nc.dram_tensor("g0w", [IC0, D], I32, kind="ExternalInput").ap()
    g0s = nc.dram_tensor("g0s", [IC0], F32, kind="ExternalInput").ap()
    u0w = nc.dram_tensor("u0w", [IC0, D], I32, kind="ExternalInput").ap()
    u0s = nc.dram_tensor("u0s", [IC0], F32, kind="ExternalInput").ap()
    d0w = nc.dram_tensor("d0w", [D, IC0], I32, kind="ExternalInput").ap()
    d0s = nc.dram_tensor("d0s", [D], F32, kind="ExternalInput").ap()
    g1w = nc.dram_tensor("g1w", [IC1, D // 2], I32, kind="ExternalInput").ap()
    g1s = nc.dram_tensor("g1s", [IC1, D // 32], F32, kind="ExternalInput").ap()
    u1w = nc.dram_tensor("u1w", [IC1, D // 2], I32, kind="ExternalInput").ap()
    u1s = nc.dram_tensor("u1s", [IC1, D // 32], F32, kind="ExternalInput").ap()
    d1w = nc.dram_tensor("d1w", [D, IC1 // 2], I32, kind="ExternalInput").ap()
    d1s = nc.dram_tensor("d1s", [D, IC1 // 32], F32, kind="ExternalInput").ap()
    outd = nc.dram_tensor("outd", [TC, D], F32, kind="ExternalOutput").ap()
    ag_dt_in = nc.dram_tensor("ag_dt_in", [R, TC], BF16, kind="Internal")
    ag_dt_out = nc.dram_tensor("ag_dt_out", [NC * R, TC], BF16, kind="Internal",
                               addr_space="Shared")
    ag_bc_in = nc.dram_tensor("ag_bc_in", [2 * N, TC], BF16, kind="Internal")
    ag_bc_out = nc.dram_tensor("ag_bc_out", [NC * 2 * N, TC], BF16, kind="Internal",
                               addr_space="Shared")
    bc_packed = nc.dram_tensor("bc_packed", [2 * N, T], BF16, kind="Internal")
    ya2a_in = [nc.dram_tensor(f"ya2a_in{b}", [NC * 128, TC], BF16, kind="Internal")
               for b in range(2)]
    ya2a_out = [nc.dram_tensor(f"ya2a_out{b}", [NC * 128, TC], BF16, kind="Internal")
                for b in range(2)]
    cw = {k: nc.dram_tensor("c_" + k, shp, BF16, kind="Internal") for k, shp in
          [("w0g", [D, IC0]), ("w0u", [D, IC0]), ("w0d", [IC0, D]),
           ("w1g", [D, IC1]), ("w1u", [D, IC1]), ("w1d", [IC1, D])]}
    aw = {k: nc.dram_tensor("a_" + k, [NC * shp0, shp1], BF16, kind="Internal",
                            addr_space="Shared") for k, (shp0, shp1) in
          [("w0g", (D, IC0)), ("w0u", (D, IC0)), ("w0d", (IC0, D)),
           ("w1g", (D, IC1)), ("w1u", (D, IC1)), ("w1d", (IC1, D))]}
    ypre1_dram = nc.dram_tensor("ypre1_d", [TC, D], F32, kind="Internal")
    ypre2_dram = nc.dram_tensor("ypre2_d", [TC, D], F32, kind="Internal")
    ypre3_dram = nc.dram_tensor("ypre3_d", [TC, D], F32, kind="Internal")
    x1d_f0 = nc.dram_tensor("x1d_f0", [TC, D], F32, kind="Internal")
    x1d_f1 = nc.dram_tensor("x1d_f1", [TC, D], F32, kind="Internal")

    import contextlib
    octx = contextlib.ExitStack()
    with octx:
        s_agw = octx.enter_context(nc.semaphore("g_agw"))
        s_agy = octx.enter_context(nc.semaphore("g_agy"))

        def _mid_prep():
            psnap = nc.snapshot_sems()
            _prep_pair(nc, (g0w, g0s, cw["w0g"].ap(), IC0 // 128, D),
                       (g1w, g1s, cw["w1g"].ap(), IC1 // 128, D), tag="pg")
            _prep_pair(nc, (u0w, u0s, cw["w0u"].ap(), IC0 // 128, D),
                       (u1w, u1s, cw["w1u"].ap(), IC1 // 128, D), tag="pu")
            _prep_pair(nc, (d0w, d0s, cw["w0d"].ap(), D // 128, IC0),
                       (d1w, d1s, cw["w1d"].ap(), D // 128, IC1), tag="pd")
            with nc.Block() as b2:
                @b2.gpsimd
                def _(g):
                    for k in ["w0g", "w0u", "w0d", "w1g", "w1u", "w1d"]:
                        g.collective_compute("AllGather", ALU.bypass,
                                             replica_groups=[list(range(NC))],
                                             ins=[cw[k].ap().opt()],
                                             outs=[aw[k].ap().opt()]).then_inc(s_agw)
            _stage_reset(nc, psnap)

        _sc_s1 = nc.enter_named_scope("ssm", False)
        snap = nc.snapshot_sems()
        emit_stage1(nc, xT_tc, xT_dc, WxT, WdtT_c, bdt_c, A_c, Dp_c, None,
                    ag_dt_in, ag_dt_out, ag_bc_in, ag_bc_out, bc_packed,
                    a2a_y=(ya2a_in, ya2a_out, s_agy), mid_hook=_mid_prep)
        _stage_reset(nc, snap)
        nc.leave_named_scope("ssm", _sc_s1[0], False)

        _sc_t1 = nc.enter_named_scope("t1", False)
        snap = nc.snapshot_sems()
        with nc.Block() as b:
            @b.gpsimd
            def _(g):
                g.wait_ge(s_agy, 2)
        nc.all_engine_barrier()
        emit_t1(nc, ya2a_out, ypre1_dram.ap())
        _stage_reset(nc, snap)
        nc.leave_named_scope("t1", _sc_t1[0], False)

        def wv0(ic):
            r, lc = (ic * 512) // IC0, (ic * 512) % IC0
            gsrc = aw["w0g"].ap().rearrange("(r k p) i -> p r k i", p=128, k=D // 128)[
                :, r, :, lc:lc + 512]
            usrc = aw["w0u"].ap().rearrange("(r k p) i -> p r k i", p=128, k=D // 128)[
                :, r, :, lc:lc + 512]
            return gsrc, usrc
        _sc_f0 = nc.enter_named_scope("ffn0", False)
        snap = nc.snapshot_sems()
        emit_ffn(nc, 0, ypre1_dram.ap(), None, None, aw["w0d"].ap(), ypre2_dram.ap(),
                 8192, final_ln=False, x1_dram=x1d_f0.ap(), w_views=wv0,
                 w_ready=(s_agw, 6))
        _stage_reset(nc, snap)
        nc.leave_named_scope("ffn0", _sc_f0[0], False)

        _sc_at = nc.enter_named_scope("attn", False)
        snap = nc.snapshot_sems()
        emit_attn(nc, ypre2_dram.ap(), ypre3_dram.ap())
        _stage_reset(nc, snap)
        nc.leave_named_scope("attn", _sc_at[0], False)
        _sc_f1 = nc.enter_named_scope("ffn1", False)

        def wv1(ic):
            r, lc = (ic * 512) // IC1, (ic * 512) % IC1
            gsrc = aw["w1g"].ap().rearrange("(r k p) i -> p r k i", p=128, k=D // 128)[
                :, r, :, lc:lc + 512]
            usrc = aw["w1u"].ap().rearrange("(r k p) i -> p r k i", p=128, k=D // 128)[
                :, r, :, lc:lc + 512]
            return gsrc, usrc
        emit_ffn(nc, 1, ypre3_dram.ap(), None, None, aw["w1d"].ap(), outd,
                 16384, final_ln=True, x1_dram=x1d_f1.ap(), w_views=wv1)
        nc.leave_named_scope("ffn1", _sc_f1[0], False)
    return nc


def fused_inputs(full):
    x = np.ascontiguousarray(np.asarray(full["L0_attn_x"], dtype=np.float32
                                        ).reshape(T, D))
    m1 = stage1_inputs(x, np.asarray(full["L0_attn_W_xproj"]),
                       np.asarray(full["L0_attn_W_dt"]),
                       np.asarray(full["L0_attn_b_dt"]),
                       np.asarray(full["L0_attn_A"]),
                       np.asarray(full["L0_attn_D"]))
    mp = prep_inputs(full)
    return [{**m1[c], **mp[c]} for c in range(NC)]



from concourse.bass_utils import run_bass_kernel_spmd


def kernel(L0_attn_x, L0_attn_W_xproj, L0_attn_W_dt, L0_attn_b_dt, L0_attn_A,
           L0_attn_D, L0_ffn_gate_w, L0_ffn_gate_scale, L0_ffn_up_w,
           L0_ffn_up_scale, L0_ffn_down_w, L0_ffn_down_scale,
           L1_ffn_gate_W_packed, L1_ffn_gate_scales, L1_ffn_up_W_packed,
           L1_ffn_up_scales, L1_ffn_down_W_packed, L1_ffn_down_scales):
    full = {
        "L0_attn_x": np.asarray(L0_attn_x),
        "L0_attn_W_xproj": np.asarray(L0_attn_W_xproj),
        "L0_attn_W_dt": np.asarray(L0_attn_W_dt),
        "L0_attn_b_dt": np.asarray(L0_attn_b_dt),
        "L0_attn_A": np.asarray(L0_attn_A),
        "L0_attn_D": np.asarray(L0_attn_D),
        "L0_ffn_gate_w": np.asarray(L0_ffn_gate_w),
        "L0_ffn_gate_scale": np.asarray(L0_ffn_gate_scale),
        "L0_ffn_up_w": np.asarray(L0_ffn_up_w),
        "L0_ffn_up_scale": np.asarray(L0_ffn_up_scale),
        "L0_ffn_down_w": np.asarray(L0_ffn_down_w),
        "L0_ffn_down_scale": np.asarray(L0_ffn_down_scale),
        "L1_ffn_gate_W_packed": np.asarray(L1_ffn_gate_W_packed),
        "L1_ffn_gate_scales": np.asarray(L1_ffn_gate_scales),
        "L1_ffn_up_W_packed": np.asarray(L1_ffn_up_W_packed),
        "L1_ffn_up_scales": np.asarray(L1_ffn_up_scales),
        "L1_ffn_down_W_packed": np.asarray(L1_ffn_down_W_packed),
        "L1_ffn_down_scales": np.asarray(L1_ffn_down_scales),
    }
    nc = build_fused()
    maps = fused_inputs(full)
    res = run_bass_kernel_spmd(nc, maps, core_ids=list(range(NC))).results
    x4 = np.concatenate([res[c]["outd"] for c in range(NC)], axis=0)
    return np.ascontiguousarray(x4.reshape(1, T, D)).astype(np.float32)


# version 13 (final)
# speedup vs baseline: 1.0862x; 1.0086x over previous
import sys
if "/opt/trn_rl_repo" not in sys.path:
    sys.path.insert(0, "/opt/trn_rl_repo")
"""Builders for the mk_seq_postln_d2 megakernel stages (dev version).

T = 8192 tokens, D = 2048 dims, N = 16 ssm state, R = 128 dt_rank
NC = 8 cores; DC = 256 dims/core (stage 1); TC = 1024 tokens/core
"xT" means feature-major [d on partitions, t on free].
"""
import os
import numpy as np
import ml_dtypes
import concourse.bass as bass
import concourse.mybir as mybir

F32 = mybir.dt.float32
BF16 = mybir.dt.bfloat16
I32 = mybir.dt.int32
AF = mybir.ActivationFunctionType
ALU = mybir.AluOpType

T, D, N, R = 8192, 2048, 16, 128
NC = 8
DC = D // NC
TC = T // NC
HALF = T // 2


def emit_stage1(nc, xT_tc, xT_dc, WxT, WdtT_c, bdt_c, A_c, Dp_c, ypreT,
                ag_dt_in, ag_dt_out, ag_bc_in, ag_bc_out, bc_packed,
                sems=None, ag_y=None, a2a_y=None, post_ag_hook=None,
                mid_hook=None):
    TT = 512
    NB = DC // 128
    NT = TC // TT
    NTILE = T // TT

    import contextlib
    ctx = contextlib.ExitStack()
    with ctx:
        wdt = ctx.enter_context(nc.sbuf_tensor("s1_wdt", [128, DC], BF16))
        bdt = ctx.enter_context(nc.sbuf_tensor("s1_bdt", [128, NB], F32))
        acol = ctx.enter_context(nc.sbuf_tensor("s1_acol", [128, NB * N], F32))
        dcol = ctx.enter_context(nc.sbuf_tensor("s1_dcol", [128, NB], F32))
        dp1 = ctx.enter_context(nc.sbuf_tensor("s1_dp1", [128, NB], F32))

        s_ld = ctx.enter_context(nc.semaphore("s1_ld"))
        s_mm = ctx.enter_context(nc.semaphore("s1_mm"))
        s_ev = ctx.enter_context(nc.semaphore("s1_ev"))
        s_ag = ctx.enter_context(nc.semaphore("s1_ag"))
        s_dma = ctx.enter_context(nc.semaphore("s1_dma"))
        s_dts = [ctx.enter_context(nc.semaphore("s1_dts0")),
                 ctx.enter_context(nc.semaphore("s1_dts1"))]
        s_sp = ctx.enter_context(nc.semaphore("s1_sp"))
        s_xb = ctx.enter_context(nc.semaphore("s1_xb"))
        s_dxr = ctx.enter_context(nc.semaphore("s1_dxr"))
        s_bc = [ctx.enter_context(nc.semaphore("s1_bc0")),
                ctx.enter_context(nc.semaphore("s1_bc1"))]
        s_ad = ctx.enter_context(nc.semaphore("s1_ad"))
        s_adc = ctx.enter_context(nc.semaphore("s1_adc"))
        s_tm = ctx.enter_context(nc.semaphore("s1_tm"))
        s_yo = ctx.enter_context(nc.semaphore("s1_yo"))
        ch_v = ctx.enter_context(nc.semaphore("s1_chv"))
        ch_sc = ctx.enter_context(nc.semaphore("s1_chsc"))
        ch_g = ctx.enter_context(nc.semaphore("s1_chg"))
        s_rp = ctx.enter_context(nc.semaphore("s1_rp"))

        with (
            nc.sbuf_tensor("s1_wx", [128, 16 * 160], BF16) as wx,
            nc.sbuf_tensor("s1_xttc", [128, 16 * TC], BF16) as xttc,
            nc.sbuf_tensor("s1_dtT", [128, TC], BF16) as dtT,
            nc.sbuf_tensor("s1_bcs", [32, TC], BF16) as bcs,
            nc.psum_tensor("s1_psdt", [128, 2 * TT], F32) as ps_dt,
            nc.psum_tensor("s1_psbc", [32, 2 * TT], F32) as ps_bc,
            nc.Block() as block,
        ):
            @block.gpsimd
            def _(g):
                g.dma_start(out=wx[:, :].rearrange("p (k e) -> p k e", k=16),
                            in_=WxT.rearrange("(k p) e -> p k e", k=16)).then_inc(s_ld, 16)
                g.dma_start(out=xttc[:, :].rearrange("p (k t) -> p k t", k=16),
                            in_=xT_tc.rearrange("(k p) t -> p k t", k=16)).then_inc(s_ld, 16)
                g.dma_start(out=wdt[:, :], in_=WdtT_c).then_inc(s_ld, 16)

            @block.sync
            def _(s):
                s.dma_start(out=bdt[:, :], in_=bdt_c.rearrange("(b p) -> p b", p=128), allow_slow_non_contiguous=True).then_inc(s_dma, 16)
                s.dma_start(out=acol[:, :].rearrange("p (b n) -> p b n", n=16), in_=A_c.rearrange("(b p) n -> p b n", p=128)).then_inc(s_dma, 16)
                s.dma_start(out=dcol[:, :], in_=Dp_c.rearrange("(b p) -> p b", p=128), allow_slow_non_contiguous=True).then_inc(s_dma, 16)

            @block.tensor
            def _(te):
                te.wait_ge(s_ld, 48)
                for tt in range(NT):
                    for k in range(16):
                        te.matmul(ps_dt[:, tt * TT:(tt + 1) * TT],
                                  wx[:, k * 160:k * 160 + 128],
                                  xttc[:, k * TC + tt * TT:k * TC + (tt + 1) * TT],
                                  start=(k == 0), stop=(k == 15))
                    for k in range(16):
                        mm = te.matmul(ps_bc[:, tt * TT:(tt + 1) * TT],
                                  wx[:, k * 160 + 128:k * 160 + 160],
                                  xttc[:, k * TC + tt * TT:k * TC + (tt + 1) * TT],
                                  start=(k == 0), stop=(k == 15))
                    mm.then_inc(s_mm, 1)

            @block.scalar
            def _(sc):
                for tt in range(NT):
                    sc.wait_ge(s_mm, tt + 1)
                    sc.copy(dtT[:, tt * TT:(tt + 1) * TT],
                            ps_dt[:, tt * TT:(tt + 1) * TT]).then_inc(s_ev, 1)

            @block.vector
            def _(v):
                v.wait_ge(s_dma, 48)
                v.tensor_scalar(dp1[:, :], dcol[:, :], 1.0, None, ALU.add).then_inc(s_ev, 1)
                for tt in range(NT):
                    v.wait_ge(s_mm, tt + 1)
                    v.tensor_copy(bcs[:, tt * TT:(tt + 1) * TT],
                                  ps_bc[:, tt * TT:(tt + 1) * TT]).then_inc(s_ev, 1)

            @block.sync
            def _(s):
                s.wait_ge(s_ev, 1 + 2 * NT)
                s.dma_start(out=ag_dt_in[:, :], in_=dtT[:, :]).then_inc(s_dma, 16)
                s.dma_start(out=ag_bc_in[:, :], in_=bcs[:, :]).then_inc(s_dma, 16)

            @block.gpsimd
            def _(g):
                g.wait_ge(s_dma, 80)
                g.collective_compute("AllGather", ALU.bypass,
                                     replica_groups=[list(range(NC))],
                                     ins=[ag_dt_in.ap().opt()],
                                     outs=[ag_dt_out.ap().opt()]).then_inc(s_ag, 1)
                g.collective_compute("AllGather", ALU.bypass,
                                     replica_groups=[list(range(NC))],
                                     ins=[ag_bc_in.ap().opt()],
                                     outs=[ag_bc_out.ap().opt()]).then_inc(s_ag, 1)
                g.wait_ge(s_ag, 2)
                g.dma_start(
                    out=bc_packed.ap().rearrange("n (r t) -> n r t", r=NC),
                    in_=ag_bc_out.ap().rearrange("(r n) t -> n r t", n=2 * N),
                ).then_inc(s_rp, 16)
                if post_ag_hook is not None:
                    post_ag_hook(g)

        nc.all_engine_barrier()

        if mid_hook is not None:
            mid_hook()

        STOP = os.environ.get("S1_STOP", "full")
        if STOP == "A":
            return
        with (
            nc.sbuf_tensor("s1_deltaT", [128, NB * T], BF16) as deltaT,
            nc.sbuf_tensor("s1_dx", [128, T], BF16) as dx,
            nc.sbuf_tensor("s1_yb", [128, T], F32) as yb,
            nc.sbuf_tensor("s1_xbf", [128, T], F32) as xbf,
            nc.sbuf_tensor("s1_dts", [128, 2 * TT], BF16) as dts,
            nc.sbuf_tensor("s1_bmb", [128, 2 * HALF], BF16) as bmb,
            nc.sbuf_tensor("s1_cmb", [128, 2 * HALF], BF16) as cmb,
            nc.sbuf_tensor("s1_adt", [128, 2 * HALF], BF16) as adt,
            nc.sbuf_tensor("s1_bxt", [128, HALF], BF16) as bxt,
            nc.sbuf_tensor("s1_ht", [128, HALF], BF16) as ht,
            nc.sbuf_tensor("s1_tmp", [128, 2 * HALF], BF16) as tmp,
            nc.sbuf_tensor("s1_carry", [128, 1], F32) as carry,
            nc.psum_tensor("s1_psde", [128, 2 * TT], F32) as ps_de,
            nc.Block() as block,
        ):
            STREAMONLY = STOP == "stream"

            def dts_load(eng, j):
                tt_ = j % NTILE
                eng.dma_start(
                    out=dts[:, (j % 2) * TT:(j % 2 + 1) * TT],
                    in_=ag_dt_out.ap().rearrange("(c p) t -> p c t", p=R)[
                        :, (tt_ * TT) // TC,
                        (tt_ * TT) % TC:(tt_ * TT) % TC + TT],
                ).then_inc(s_dts[j % 2], 16)

            def cmb_load(eng, idx2):
                buf2 = idx2 % 2
                hf2 = idx2 % 2
                n2 = (idx2 // 2) % N
                src_c = bc_packed.ap()[N + n2:N + n2 + 1,
                                       hf2 * HALF:(hf2 + 1) * HALF
                                       ].broadcast_to([128, HALF])
                eng.dma_start(out=cmb[:, buf2 * HALF:(buf2 + 1) * HALF],
                              in_=src_c).then_inc(s_bc[buf2], 16)

            @block.sync
            def _(s):
                dts_load(s, 0)
                dts_load(s, 1)
                for b in range(NB):
                    if STOP in ("delta", "stream", "noscalar"):
                        continue
                    if b > 0:
                        s.wait_ge(s_dxr, b)
                    s.dma_start(out=xbf[:, :],
                                in_=xT_dc[b * 128:(b + 1) * 128, :]).then_inc(s_xb, 16)
                    if b == 0:
                        s.wait_ge(s_rp, 16)
                    for n in range(N):
                        for hf in range(2):
                            idx = (b * N + n) * 2 + hf
                            buf = idx % 2
                            if idx >= 2:
                                s.wait_ge(s_tm, idx - 1)
                            src_b = bc_packed.ap()[n:n + 1, hf * HALF:(hf + 1) * HALF
                                                   ].broadcast_to([128, HALF])
                            s.dma_start(out=bmb[:, buf * HALF:(buf + 1) * HALF],
                                        in_=src_b).then_inc(s_bc[buf], 16)

            @block.tensor
            def _(te):
                if STOP == "stream":
                    return
                for b in range(NB):
                    for tt in range(NTILE):
                        j = b * NTILE + tt
                        te.wait_ge(s_dts[j % 2], (j // 2 + 1) * 16)
                        if j >= 2 and STOP != "noscalar":
                            te.wait_ge(s_sp, j - 1)
                        te.matmul(ps_de[:, (j % 2) * TT:(j % 2 + 1) * TT],
                                  wdt[:, b * 128:(b + 1) * 128],
                                  dts[:, (j % 2) * TT:(j % 2 + 1) * TT],
                                  start=True, stop=True).then_inc(s_mm, 1)

            @block.scalar
            def _(sc):
                if STOP in ("stream", "noscalar"):
                    return
                CHSC = [0]
                sc.wait_ge(s_rp, 16)
                cmb_load(sc, 0)
                cmb_load(sc, 1)

                def sp_tile(b, tt):
                    j = b * NTILE + tt
                    sc.wait_ge(s_mm, NT + j + 1)
                    if j + 2 < NB * NTILE:
                        dts_load(sc, j + 2)
                    sl = slice(b * T + tt * TT, b * T + (tt + 1) * TT)
                    psl = slice((j % 2) * TT, (j % 2 + 1) * TT)
                    sc.activation(deltaT[:, sl], ps_de[:, psl], AF.Exp,
                                  bias=bdt[:, b:b + 1], scale=1.0).then_inc(ch_sc, 1)
                    CHSC[0] += 1
                    sc.wait_ge(ch_sc, CHSC[0])
                    sc.activation(deltaT[:, sl], deltaT[:, sl], AF.Ln,
                                  bias=1.0, scale=1.0).then_inc(s_sp, 1)
                for tt in range(NTILE):
                    sp_tile(0, tt)
                if STOP != "delta":
                    for b in range(NB):
                        sc.wait_ge(s_sp, (b + 1) * NTILE)
                        for n in range(N):
                            for hf in range(2):
                                idx = (b * N + n) * 2 + hf
                                loc = idx - b * N * 2
                                if b + 1 < NB and loc < NTILE:
                                    sp_tile(b + 1, loc)
                                if idx >= 2:
                                    sc.wait_ge(s_adc, idx - 1)
                                sc.activation(adt[:, (idx % 2) * HALF:(idx % 2 + 1) * HALF],
                                              deltaT[:, b * T + hf * HALF:
                                                     b * T + (hf + 1) * HALF], AF.Exp,
                                              scale=acol[:, b * N + n:b * N + n + 1]
                                              ).then_inc(s_ad, 1)
                                if idx >= 1 and idx + 1 < NB * N * 2:
                                    sc.wait_ge(s_tm, idx)
                                    cmb_load(sc, idx + 1)

            CHV = [0]
            CHG = [0]

            @block.vector
            def _(v):
                if STOP in ("delta", "stream", "noscalar"):
                    return
                for b in range(NB):
                    v.wait_ge(s_xb, (b + 1) * 16)
                    v.wait_ge(s_sp, (b + 1) * NTILE)
                    if b > 0:
                        v.wait_ge(s_yo, b * 16)
                    v.tensor_scalar(yb[:, :], xbf[:, :], dp1[:, b:b + 1], None, ALU.mult)
                    v.tensor_tensor(dx[:, :], deltaT[:, b * T:(b + 1) * T],
                                    xbf[:, :], ALU.mult).then_inc(s_dxr, 1)
                    v.wait_ge(s_dxr, b + 1)
                    for n in range(N):
                        for hf in range(2):
                            idx = (b * N + n) * 2 + hf
                            hsl = slice(hf * HALF, (hf + 1) * HALF)
                            bsl = slice((idx % 2) * HALF, (idx % 2 + 1) * HALF)
                            v.wait_ge(s_bc[idx % 2], (idx // 2 + 1) * 32)
                            v.wait_ge(s_ad, idx + 1)
                            v.wait_ge(s_adc, idx)
                            v.tensor_tensor(bxt[:, :], dx[:, hsl], bmb[:, bsl],
                                            ALU.mult).then_inc(ch_v, 1)
                            CHV[0] += 1
                            if hf == 1:
                                v.tensor_copy(carry[:, 0:1], ht[:, HALF - 1:HALF]).then_inc(ch_v, 1)
                                CHV[0] += 1
                            v.wait_ge(ch_v, CHV[0])
                            v.wait_ge(s_tm, idx)
                            init = 0.0 if hf == 0 else carry[:, 0:1]
                            v.tensor_tensor_scan(ht[:, :], adt[:, bsl], bxt[:, :], init,
                                                 ALU.mult, ALU.add).then_inc(s_adc, 1)
                            if idx >= 2:
                                v.wait_ge(ch_g, idx - 1)
                            v.wait_ge(s_adc, idx + 1)
                            v.tensor_tensor(tmp[:, bsl], ht[:, :], cmb[:, bsl],
                                            ALU.mult).then_inc(s_tm, 1)

            @block.gpsimd
            def _(g):
                if STOP in ("delta", "stream", "noscalar"):
                    return

                for b in range(NB):
                    for n in range(N):
                        for hf in range(2):
                            idx = (b * N + n) * 2 + hf
                            hsl = slice(hf * HALF, (hf + 1) * HALF)
                            bsl = slice((idx % 2) * HALF, (idx % 2 + 1) * HALF)
                            g.wait_ge(s_tm, idx + 1)
                            g.wait_ge(ch_g, CHG[0])
                            g.tensor_tensor(yb[:, hsl], yb[:, hsl], tmp[:, bsl],
                                            ALU.add).then_inc(ch_g, 1)
                            CHG[0] += 1
                    g.wait_ge(ch_g, CHG[0])
                    if a2a_y is not None:
                        a2a_ins, a2a_outs, s_agy = a2a_y
                        g.dma_start(
                            out=a2a_ins[b].ap().rearrange("(j p) t -> p j t", p=128),
                            in_=yb[:, :].rearrange("p (j t) -> p j t", j=NC),
                        ).then_inc(s_yo, 16)
                        g.wait_ge(s_yo, (b + 1) * 16)
                        from concourse.aot_env import allow_ici_all_to_all
                        with allow_ici_all_to_all():
                            g.collective_compute(
                                "AllToAll", ALU.bypass,
                                replica_groups=[list(range(NC))],
                                ins=[a2a_ins[b].ap().opt()],
                                outs=[a2a_outs[b].ap().opt()]).then_inc(s_agy, 1)
                    else:
                        g.dma_start(out=ypreT[b * 128:(b + 1) * 128, :],
                                    in_=yb[:, :]).then_inc(s_yo, 16)
                g.wait_ge(s_yo, NB * 16)
                if ag_y is not None:
                    ag_out_t, s_agy = ag_y
                    g.collective_compute("AllGather", ALU.bypass,
                                         replica_groups=[list(range(NC))],
                                         ins=[ypreT.tensor.ap().opt()],
                                         outs=[ag_out_t.ap().opt()]).then_inc(s_agy)


def build_stage1():
    nc = bass.Bass("TRN2", target_bir_lowering=False)
    xT_tc = nc.dram_tensor("xT_tc", [D, TC], F32, kind="ExternalInput").ap()
    xT_dc = nc.dram_tensor("xT_dc", [DC, T], F32, kind="ExternalInput").ap()
    WxT = nc.dram_tensor("WxT", [D, 160], F32, kind="ExternalInput").ap()
    WdtT_c = nc.dram_tensor("WdtT_c", [R, DC], F32, kind="ExternalInput").ap()
    bdt_c = nc.dram_tensor("bdt_c", [DC], F32, kind="ExternalInput").ap()
    A_c = nc.dram_tensor("A_c", [DC, N], F32, kind="ExternalInput").ap()
    Dp_c = nc.dram_tensor("Dp_c", [DC], F32, kind="ExternalInput").ap()
    ypreT = nc.dram_tensor("ypreT", [DC, T], F32, kind="ExternalOutput").ap()

    ag_dt_in = nc.dram_tensor("ag_dt_in", [R, TC], BF16, kind="Internal")
    ag_dt_out = nc.dram_tensor("ag_dt_out", [NC * R, TC], BF16, kind="Internal",
                               addr_space="Shared")
    ag_bc_in = nc.dram_tensor("ag_bc_in", [2 * N, TC], BF16, kind="Internal")
    ag_bc_out = nc.dram_tensor("ag_bc_out", [NC * 2 * N, TC], BF16, kind="Internal",
                               addr_space="Shared")
    bc_packed = nc.dram_tensor("bc_packed", [2 * N, T], BF16, kind="Internal")

    emit_stage1(nc, xT_tc, xT_dc, WxT, WdtT_c, bdt_c, A_c, Dp_c, ypreT,
                ag_dt_in, ag_dt_out, ag_bc_in, ag_bc_out, bc_packed)
    return nc


def stage1_inputs(x, W_xproj, W_dt, b_dt, A, Dp):
    x = np.ascontiguousarray(x.reshape(T, D))
    xT = np.ascontiguousarray(x.T)
    maps = []
    for c in range(NC):
        dc = slice(c * DC, (c + 1) * DC)
        tc = slice(c * TC, (c + 1) * TC)
        maps.append({
            "xT_tc": np.ascontiguousarray(xT[:, tc]),
            "xT_dc": np.ascontiguousarray(xT[dc, :]),
            "WxT": np.ascontiguousarray(W_xproj.T),
            "WdtT_c": np.ascontiguousarray(W_dt[dc, :].T),
            "bdt_c": np.ascontiguousarray(b_dt[dc]),
            "A_c": np.ascontiguousarray(A[dc, :]),
            "Dp_c": np.ascontiguousarray(Dp[dc]),
        })
    return maps



def _prep_pair(nc, deq, dec, tag=""):
    qsrc, qscol_src, qoutT, QNI, QW = deq
    dsrc, dsc2_src, doutT, DNI, DW = dec
    QNK = QW // 128
    DNK = DW // 128
    DWP = DW // 2
    DNSC = DW // 32
    import contextlib
    ctx = contextlib.ExitStack()
    with ctx:
        _e = ctx.enter_context
        qlw = _e(nc.sbuf_tensor("qlw" + tag, [128, 2 * QW], I32))
        qwq = _e(nc.sbuf_tensor("qwq" + tag, [128, QNI * QW], BF16))
        qscol = _e(nc.sbuf_tensor("qsc" + tag, [128, QNI], F32))
        qbcol = _e(nc.sbuf_tensor("qbc" + tag, [128, QNI], F32))
        qarow = _e(nc.sbuf_tensor("qar" + tag, [128, 2 * QNI * 128], BF16))
        ident = _e(nc.sbuf_tensor("pid" + tag, [128, 128], BF16))
        qp0 = _e(nc.psum_tensor("qp0" + tag, [128, 512], BF16))
        qp1 = _e(nc.psum_tensor("qp1" + tag, [128, 512], BF16))
        dlw = _e(nc.sbuf_tensor("dlw" + tag, [128, 2 * DWP], I32))
        dwq = _e(nc.sbuf_tensor("dwq" + tag, [128, DNI * DW], BF16))
        s1t = _e(nc.sbuf_tensor("ds1" + tag, [128, DNI * DNSC], F32))
        scr = {}
        for en in ("v", "g"):
            scr[en] = {nm: _e(nc.sbuf_tensor(f"d{nm}{en}" + tag, [128, DWP],
                                             I32 if nm in ("lo", "hi", "t0", "ub", "sb")
                                             else F32))
                       for nm in ("lo", "hi", "t0", "ub", "sb", "err", "vm")}
        darow = _e(nc.sbuf_tensor("dar" + tag, [128, 2 * DNI * 128], BF16))
        dp0 = _e(nc.psum_tensor("dp0" + tag, [128, 512], BF16))
        dp1 = _e(nc.psum_tensor("dp1" + tag, [128, 512], BF16))
        sm = {nm: _e(nc.semaphore(nm + tag)) for nm in
              ("ql0", "ql1", "qdq", "qtp", "qev", "qa0", "qa1", "qbi",
               "dl0", "dl1", "chv", "chg", "dtp", "dev", "da0", "da1", "psm")}
        q_l = [sm["ql0"], sm["ql1"]]
        q_ar = [sm["qa0"], sm["qa1"]]
        d_l = [sm["dl0"], sm["dl1"]]
        d_ar = [sm["da0"], sm["da1"]]
        chains = {"v": sm["chv"], "g": sm["chg"]}
        identT = nc.inline_tensor(np.eye(128, dtype=np.float32).astype(
            ml_dtypes.bfloat16), "pidc" + tag)
        eng_of = lambda ib: "v"
        n_v = sum(1 for ib in range(DNI) if eng_of(ib) == "v")
        n_g = DNI - n_v
        done_at = {}
        cnt_e = {"v": 0, "g": 0}
        for ib in range(DNI):
            cnt_e[eng_of(ib)] += 18
            done_at[ib] = cnt_e[eng_of(ib)]

        with nc.Block() as block:
            @block.sync
            def _(s):
                s.dma_start(out=ident[:, :], in_=identT.ap()).then_inc(sm["psm"], 16)
                s.dma_start(out=qscol[:, :],
                            in_=qscol_src.rearrange("(b p) -> p b", p=128),
                            allow_slow_non_contiguous=True).then_inc(sm["psm"], 16)
                s.dma_start(out=s1t[:, :].rearrange("p (b n) -> p b n", n=DNSC),
                            in_=dsc2_src.rearrange("(b p) n -> p b n", p=128)
                            ).then_inc(sm["psm"], 16)
                for ib in range(max(QNI, DNI)):
                    if ib < DNI:
                        if ib >= 2:
                            s.wait_ge(chains[eng_of(ib - 2)], done_at[ib - 2])
                        s.dma_start(out=dlw[:, (ib % 2) * DWP:(ib % 2 + 1) * DWP],
                                    in_=dsrc[ib * 128:(ib + 1) * 128, :]
                                    ).then_inc(d_l[ib % 2], 16)
                    if ib < QNI:
                        if ib >= 2:
                            s.wait_ge(sm["qdq"], ib - 1)
                        s.dma_start(out=qlw[:, (ib % 2) * QW:(ib % 2 + 1) * QW],
                                    in_=qsrc[ib * 128:(ib + 1) * 128, :]
                                    ).then_inc(q_l[ib % 2], 16)
                for k in range(QNK):
                    s.wait_ge(sm["qev"], (k + 1) * QNI)
                    s.dma_start(out=qoutT[k * 128:(k + 1) * 128, :],
                                in_=qarow[:, (k % 2) * QNI * 128:(k % 2 + 1) * QNI * 128]
                                ).then_inc(q_ar[k % 2], 16)
                s.wait_ge(q_ar[0], ((QNK + 1) // 2) * 16)
                s.wait_ge(q_ar[1], (QNK // 2) * 16)

            def decode_tile(en, ib, CH):
                key = eng_of(ib)
                ch = chains[key]
                sc_ = scr[key]
                lo, hi, t0 = sc_["lo"], sc_["hi"], sc_["t0"]
                ub, sb, errt, vm = sc_["ub"], sc_["sb"], sc_["err"], sc_["vm"]
                en.wait_ge(d_l[ib % 2], (ib // 2 + 1) * 16)
                lws = dlw[:, (ib % 2) * DWP:(ib % 2 + 1) * DWP]
                en.tensor_scalar(lo[:, :], lws, 15, None, ALU.bitwise_and).then_inc(ch, 1)
                en.tensor_scalar(hi[:, :], lws, 4, 15, ALU.logical_shift_right,
                                 ALU.bitwise_and).then_inc(ch, 1)
                CH[0] += 2
                for pl, csrc in ((0, lo), (1, hi)):
                    en.wait_ge(ch, CH[0])
                    en.tensor_scalar(t0[:, :], csrc[:, :], 7, None,
                                     ALU.bitwise_and).then_inc(ch, 1)
                    CH[0] += 1
                    en.wait_ge(ch, CH[0])
                    en.tensor_scalar(ub[:, :], t0[:, :], 252, 4194304, ALU.add,
                                     ALU.mult).then_inc(ch, 1)
                    en.tensor_scalar(sb[:, :], csrc[:, :], 8, 28, ALU.bitwise_and,
                                     ALU.logical_shift_left).then_inc(ch, 1)
                    CH[0] += 2
                    en.wait_ge(ch, CH[0])
                    en.tensor_scalar(errt[:, :], t0[:, :], -0.25, 0.5, ALU.mult,
                                     ALU.add).then_inc(ch, 1)
                    CH[0] += 1
                    en.wait_ge(ch, CH[0])
                    en.tensor_scalar(errt[:, :], errt[:, :], 0.0, None,
                                     ALU.max).then_inc(ch, 1)
                    CH[0] += 1
                    en.wait_ge(ch, CH[0])
                    en.tensor_tensor(vm[:, :], ub[:, :].bitcast(F32), errt[:, :],
                                     ALU.subtract).then_inc(ch, 1)
                    CH[0] += 1
                    en.wait_ge(ch, CH[0])
                    en.tensor_tensor(vm[:, :].bitcast(I32), vm[:, :].bitcast(I32),
                                     sb[:, :], ALU.bitwise_or).then_inc(ch, 1)
                    CH[0] += 1
                    en.wait_ge(ch, CH[0])
                    sbc = s1t[:, ib * DNSC:(ib + 1) * DNSC].rearrange(
                        "p (n o) -> p n o", o=1).broadcast_to([128, DNSC, 16])
                    ow = dwq[:, ib * DW + pl:(ib + 1) * DW:2].rearrange(
                        "p (n o) -> p n o", o=16)
                    en.tensor_tensor(ow, vm[:, :].rearrange("p (n o) -> p n o", o=16),
                                     sbc, ALU.mult).then_inc(ch, 1)
                    CH[0] += 1

            @block.vector
            def _(v):
                v.wait_ge(sm["psm"], 48)
                v.tensor_scalar(qbcol[:, :], qscol[:, :], -128.0, None,
                                ALU.mult).then_inc(sm["qbi"], 1)
                CH = [0]
                for ib in range(DNI):
                    if eng_of(ib) == "v":
                        decode_tile(v, ib, CH)

            @block.gpsimd
            def _(g):
                CH = [0]
                if n_g:
                    g.wait_ge(sm["psm"], 48)
                for ib in range(DNI):
                    if eng_of(ib) == "g":
                        decode_tile(g, ib, CH)
                for ib in range(DNI):
                    g.wait_ge(sm["dev"], (ib + 1) * DNK)
                    g.dma_start(
                        out=doutT.rearrange("(k p) i -> p k i", p=128)[
                            :, :, ib * 128:(ib + 1) * 128],
                        in_=darow[:, (ib % 2) * DNK * 128:(ib % 2 + 1) * DNK * 128
                                  ].rearrange("p (k i) -> p k i", k=DNK)
                    ).then_inc(d_ar[ib % 2], 16)
                g.wait_ge(d_ar[0], ((DNI + 1) // 2) * 16)
                g.wait_ge(d_ar[1], (DNI // 2) * 16)

            @block.scalar
            def _(sc):
                sc.wait_ge(sm["qbi"], 1)
                for ib in range(QNI):
                    sc.wait_ge(q_l[ib % 2], (ib // 2 + 1) * 16)
                    sc.activation(qwq[:, ib * QW:(ib + 1) * QW],
                                  qlw[:, (ib % 2) * QW:(ib % 2 + 1) * QW], AF.Identity,
                                  bias=qbcol[:, ib:ib + 1],
                                  scale=qscol[:, ib:ib + 1]).then_inc(sm["qdq"], 1)
                cnt = 0
                for k in range(QNK):
                    for ib in range(QNI):
                        cnt += 1
                        if ib == 0 and k >= 2:
                            sc.wait_ge(q_ar[k % 2], (k // 2) * 16)
                        sc.wait_ge(sm["qtp"], cnt)
                        sc.copy(qarow[:, (k % 2) * QNI * 128 + ib * 128:
                                      (k % 2) * QNI * 128 + (ib + 1) * 128],
                                (qp0 if cnt % 2 == 0 else qp1)[:, 0:128]
                                ).then_inc(sm["qev"], 1)
                cnt = 0
                for ib in range(DNI):
                    for k in range(DNK):
                        cnt += 1
                        if k == 0 and ib >= 2:
                            sc.wait_ge(d_ar[ib % 2], (ib // 2) * 16)
                        sc.wait_ge(sm["dtp"], cnt)
                        sc.copy(darow[:, (ib % 2) * DNK * 128 + k * 128:
                                      (ib % 2) * DNK * 128 + (k + 1) * 128],
                                (dp0 if cnt % 2 == 0 else dp1)[:, 0:128]
                                ).then_inc(sm["dev"], 1)

            @block.tensor
            def _(te):
                te.wait_ge(sm["qdq"], QNI)
                cnt = 0
                for k in range(QNK):
                    for ib in range(QNI):
                        cnt += 1
                        if cnt > 2:
                            te.wait_ge(sm["qev"], cnt - 2)
                        te.transpose((qp0 if cnt % 2 == 0 else qp1)[:, 0:128],
                                     qwq[:, ib * QW + k * 128:ib * QW + (k + 1) * 128],
                                     ident[:, :]).then_inc(sm["qtp"], 1)
                cnt = 0
                for ib in range(DNI):
                    te.wait_ge(chains[eng_of(ib)], done_at[ib])
                    for k in range(DNK):
                        cnt += 1
                        if cnt > 2:
                            te.wait_ge(sm["dev"], cnt - 2)
                        te.transpose((dp0 if cnt % 2 == 0 else dp1)[:, 0:128],
                                     dwq[:, ib * DW + k * 128:ib * DW + (k + 1) * 128],
                                     ident[:, :]).then_inc(sm["dtp"], 1)
    nc.all_engine_barrier()


def _prep_dequant_matrix(nc, ctx, sems, src, scol_src, outT, NI, W, tag=""):
    import contextlib
    _mctx = contextlib.ExitStack()
    with _mctx:
        sems = [_mctx.enter_context(nc.semaphore(f"pm{tag}_{i}")) for i in range(9)]
        _prep_dequant_body(nc, sems, src, scol_src, outT, NI, W, tag)
    return


def _prep_dequant_body(nc, sems, src, scol_src, outT, NI, W, tag=""):
    s_l0, s_dq, s_tp, s_ev, s_ar0, s_bi, s_sm, s_l1, s_ar1 = sems
    s_l = [s_l0, s_l1]
    s_ar = [s_ar0, s_ar1]
    NK = W // 128
    with (
        nc.sbuf_tensor("p_lw"+tag, [128, 2 * W], I32) as lw,
        nc.sbuf_tensor("p_wq"+tag, [128, NI * W], BF16) as wq,
        nc.sbuf_tensor("p_scol"+tag, [128, NI], F32) as scol,
        nc.sbuf_tensor("p_bcol"+tag, [128, NI], F32) as bcol,
        nc.sbuf_tensor("p_arow"+tag, [128, 2 * NI * 128], BF16) as arow,
        nc.sbuf_tensor("p_ident"+tag, [128, 128], BF16) as ident,
        nc.psum_tensor("p_pst0"+tag, [128, 512], BF16) as pst0,
        nc.psum_tensor("p_pst1"+tag, [128, 512], BF16) as pst1,
        nc.Block() as block,
    ):
        identT = nc.inline_tensor(np.eye(128, dtype=np.float32).astype(
            ml_dtypes.bfloat16), "p_identc"+tag)

        @block.sync
        def _(s):
            s.dma_start(out=ident[:, :], in_=identT.ap()).then_inc(s_sm, 16)
            s.dma_start(out=scol[:, :], in_=scol_src.rearrange("(b p) -> p b", p=128),
                        allow_slow_non_contiguous=True).then_inc(s_sm, 16)
            for ib in range(NI):
                if ib >= 2:
                    s.wait_ge(s_dq, ib - 1)
                s.dma_start(out=lw[:, (ib % 2) * W:(ib % 2 + 1) * W],
                            in_=src[ib * 128:(ib + 1) * 128, :]).then_inc(s_l[ib % 2], 16)

        @block.vector
        def _(v):
            v.wait_ge(s_sm, 32)
            v.tensor_scalar(bcol[:, :], scol[:, :], -128.0, None, ALU.mult).then_inc(s_bi, 1)

        @block.scalar
        def _(sc):
            sc.wait_ge(s_bi, 1)
            for ib in range(NI):
                sc.wait_ge(s_l[ib % 2], (ib // 2 + 1) * 16)
                sc.activation(wq[:, ib * W:(ib + 1) * W],
                              lw[:, (ib % 2) * W:(ib % 2 + 1) * W], AF.Identity,
                              bias=bcol[:, ib:ib + 1],
                              scale=scol[:, ib:ib + 1]).then_inc(s_dq, 1)
            cnt = 0
            for k in range(NK):
                for ib in range(NI):
                    cnt += 1
                    if ib == 0 and k >= 2:
                        sc.wait_ge(s_ar[k % 2], (k // 2) * 16)
                    sc.wait_ge(s_tp, cnt)
                    sc.copy(arow[:, (k % 2) * NI * 128 + ib * 128:
                                 (k % 2) * NI * 128 + (ib + 1) * 128],
                            (pst0 if cnt % 2 == 0 else pst1)[:, 0:128]).then_inc(s_ev, 1)

        @block.tensor
        def _(te):
            te.wait_ge(s_dq, NI)
            cnt = 0
            for k in range(NK):
                for ib in range(NI):
                    cnt += 1
                    if cnt > 2:
                        te.wait_ge(s_ev, cnt - 2)
                    if k >= 2 and ib == 0:
                        pass
                    te.transpose((pst0 if cnt % 2 == 0 else pst1)[:, 0:128],
                                 wq[:, ib * W + k * 128:ib * W + (k + 1) * 128],
                                 ident[:, :]).then_inc(s_tp, 1)

        @block.gpsimd
        def _(g):
            for k in range(NK):
                g.wait_ge(s_ev, (k + 1) * NI)
                g.dma_start(out=outT[k * 128:(k + 1) * 128, :],
                            in_=arow[:, (k % 2) * NI * 128:(k % 2 + 1) * NI * 128]
                            ).then_inc(s_ar[k % 2], 16)
            g.wait_ge(s_ar[0], (NK // 2) * 16)
            g.wait_ge(s_ar[1], (NK // 2) * 16)

    nc.all_engine_barrier()


def _prep_decode_matrix(nc, ctx, sems, src, sc2_src, outT, NI=16, W=2048, tag=""):
    import contextlib
    _mctx = contextlib.ExitStack()
    with _mctx:
        sems = [_mctx.enter_context(nc.semaphore(f"pq{tag}_{i}")) for i in range(9)]
        _prep_decode_body(nc, sems, src, sc2_src, outT, NI, W, tag)
    return


def _prep_decode_body(nc, sems, src, sc2_src, outT, NI=16, W=2048, tag=""):
    s_l0, s_dq, s_tp, s_ev, s_ar0, s_er, s_sm, s_l1, s_ar1 = sems
    s_l = [s_l0, s_l1]
    s_ar = [s_ar0, s_ar1]
    NK = W // 128
    WP = W // 2
    NSC = W // 32
    with (
        nc.sbuf_tensor("q_lw"+tag, [128, 2 * WP], I32) as lw,
        nc.sbuf_tensor("q_wq"+tag, [128, NI * W], BF16) as wq,
        nc.sbuf_tensor("q_s1t"+tag, [128, NI * NSC], F32) as s1t,
        nc.sbuf_tensor("q_lo"+tag, [128, WP], I32) as lo,
        nc.sbuf_tensor("q_hi"+tag, [128, WP], I32) as hi,
        nc.sbuf_tensor("q_t0"+tag, [128, WP], I32) as t0,
        nc.sbuf_tensor("q_ub"+tag, [128, WP], I32) as ub,
        nc.sbuf_tensor("q_sb"+tag, [128, WP], I32) as sb,
        nc.sbuf_tensor("q_err"+tag, [128, WP], F32) as errt,
        nc.sbuf_tensor("q_vm"+tag, [128, WP], F32) as vm,
        nc.sbuf_tensor("q_arow"+tag, [128, 2 * NI * 128], BF16) as arow,
        nc.sbuf_tensor("q_ident"+tag, [128, 128], BF16) as ident,
        nc.psum_tensor("q_pst0"+tag, [128, 512], BF16) as pst0,
        nc.psum_tensor("q_pst1"+tag, [128, 512], BF16) as pst1,
        nc.Block() as block,
    ):
        identT = nc.inline_tensor(np.eye(128, dtype=np.float32).astype(
            ml_dtypes.bfloat16), "q_identc"+tag)
        CH = [0]
        ch = s_dq

        @block.sync
        def _(s):
            s.dma_start(out=ident[:, :], in_=identT.ap()).then_inc(s_sm, 16)
            s.dma_start(out=s1t[:, :].rearrange("p (b n) -> p b n", n=NSC),
                        in_=sc2_src.rearrange("(b p) n -> p b n", p=128)).then_inc(s_sm, 16)
            for ib in range(NI):
                if ib >= 2:
                    s.wait_ge(ch, (ib - 1) * 18)
                s.dma_start(out=lw[:, (ib % 2) * WP:(ib % 2 + 1) * WP],
                            in_=src[ib * 128:(ib + 1) * 128, :]).then_inc(s_l[ib % 2], 16)

        @block.vector
        def _(v):
            v.wait_ge(s_sm, 32)
            for ib in range(NI):
                v.wait_ge(s_l[ib % 2], (ib // 2 + 1) * 16)
                lws = lw[:, (ib % 2) * WP:(ib % 2 + 1) * WP]
                v.tensor_scalar(lo[:, :], lws, 15, None, ALU.bitwise_and).then_inc(ch, 1)
                v.tensor_scalar(hi[:, :], lws, 4, 15, ALU.logical_shift_right,
                                ALU.bitwise_and).then_inc(ch, 1)
                CH[0] += 2
                for pl, csrc in ((0, lo), (1, hi)):
                    v.wait_ge(ch, CH[0])
                    v.tensor_scalar(t0[:, :], csrc[:, :], 7, None,
                                    ALU.bitwise_and).then_inc(ch, 1)
                    CH[0] += 1
                    v.wait_ge(ch, CH[0])
                    v.tensor_scalar(ub[:, :], t0[:, :], 252, 4194304, ALU.add,
                                    ALU.mult).then_inc(ch, 1)
                    v.tensor_scalar(sb[:, :], csrc[:, :], 8, 28, ALU.bitwise_and,
                                    ALU.logical_shift_left).then_inc(ch, 1)
                    CH[0] += 2
                    v.wait_ge(ch, CH[0])
                    v.tensor_scalar(errt[:, :], t0[:, :], -0.25, 0.5, ALU.mult,
                                    ALU.add).then_inc(ch, 1)
                    CH[0] += 1
                    v.wait_ge(ch, CH[0])
                    v.tensor_scalar(errt[:, :], errt[:, :], 0.0, None,
                                    ALU.max).then_inc(ch, 1)
                    CH[0] += 1
                    v.wait_ge(ch, CH[0])
                    v.tensor_tensor(vm[:, :], ub[:, :].bitcast(F32), errt[:, :],
                                    ALU.subtract).then_inc(ch, 1)
                    CH[0] += 1
                    v.wait_ge(ch, CH[0])
                    v.tensor_tensor(vm[:, :].bitcast(I32), vm[:, :].bitcast(I32),
                                    sb[:, :], ALU.bitwise_or).then_inc(ch, 1)
                    CH[0] += 1
                    v.wait_ge(ch, CH[0])
                    sbc = s1t[:, ib * NSC:(ib + 1) * NSC].rearrange(
                        "p (n o) -> p n o", o=1).broadcast_to([128, NSC, 16])
                    ow = wq[:, ib * W + pl:(ib + 1) * W:2].rearrange(
                        "p (n o) -> p n o", o=16)
                    v.tensor_tensor(ow, vm[:, :].rearrange("p (n o) -> p n o", o=16),
                                    sbc, ALU.mult).then_inc(ch, 1)
                    CH[0] += 1

        @block.scalar
        def _(sc):
            tcnt = 0
            for k in range(NK):
                for ib in range(NI):
                    tcnt += 1
                    if ib == 0 and k >= 2:
                        sc.wait_ge(s_ar[k % 2], (k // 2) * 16)
                    sc.wait_ge(s_tp, tcnt)
                    sc.copy(arow[:, (k % 2) * NI * 128 + ib * 128:
                                 (k % 2) * NI * 128 + (ib + 1) * 128],
                            (pst0 if tcnt % 2 == 0 else pst1)[:, 0:128]).then_inc(s_ev, 1)

        @block.tensor
        def _(te):
            te.wait_ge(ch, NI * 18)
            tcnt = 0
            for k in range(NK):
                for ib in range(NI):
                    tcnt += 1
                    if tcnt > 2:
                        te.wait_ge(s_ev, tcnt - 2)
                    te.transpose((pst0 if tcnt % 2 == 0 else pst1)[:, 0:128],
                                 wq[:, ib * W + k * 128:ib * W + (k + 1) * 128],
                                 ident[:, :]).then_inc(s_tp, 1)

        @block.gpsimd
        def _(g):
            for k in range(NK):
                g.wait_ge(s_ev, (k + 1) * NI)
                g.dma_start(out=outT[k * 128:(k + 1) * 128, :],
                            in_=arow[:, (k % 2) * NI * 128:(k % 2 + 1) * NI * 128]
                            ).then_inc(s_ar[k % 2], 16)
            g.wait_ge(s_ar[0], (NK // 2) * 16)
            g.wait_ge(s_ar[1], (NK // 2) * 16)
    nc.all_engine_barrier()


def build_prep():
    nc = bass.Bass("TRN2", target_bir_lowering=False)
    IC0 = 8192 // NC
    IC1 = 16384 // NC
    g0w = nc.dram_tensor("g0w", [IC0, D], I32, kind="ExternalInput").ap()
    g0s = nc.dram_tensor("g0s", [IC0], F32, kind="ExternalInput").ap()
    u0w = nc.dram_tensor("u0w", [IC0, D], I32, kind="ExternalInput").ap()
    u0s = nc.dram_tensor("u0s", [IC0], F32, kind="ExternalInput").ap()
    d0w = nc.dram_tensor("d0w", [D, IC0], I32, kind="ExternalInput").ap()
    d0s = nc.dram_tensor("d0s", [D], F32, kind="ExternalInput").ap()
    g1w = nc.dram_tensor("g1w", [IC1, D // 2], I32, kind="ExternalInput").ap()
    g1s = nc.dram_tensor("g1s", [IC1, D // 32], F32, kind="ExternalInput").ap()
    u1w = nc.dram_tensor("u1w", [IC1, D // 2], I32, kind="ExternalInput").ap()
    u1s = nc.dram_tensor("u1s", [IC1, D // 32], F32, kind="ExternalInput").ap()
    d1w = nc.dram_tensor("d1w", [D, IC1 // 2], I32, kind="ExternalInput").ap()
    d1s = nc.dram_tensor("d1s", [D, IC1 // 32], F32, kind="ExternalInput").ap()
    W0gT = nc.dram_tensor("W0gT", [D, IC0], BF16, kind="ExternalOutput").ap()
    W0uT = nc.dram_tensor("W0uT", [D, IC0], BF16, kind="ExternalOutput").ap()
    W0dT = nc.dram_tensor("W0dT", [IC0, D], BF16, kind="ExternalOutput").ap()
    W1gT = nc.dram_tensor("W1gT", [D, IC1], BF16, kind="ExternalOutput").ap()
    W1uT = nc.dram_tensor("W1uT", [D, IC1], BF16, kind="ExternalOutput").ap()
    W1dT = nc.dram_tensor("W1dT", [IC1, D], BF16, kind="ExternalOutput").ap()

    import contextlib
    ctx = contextlib.ExitStack()
    with ctx:
        sems = [ctx.enter_context(nc.semaphore(f"p_s{i}")) for i in range(6)]
        only = os.environ.get("S0_ONLY", "")
        def want(t):
            return (not only) or t in only.split(",")
        if want("g0"):
            _prep_dequant_matrix(nc, ctx, sems, g0w, g0s, W0gT, NI=IC0 // 128, W=D, tag="g0")
        if want("u0"):
            _prep_dequant_matrix(nc, ctx, sems, u0w, u0s, W0uT, NI=IC0 // 128, W=D, tag="u0")
        if want("d0"):
            _prep_dequant_matrix(nc, ctx, sems, d0w, d0s, W0dT, NI=D // 128, W=IC0, tag="d0")
        if want("g1"):
            _prep_decode_matrix(nc, ctx, sems, g1w, g1s, W1gT, NI=IC1 // 128, W=D, tag="g1")
        if want("u1"):
            _prep_decode_matrix(nc, ctx, sems, u1w, u1s, W1uT, NI=IC1 // 128, W=D, tag="u1")
        if want("d1"):
            _prep_decode_matrix(nc, ctx, sems, d1w, d1s, W1dT, NI=D // 128, W=IC1, tag="d1")
    return nc


def prep_inputs(inp):
    IC0 = 8192 // NC
    IC1 = 16384 // NC
    maps = []
    for c in range(NC):
        i0 = slice(c * IC0, (c + 1) * IC0)
        i1 = slice(c * IC1, (c + 1) * IC1)
        pk1 = slice(c * IC1 // 2, (c + 1) * IC1 // 2)
        sc1 = slice(c * IC1 // 32, (c + 1) * IC1 // 32)
        maps.append({
            "g0w": np.ascontiguousarray(inp["L0_ffn_gate_w"][i0, :]),
            "g0s": np.ascontiguousarray(inp["L0_ffn_gate_scale"][i0]),
            "u0w": np.ascontiguousarray(inp["L0_ffn_up_w"][i0, :]),
            "u0s": np.ascontiguousarray(inp["L0_ffn_up_scale"][i0]),
            "d0w": np.ascontiguousarray(inp["L0_ffn_down_w"][:, i0]),
            "d0s": np.ascontiguousarray(inp["L0_ffn_down_scale"]),
            "g1w": np.ascontiguousarray(inp["L1_ffn_gate_W_packed"][i1, :]),
            "g1s": np.ascontiguousarray(inp["L1_ffn_gate_scales"][i1, :]),
            "u1w": np.ascontiguousarray(inp["L1_ffn_up_W_packed"][i1, :]),
            "u1s": np.ascontiguousarray(inp["L1_ffn_up_scales"][i1, :]),
            "d1w": np.ascontiguousarray(inp["L1_ffn_down_W_packed"][:, pk1]),
            "d1s": np.ascontiguousarray(inp["L1_ffn_down_scales"][:, sc1]),
        })
    return maps



def emit_ffn(nc, layer, ypre_in, WgT, WuT, WdT, out_dram, I, final_ln, x1_dram,
             w_views=None, w_ready=None):
    tag = f"f{layer}"
    NTB = TC // 128
    NK = D // 128
    IC = I // 512
    import contextlib
    ctx = contextlib.ExitStack()
    with ctx:
        x1T = ctx.enter_context(nc.sbuf_tensor("x1T" + tag, [128, NK * TC], BF16))
        s_ev = ctx.enter_context(nc.semaphore("sev" + tag))
        s_ya = ctx.enter_context(nc.semaphore("sya" + tag))
        ya = ctx.enter_context(nc.sbuf_tensor("ya" + tag, [128, NTB * D], F32))
        identb = nc.inline_tensor(np.eye(128, dtype=np.float32).astype(
            ml_dtypes.bfloat16), "idb" + tag)

        with (
            nc.sbuf_tensor("yp" + tag, [128, 2 * D], F32) as yp,
            nc.sbuf_tensor("sq" + tag, [128, D], F32) as sq,
            nc.sbuf_tensor("x1f" + tag, [128, 2 * D], F32) as x1f,
            nc.sbuf_tensor("x1b" + tag, [128, D], BF16) as x1b,
            nc.sbuf_tensor("st" + tag, [128, 8], F32) as st,
            nc.sbuf_tensor("rstd" + tag, [128, 1], F32) as rstd,
            nc.sbuf_tensor("idl" + tag, [128, 128], BF16) as idl,
            nc.psum_tensor("pt0" + tag, [128, 512], BF16) as pt0,
            nc.psum_tensor("pt1" + tag, [128, 512], BF16) as pt1,
            nc.semaphore("sl0" + tag) as s_l0,
            nc.semaphore("sl1" + tag) as s_l1,
            nc.semaphore("ssq" + tag) as s_sq,
            nc.semaphore("sst" + tag) as s_st,
            nc.semaphore("sx1" + tag) as s_x1,
            nc.semaphore("sxb" + tag) as s_xb,
            nc.semaphore("stp" + tag) as s_tp,
            nc.semaphore("sxd" + tag) as s_xd,
            nc.semaphore("sidl" + tag) as s_idl,
            nc.Block() as block,
        ):
            s_l = [s_l0, s_l1]

            @block.sync
            def _(s):
                s.dma_start(out=idl[:, :], in_=identb.ap()).then_inc(s_idl, 16)
                for tb in range(NTB):
                    if tb >= 2:
                        s.wait_ge(s_x1, tb - 1)
                    s.dma_start(out=yp[:, (tb % 2) * D:(tb % 2 + 1) * D],
                                in_=ypre_in[tb * 128:(tb + 1) * 128, :]).then_inc(
                                    s_l[tb % 2], 16)

            @block.scalar
            def _(sc):
                for tb in range(NTB):
                    yps = yp[:, (tb % 2) * D:(tb % 2 + 1) * D]
                    sc.wait_ge(s_l[tb % 2], (tb // 2 + 1) * 16)
                    if tb > 0:
                        sc.wait_ge(s_st, tb * 9)
                    sc.activation(sq[:, :], yps, AF.Square).then_inc(s_sq, 1)
                    sc.wait_ge(s_st, tb * 9 + 7)
                    sc.sqrt(st[:, 6:7], st[:, 5:6]).then_inc(s_sq, 1)
                    sc.wait_ge(s_st, tb * 9 + 9)
                    if tb >= 2:
                        sc.wait_ge(s_xd, tb * 16)
                    sc.activation(x1f[:, (tb % 2) * D:(tb % 2 + 1) * D], yps,
                                  AF.Identity, bias=st[:, 7:8],
                                  scale=rstd[:, 0:1]).then_inc(s_x1, 1)

            @block.vector
            def _(v):
                for tb in range(NTB):
                    yps = yp[:, (tb % 2) * D:(tb % 2 + 1) * D]
                    v.wait_ge(s_l[tb % 2], (tb // 2 + 1) * 16)
                    if tb > 0:
                        v.wait_ge(s_x1, tb)
                    v.reduce_sum(st[:, 0:1], yps, mybir.AxisListType.X).then_inc(s_st, 1)
                    v.wait_ge(s_sq, 2 * tb + 1)
                    v.reduce_sum(st[:, 1:2], sq[:, :], mybir.AxisListType.X).then_inc(s_st, 1)
                    v.wait_ge(s_st, tb * 9 + 2)
                    v.tensor_scalar(st[:, 2:3], st[:, 0:1], 1.0 / D, None, ALU.mult).then_inc(s_st, 1)
                    v.tensor_scalar(st[:, 3:4], st[:, 1:2], 1.0 / D, None, ALU.mult).then_inc(s_st, 1)
                    v.wait_ge(s_st, tb * 9 + 4)
                    v.tensor_tensor(st[:, 4:5], st[:, 2:3], st[:, 2:3], ALU.mult).then_inc(s_st, 1)
                    v.wait_ge(s_st, tb * 9 + 5)
                    v.tensor_tensor(st[:, 5:6], st[:, 3:4], st[:, 4:5], ALU.subtract).then_inc(s_st, 1)
                    v.wait_ge(s_st, tb * 9 + 6)
                    v.tensor_scalar(st[:, 5:6], st[:, 5:6], 1e-5, None, ALU.add).then_inc(s_st, 1)
                    v.wait_ge(s_sq, 2 * tb + 2)
                    v.reciprocal(rstd[:, 0:1], st[:, 6:7]).then_inc(s_st, 1)
                    v.wait_ge(s_st, tb * 9 + 8)
                    v.tensor_scalar(st[:, 7:8], st[:, 2:3], -1.0, rstd[:, 0:1],
                                    ALU.mult, ALU.mult).then_inc(s_st, 1)
                    v.wait_ge(s_x1, tb + 1)
                    if tb > 0:
                        v.wait_ge(s_tp, tb * NK)
                    v.tensor_copy(x1b[:, :], x1f[:, (tb % 2) * D:(tb % 2 + 1) * D]
                                  ).then_inc(s_xb, 1)
                    for k in range(NK):
                        cntv = tb * NK + k + 1
                        v.wait_ge(s_tp, cntv)
                        v.tensor_copy(x1T[:, k * TC + tb * 128:k * TC + (tb + 1) * 128],
                                      (pt0 if cntv % 2 == 0 else pt1)[:, 0:128]
                                      ).then_inc(s_ev, 1)

            @block.gpsimd
            def _(g):
                for tb in range(NTB):
                    g.wait_ge(s_x1, tb + 1)
                    g.dma_start(out=x1_dram[tb * 128:(tb + 1) * 128, :],
                                in_=x1f[:, (tb % 2) * D:(tb % 2 + 1) * D]).then_inc(s_xd, 16)
                g.wait_ge(s_xd, NTB * 16)

            @block.tensor
            def _(te):
                te.wait_ge(s_idl, 16)
                cnt = 0
                for tb in range(NTB):
                    te.wait_ge(s_xb, tb + 1)
                    for k in range(NK):
                        cnt += 1
                        if cnt > 2:
                            te.wait_ge(s_ev, cnt - 2)
                        te.transpose((pt0 if cnt % 2 == 0 else pt1)[:, 0:128],
                                     x1b[:, k * 128:(k + 1) * 128],
                                     idl[:, :]).then_inc(s_tp, 1)

        nc.all_engine_barrier()

        NSUB = 4
        TH = TC // 512
        import contextlib as _ct
        _fctx = _ct.ExitStack()
        with _fctx:
            _e = _fctx.enter_context
            wg = _e(nc.sbuf_tensor("wg" + tag, [128, 2 * NK * 512], BF16))
            wu = _e(nc.sbuf_tensor("wu" + tag, [128, 2 * NK * 512], BF16))
            wd = _e(nc.sbuf_tensor("wd" + tag, [128, 4 * D], BF16))
            sgt = _e(nc.sbuf_tensor("sg" + tag, [128, NSUB * TC], BF16))
            uet = _e(nc.sbuf_tensor("ue" + tag, [128, NSUB * TC], BF16))
            itT = _e(nc.sbuf_tensor("itT" + tag, [128, NSUB * TC], BF16))
            if layer == 1:
                aat = _e(nc.sbuf_tensor("aa" + tag, [128, TC], BF16))
                bbt = _e(nc.sbuf_tensor("bb" + tag, [128, TC], BF16))
            pg = _e(nc.psum_tensor("pg" + tag, [128, TC], F32))
            pu = _e(nc.psum_tensor("pu" + tag, [128, TC], F32))
            py0 = _e(nc.psum_tensor("py0" + tag, [128, D // 2], F32))
            py1 = _e(nc.psum_tensor("py1" + tag, [128, D // 2], F32))
            pys = [py0, py1]
            s_w0 = _e(nc.semaphore("sw0" + tag))
            s_w1 = _e(nc.semaphore("sw1" + tag))
            s_wd = _e(nc.semaphore("swd" + tag))
            s_mg = _e(nc.semaphore("smg" + tag))
            s_mu = _e(nc.semaphore("smu" + tag))
            s_my = _e(nc.semaphore("smy" + tag))
            s_ge = _e(nc.semaphore("sge" + tag))
            s_ue = _e(nc.semaphore("sue" + tag))
            s_it = _e(nc.semaphore("sit" + tag))
            s_qq = _e(nc.semaphore("sqq" + tag))
            s_sg = _e(nc.semaphore("ssg" + tag))
            ch_v = _e(nc.semaphore("svch" + tag))
            block = _e(nc.Block())
            s_w = [s_w0, s_w1]

            @block.sync
            def _(s):
                if w_ready is not None:
                    s.wait_ge(w_ready[0], w_ready[1])
                for ic in range(IC):
                    if ic >= 2:
                        s.wait_ge(s_mg, (ic - 1) * NSUB * TH)
                        s.wait_ge(s_mu, (ic - 1) * NSUB * TH)
                    if w_views is None:
                        gsrc = WgT.rearrange("(k p) i -> p k i", p=128)[
                            :, :, ic * 512:(ic + 1) * 512]
                        usrc = WuT.rearrange("(k p) i -> p k i", p=128)[
                            :, :, ic * 512:(ic + 1) * 512]
                    else:
                        gsrc, usrc = w_views(ic)
                    s.dma_start(
                        out=wg[:, (ic % 2) * NK * 512:(ic % 2 + 1) * NK * 512].rearrange(
                            "p (k i) -> p k i", k=NK),
                        in_=gsrc).then_inc(s_w[ic % 2], 16)
                    s.dma_start(
                        out=wu[:, (ic % 2) * NK * 512:(ic % 2 + 1) * NK * 512].rearrange(
                            "p (k i) -> p k i", k=NK),
                        in_=usrc).then_inc(s_w[ic % 2], 16)

            @block.gpsimd
            def _(g):
                for ic in range(IC):
                    if ic >= 1:
                        g.wait_ge(s_my, ic * NTB * 2)
                    g.dma_start(
                        out=wd[:, :].rearrange("p (r h) -> p r h", r=4),
                        in_=WdT[ic * 512:(ic + 1) * 512, :].rearrange(
                            "(r p) h -> p r h", p=128)).then_inc(s_wd, 16)

            @block.tensor
            def _(te):
                def down(j):
                    te.wait_ge(s_it, (j + 1) * NSUB)
                    te.wait_ge(s_wd, (j + 1) * 16)
                    for tb in range(NTB):
                        for dh in range(2):
                            g = (j * NTB + tb) * 2 + dh
                            if g > 1:
                                te.wait_ge(s_ya, g - 1)
                            for isub in range(NSUB):
                                for hc in range(2):
                                    mm = te.matmul(
                                        pys[g % 2][:, hc * 512:(hc + 1) * 512],
                                        itT[:, isub * TC + tb * 128:
                                            isub * TC + (tb + 1) * 128],
                                        wd[:, isub * D + dh * 1024 + hc * 512:
                                           isub * D + dh * 1024 + (hc + 1) * 512],
                                        start=(isub == 0), stop=(isub == NSUB - 1))
                            mm.then_inc(s_my, 1)
                for ic in range(IC):
                    for isub in range(NSUB):
                        g_ = ic * NSUB + isub
                        if isub == 0:
                            te.wait_ge(s_w[ic % 2], (ic // 2 + 1) * 32)
                        if g_ == 0:
                            te.wait_ge(s_ev, NTB * NK)
                        if g_ > 0:
                            te.wait_ge(s_ge, g_)
                        wslot = (ic % 2) * NK * 512
                        for k in range(NK):
                            for th in range(TH):
                                mm = te.matmul(
                                    pg[:, th * 512:(th + 1) * 512],
                                    wg[:, wslot + k * 512 + isub * 128:
                                       wslot + k * 512 + (isub + 1) * 128],
                                    x1T[:, k * TC + th * 512:k * TC + (th + 1) * 512],
                                    start=(k == 0), stop=(k == NK - 1))
                                if k == NK - 1:
                                    mm.then_inc(s_mg, 1)
                        if g_ > 0:
                            te.wait_ge(s_ue, g_)
                        for k in range(NK):
                            for th in range(TH):
                                mm = te.matmul(
                                    pu[:, th * 512:(th + 1) * 512],
                                    wu[:, wslot + k * 512 + isub * 128:
                                       wslot + k * 512 + (isub + 1) * 128],
                                    x1T[:, k * TC + th * 512:k * TC + (th + 1) * 512],
                                    start=(k == 0), stop=(k == NK - 1))
                                if k == NK - 1:
                                    mm.then_inc(s_mu, 1)
                    if ic >= 1:
                        down(ic - 1)
                down(IC - 1)

            @block.scalar
            def _(sc):
                def sig(j, isub):
                    sc.wait_ge(s_qq, j * NSUB + isub + 1)
                    sc.activation(bbt[:, :], aat[:, :],
                                  AF.Sigmoid).then_inc(s_sg, 1)

                def evac(ic, isub):
                    g_ = ic * NSUB + isub
                    sc.wait_ge(s_mg, g_ * TH + TH)
                    if ic >= 1:
                        sc.wait_ge(s_it, (ic - 1) * NSUB + isub + 1)
                    if layer == 0:
                        sc.activation(sgt[:, isub * TC:(isub + 1) * TC],
                                      pg[:, :], AF.Silu).then_inc(s_ge, 1)
                    else:
                        sc.copy(sgt[:, isub * TC:(isub + 1) * TC],
                                pg[:, :]).then_inc(s_ge, 1)
                for ic in range(IC):
                    if layer == 1 and ic >= 1:
                        sig(ic - 1, 0)
                    evac(ic, 0)
                    if layer == 1 and ic >= 1:
                        sig(ic - 1, 1)
                        sig(ic - 1, 2)
                        sig(ic - 1, 3)
                    evac(ic, 1)
                    evac(ic, 2)
                    evac(ic, 3)
                if layer == 1:
                    for isub in range(NSUB):
                        sig(IC - 1, isub)

            CHV = [0]

            @block.vector
            def _(v):
                def accum(j):
                    for tb in range(NTB):
                        for dh in range(2):
                            g = (j * NTB + tb) * 2 + dh
                            v.wait_ge(s_my, g + 1)
                            if j >= 1:
                                v.wait_ge(s_ya, g - NTB * 2 + 1)
                            ysl = ya[:, tb * D + dh * 1024:tb * D + (dh + 1) * 1024]
                            if j == 0:
                                v.tensor_copy(ysl, pys[g % 2][:, :]).then_inc(s_ya, 1)
                            else:
                                v.tensor_tensor(ysl, ysl, pys[g % 2][:, :],
                                                ALU.add).then_inc(s_ya, 1)
                for ic in range(IC):
                    for isub in range(NSUB):
                        g_ = ic * NSUB + isub
                        v.wait_ge(s_mu, g_ * TH + TH)
                        if ic >= 1:
                            v.wait_ge(s_it, (ic - 1) * NSUB + isub + 1)
                        v.tensor_copy(uet[:, isub * TC:(isub + 1) * TC],
                                      pu[:, :]).then_inc(s_ue, 1)
                    if ic >= 1:
                        accum(ic - 1)
                    v.wait_ge(s_my, ic * NTB * 2)
                    for isub in range(NSUB):
                        g_ = ic * NSUB + isub
                        sgs = sgt[:, isub * TC:(isub + 1) * TC]
                        ues = uet[:, isub * TC:(isub + 1) * TC]
                        its = itT[:, isub * TC:(isub + 1) * TC]
                        if layer == 0:
                            v.wait_ge(s_ge, g_ + 1)
                            v.tensor_tensor(its, sgs, ues, ALU.mult).then_inc(s_it, 1)
                        else:
                            v.wait_ge(s_ge, g_ + 1)
                            v.tensor_tensor(aat[:, :], sgs, sgs, ALU.mult).then_inc(ch_v, 1)
                            CHV[0] += 1
                            v.wait_ge(ch_v, CHV[0])
                            v.tensor_scalar(bbt[:, :], aat[:, :], 0.07135481627,
                                            1.595769122, ALU.mult, ALU.add).then_inc(ch_v, 1)
                            CHV[0] += 1
                            v.wait_ge(ch_v, CHV[0])
                            v.tensor_tensor(aat[:, :], sgs, bbt[:, :],
                                            ALU.mult).then_inc(s_qq, 1)
                            v.wait_ge(s_sg, g_ + 1)
                            v.tensor_tensor(aat[:, :], sgs, bbt[:, :],
                                            ALU.mult).then_inc(ch_v, 1)
                            CHV[0] += 1
                            v.wait_ge(ch_v, CHV[0])
                            v.tensor_tensor(its, aat[:, :], ues, ALU.mult).then_inc(s_it, 1)
                accum(IC - 1)
        nc.all_engine_barrier()

        with (
            nc.sbuf_tensor("x1r" + tag, [128, 2 * D], F32) as x1r,
            nc.sbuf_tensor("rr" + tag, [128, 2 * D], F32) as rr,
            nc.sbuf_tensor("ro" + tag, [128, 2 * D], F32) as ro,
            nc.sbuf_tensor("sqR" + tag, [128, D], F32) as sqR,
            nc.sbuf_tensor("stR" + tag, [128, 8], F32) as stR,
            nc.sbuf_tensor("rstdR" + tag, [128, 1], F32) as rstdR,
            nc.semaphore("srl0" + tag) as s_rl0,
            nc.semaphore("srl1" + tag) as s_rl1,
            nc.semaphore("sra" + tag) as s_ra,
            nc.semaphore("srs" + tag) as s_rs,
            nc.semaphore("srq" + tag) as s_rq,
            nc.semaphore("sro" + tag) as s_ro,
            nc.semaphore("sfin" + tag) as s_fin,
            nc.Block() as block,
        ):
            s_rl = [s_rl0, s_rl1]

            @block.sync
            def _(s):
                for tb in range(NTB):
                    if tb >= 2:
                        s.wait_ge(s_ra, tb - 1)
                    s.dma_start(out=x1r[:, (tb % 2) * D:(tb % 2 + 1) * D],
                                in_=x1_dram[tb * 128:(tb + 1) * 128, :]).then_inc(
                                    s_rl[tb % 2], 16)

            @block.vector
            def _(v):
                for tb in range(NTB):
                    v.wait_ge(s_rl[tb % 2], (tb // 2 + 1) * 16)
                    if tb >= 2:
                        v.wait_ge(s_ro, tb * 16)
                    v.tensor_tensor(rr[:, (tb % 2) * D:(tb % 2 + 1) * D],
                                    x1r[:, (tb % 2) * D:(tb % 2 + 1) * D],
                                    ya[:, tb * D:(tb + 1) * D], ALU.add).then_inc(s_ra, 1)
                    if final_ln:
                        rrs = rr[:, (tb % 2) * D:(tb % 2 + 1) * D]
                        if tb > 0:
                            v.wait_ge(s_fin, tb)
                        v.wait_ge(s_ra, tb + 1)
                        v.reduce_sum(stR[:, 0:1], rrs, mybir.AxisListType.X).then_inc(s_rs, 1)
                        v.wait_ge(s_rq, 2 * tb + 1)
                        v.reduce_sum(stR[:, 1:2], sqR[:, :], mybir.AxisListType.X).then_inc(s_rs, 1)
                        v.wait_ge(s_rs, tb * 9 + 2)
                        v.tensor_scalar(stR[:, 2:3], stR[:, 0:1], 1.0 / D, None, ALU.mult).then_inc(s_rs, 1)
                        v.tensor_scalar(stR[:, 3:4], stR[:, 1:2], 1.0 / D, None, ALU.mult).then_inc(s_rs, 1)
                        v.wait_ge(s_rs, tb * 9 + 4)
                        v.tensor_tensor(stR[:, 4:5], stR[:, 2:3], stR[:, 2:3], ALU.mult).then_inc(s_rs, 1)
                        v.wait_ge(s_rs, tb * 9 + 5)
                        v.tensor_tensor(stR[:, 5:6], stR[:, 3:4], stR[:, 4:5], ALU.subtract).then_inc(s_rs, 1)
                        v.wait_ge(s_rs, tb * 9 + 6)
                        v.tensor_scalar(stR[:, 5:6], stR[:, 5:6], 1e-5, None, ALU.add).then_inc(s_rs, 1)
                        v.wait_ge(s_rq, 2 * tb + 2)
                        v.reciprocal(rstdR[:, 0:1], stR[:, 6:7]).then_inc(s_rs, 1)
                        v.wait_ge(s_rs, tb * 9 + 8)
                        v.tensor_scalar(stR[:, 7:8], stR[:, 2:3], -1.0, rstdR[:, 0:1],
                                        ALU.mult, ALU.mult).then_inc(s_rs, 1)

            if final_ln:
                @block.scalar
                def _(sc):
                    for tb in range(NTB):
                        rrs = rr[:, (tb % 2) * D:(tb % 2 + 1) * D]
                        sc.wait_ge(s_ra, tb + 1)
                        if tb > 0:
                            sc.wait_ge(s_rs, tb * 9)
                        sc.activation(sqR[:, :], rrs, AF.Square).then_inc(s_rq, 1)
                        sc.wait_ge(s_rs, tb * 9 + 7)
                        sc.sqrt(stR[:, 6:7], stR[:, 5:6]).then_inc(s_rq, 1)
                        sc.wait_ge(s_rs, tb * 9 + 9)
                        if tb >= 2:
                            sc.wait_ge(s_ro, tb * 16)
                        sc.activation(ro[:, (tb % 2) * D:(tb % 2 + 1) * D], rrs,
                                      AF.Identity, bias=stR[:, 7:8],
                                      scale=rstdR[:, 0:1]).then_inc(s_fin, 1)

            @block.gpsimd
            def _(g):
                for tb in range(NTB):
                    if final_ln:
                        g.wait_ge(s_fin, tb + 1)
                        srcap = ro[:, (tb % 2) * D:(tb % 2 + 1) * D]
                    else:
                        g.wait_ge(s_ra, tb + 1)
                        srcap = rr[:, (tb % 2) * D:(tb % 2 + 1) * D]
                    g.dma_start(out=out_dram[tb * 128:(tb + 1) * 128, :],
                                in_=srcap).then_inc(s_ro, 16)
                g.wait_ge(s_ro, NTB * 16)
        nc.all_engine_barrier()


def build_ffn(layer):
    nc = bass.Bass("TRN2", target_bir_lowering=False)
    I = 8192 if layer == 0 else 16384
    ypre = nc.dram_tensor("ypre", [TC, D], F32, kind="ExternalInput").ap()
    WgT = nc.dram_tensor("WgT", [D, I], BF16, kind="ExternalInput").ap()
    WuT = nc.dram_tensor("WuT", [D, I], BF16, kind="ExternalInput").ap()
    WdT = nc.dram_tensor("WdT", [I, D], BF16, kind="ExternalInput").ap()
    outd = nc.dram_tensor("outd", [TC, D], F32, kind="ExternalOutput").ap()
    x1_dram = nc.dram_tensor("x1_dram", [TC, D], F32, kind="Internal")
    emit_ffn(nc, layer, ypre, WgT, WuT, WdT, outd, I, final_ln=(layer == 1),
             x1_dram=x1_dram.ap())
    return nc



def emit_attn(nc, ypre_in, out_dram):
    tag = "a3"
    NTB = TC // 128
    NK = D // 128
    NSEG = TC // 512
    H = 16
    SC = 0.08838834764831845
    import contextlib
    ctx = contextlib.ExitStack()
    with ctx:
        _e = ctx.enter_context
        x2f = _e(nc.sbuf_tensor("x2f" + tag, [128, NTB * D], F32))
        x2T = _e(nc.sbuf_tensor("x2T" + tag, [128, NK * TC], BF16))
        s_ev = _e(nc.semaphore("sev" + tag))

        perms = {}
        for p_ in range(2):
            for hf in range(2):
                m = np.zeros((128, 128), np.float32)
                for qp in range(64 * hf, 64 * (hf + 1)):
                    m[qp, 2 * (qp - 64 * hf) + p_] = 1.0
                perms[(p_, hf)] = nc.inline_tensor(
                    m.astype(ml_dtypes.bfloat16), f"perm{p_}{hf}" + tag)
        identb = nc.inline_tensor(np.eye(128, dtype=np.float32).astype(
            ml_dtypes.bfloat16), "ida" + tag)

        with (
            nc.sbuf_tensor("yp" + tag, [128, 2 * D], F32) as yp,
            nc.sbuf_tensor("sq" + tag, [128, D], F32) as sq,
            nc.sbuf_tensor("x1b" + tag, [128, D], BF16) as x1b,
            nc.sbuf_tensor("st" + tag, [128, 8], F32) as st,
            nc.sbuf_tensor("rstd" + tag, [128, 1], F32) as rstd,
            nc.sbuf_tensor("idl" + tag, [128, 128], BF16) as idl,
            nc.psum_tensor("pt0" + tag, [128, 512], BF16) as pt0,
            nc.psum_tensor("pt1" + tag, [128, 512], BF16) as pt1,
            nc.semaphore("sl0" + tag) as s_l0,
            nc.semaphore("sl1" + tag) as s_l1,
            nc.semaphore("ssq" + tag) as s_sq,
            nc.semaphore("sst" + tag) as s_st,
            nc.semaphore("sx1" + tag) as s_x1,
            nc.semaphore("sxb" + tag) as s_xb,
            nc.semaphore("stp" + tag) as s_tp,
            nc.semaphore("sidl" + tag) as s_idl,
            nc.Block() as block,
        ):
            s_l = [s_l0, s_l1]

            @block.sync
            def _(s):
                s.dma_start(out=idl[:, :], in_=identb.ap()).then_inc(s_idl, 16)
                for tb in range(NTB):
                    if tb >= 2:
                        s.wait_ge(s_x1, tb - 1)
                    s.dma_start(out=yp[:, (tb % 2) * D:(tb % 2 + 1) * D],
                                in_=ypre_in[tb * 128:(tb + 1) * 128, :]).then_inc(
                                    s_l[tb % 2], 16)

            @block.scalar
            def _(sc):
                for tb in range(NTB):
                    yps = yp[:, (tb % 2) * D:(tb % 2 + 1) * D]
                    sc.wait_ge(s_l[tb % 2], (tb // 2 + 1) * 16)
                    if tb > 0:
                        sc.wait_ge(s_st, tb * 9)
                    sc.activation(sq[:, :], yps, AF.Square).then_inc(s_sq, 1)
                    sc.wait_ge(s_st, tb * 9 + 7)
                    sc.sqrt(st[:, 6:7], st[:, 5:6]).then_inc(s_sq, 1)
                    sc.wait_ge(s_st, tb * 9 + 9)
                    sc.activation(x2f[:, tb * D:(tb + 1) * D], yps,
                                  AF.Identity, bias=st[:, 7:8],
                                  scale=rstd[:, 0:1]).then_inc(s_x1, 1)

            @block.vector
            def _(v):
                for tb in range(NTB):
                    yps = yp[:, (tb % 2) * D:(tb % 2 + 1) * D]
                    v.wait_ge(s_l[tb % 2], (tb // 2 + 1) * 16)
                    if tb > 0:
                        v.wait_ge(s_x1, tb)
                    v.reduce_sum(st[:, 0:1], yps, mybir.AxisListType.X).then_inc(s_st, 1)
                    v.wait_ge(s_sq, 2 * tb + 1)
                    v.reduce_sum(st[:, 1:2], sq[:, :], mybir.AxisListType.X).then_inc(s_st, 1)
                    v.wait_ge(s_st, tb * 9 + 2)
                    v.tensor_scalar(st[:, 2:3], st[:, 0:1], 1.0 / D, None, ALU.mult).then_inc(s_st, 1)
                    v.tensor_scalar(st[:, 3:4], st[:, 1:2], 1.0 / D, None, ALU.mult).then_inc(s_st, 1)
                    v.wait_ge(s_st, tb * 9 + 4)
                    v.tensor_tensor(st[:, 4:5], st[:, 2:3], st[:, 2:3], ALU.mult).then_inc(s_st, 1)
                    v.wait_ge(s_st, tb * 9 + 5)
                    v.tensor_tensor(st[:, 5:6], st[:, 3:4], st[:, 4:5], ALU.subtract).then_inc(s_st, 1)
                    v.wait_ge(s_st, tb * 9 + 6)
                    v.tensor_scalar(st[:, 5:6], st[:, 5:6], 1e-5, None, ALU.add).then_inc(s_st, 1)
                    v.wait_ge(s_sq, 2 * tb + 2)
                    v.reciprocal(rstd[:, 0:1], st[:, 6:7]).then_inc(s_st, 1)
                    v.wait_ge(s_st, tb * 9 + 8)
                    v.tensor_scalar(st[:, 7:8], st[:, 2:3], -1.0, rstd[:, 0:1],
                                    ALU.mult, ALU.mult).then_inc(s_st, 1)
                    v.wait_ge(s_x1, tb + 1)
                    if tb > 0:
                        v.wait_ge(s_tp, tb * NK)
                    v.tensor_copy(x1b[:, :], x2f[:, tb * D:(tb + 1) * D]).then_inc(s_xb, 1)
                    for k in range(NK):
                        cntv = tb * NK + k + 1
                        v.wait_ge(s_tp, cntv)
                        v.tensor_copy(x2T[:, k * TC + tb * 128:k * TC + (tb + 1) * 128],
                                      (pt0 if cntv % 2 == 0 else pt1)[:, 0:128]
                                      ).then_inc(s_ev, 1)

            @block.tensor
            def _(te):
                te.wait_ge(s_idl, 16)
                cnt = 0
                for tb in range(NTB):
                    te.wait_ge(s_xb, tb + 1)
                    for k in range(NK):
                        cnt += 1
                        if cnt > 2:
                            te.wait_ge(s_ev, cnt - 2)
                        te.transpose((pt0 if cnt % 2 == 0 else pt1)[:, 0:128],
                                     x1b[:, k * 128:(k + 1) * 128],
                                     idl[:, :]).then_inc(s_tp, 1)

        nc.all_engine_barrier()

        import contextlib as _ct
        _actx = _ct.ExitStack()
        with _actx:
            _a = _actx.enter_context
            pm = _a(nc.sbuf_tensor("pm" + tag, [128, 4 * 128], BF16))
            ida2 = _a(nc.sbuf_tensor("ida2" + tag, [128, 128], BF16))
            pb = _a(nc.sbuf_tensor("pb" + tag, [128, 2 * 256], BF16))
            atn = _a(nc.sbuf_tensor("atn" + tag, [128, 2 * 256], BF16))
            atT = _a(nc.sbuf_tensor("atT" + tag, [128, 2 * 256], BF16))
            gk = _a(nc.sbuf_tensor("gk" + tag, [128, 2 * 128], BF16))
            ob = _a(nc.sbuf_tensor("ob" + tag, [128, 2 * 128], BF16))
            sm = _a(nc.sbuf_tensor("sm" + tag, [128, 6], F32))
            attsb = _a(nc.sbuf_tensor("att" + tag, [128, NTB * D], F32))
            pss0 = _a(nc.psum_tensor("pss0" + tag, [128, 256], F32))
            pss1 = _a(nc.psum_tensor("pss1" + tag, [128, 256], F32))
            ptr0 = _a(nc.psum_tensor("ptr0" + tag, [128, 128], BF16))
            ptr1 = _a(nc.psum_tensor("ptr1" + tag, [128, 128], BF16))
            pso = _a(nc.psum_tensor("pso" + tag, [128, 128], F32))
            ppm = _a(nc.psum_tensor("ppm" + tag, [128, 128], F32))
            s_pm = _a(nc.semaphore("spm" + tag))
            s_ms = _a(nc.semaphore("sms" + tag))
            s_sx = _a(nc.semaphore("ssx" + tag))
            ch = _a(nc.semaphore("svx" + tag))
            s_tr = _a(nc.semaphore("str" + tag))
            s_te = _a(nc.semaphore("ste" + tag))
            s_mo = _a(nc.semaphore("smo" + tag))
            s_oe = _a(nc.semaphore("soe" + tag))
            s_mp = _a(nc.semaphore("smp" + tag))
            s_pc = _a(nc.semaphore("spc" + tag))
            s_ro = _a(nc.semaphore("sro" + tag))
            s_ra = _a(nc.semaphore("sraa" + tag))
            block = _a(nc.Block())

            @block.sync
            def _(s):
                s.dma_start(out=ida2[:, :], in_=identb.ap()).then_inc(s_pm, 16)
                for i, (p_, hf) in enumerate([(0, 0), (0, 1), (1, 0), (1, 1)]):
                    s.dma_start(out=pm[:, i * 128:(i + 1) * 128],
                                in_=perms[(p_, hf)].ap()).then_inc(s_pm, 16)

            @block.tensor
            def _(te):
                te.wait_ge(s_pm, 80)
                trc = 0
                for s_ in range(NSEG):
                    for h in range(H):
                        idx = s_ * H + h
                        p_ = h % 2
                        base = h * TC + s_ * 512 + p_
                        gTs = [x2T[:, base + 2 * qh * 128:base + 2 * (qh + 1) * 128 - 1:2]
                               for qh in range(2)]
                        gTfull = x2T[:, base:base + 511:2]
                        for qh in range(2):
                            if idx >= 1:
                                te.wait_ge(ch, (idx - 1) * 10 + 5 * qh + 1)
                                te.wait_ge(s_sx, 2 * (idx - 1) + qh + 1)
                            te.matmul((pss0 if qh == 0 else pss1)[:, :], gTs[qh],
                                      gTfull, start=True, stop=True).then_inc(s_ms, 1)
                        te.wait_ge(ch, idx * 10 + 10)
                        for j in range(6):
                            trc += 1
                            if trc > 2:
                                te.wait_ge(s_te, trc - 2)
                            if j < 4:
                                qh, kc = j // 2, j % 2
                                te.transpose((ptr0 if trc % 2 == 0 else ptr1)[:, :],
                                             atn[:, qh * 256 + kc * 128:
                                                 qh * 256 + (kc + 1) * 128],
                                             ida2[:, :]).then_inc(s_tr, 1)
                            else:
                                kc = j - 4
                                te.transpose((ptr0 if trc % 2 == 0 else ptr1)[:, :],
                                             gTs[kc], ida2[:, :]).then_inc(s_tr, 1)
                        te.wait_ge(s_te, idx * 6 + 6)
                        for qh in range(2):
                            te.wait_ge(s_oe, idx * 2 + qh)
                            for kc in range(2):
                                mm = te.matmul(pso[:, :],
                                               atT[:, kc * 256 + qh * 128:
                                                   kc * 256 + (qh + 1) * 128],
                                               gk[:, kc * 128:(kc + 1) * 128],
                                               start=(kc == 0), stop=(kc == 1))
                            mm.then_inc(s_mo, 1)
                        te.wait_ge(s_oe, idx * 2 + 2)
                        for qh in range(2):
                            for hf in range(2):
                                pmi = idx * 4 + 2 * qh + hf
                                te.wait_ge(s_pc, pmi)
                                te.matmul(ppm[:, :],
                                          pm[:, (p_ * 2 + hf) * 128:(p_ * 2 + hf + 1) * 128],
                                          ob[:, qh * 128:(qh + 1) * 128],
                                          start=True, stop=True).then_inc(s_mp, 1)

            @block.scalar
            def _(sc):
                for s_ in range(NSEG):
                    for h in range(H):
                        idx = s_ * H + h
                        for qh in range(2):
                            k_ = idx * 2 + qh
                            sc.wait_ge(s_ms, k_ + 1)
                            sc.wait_ge(ch, idx * 10 + 5 * qh + 2)
                            sc.activation(pb[:, qh * 256:(qh + 1) * 256],
                                          (pss0 if qh == 0 else pss1)[:, :], AF.Exp,
                                          bias=sm[:, 4 + qh:5 + qh],
                                          scale=SC).then_inc(s_sx, 1)

            @block.vector
            def _(v):
                for s_ in range(NSEG):
                    for h in range(H):
                        idx = s_ * H + h
                        for qh in range(2):
                            k_ = idx * 2 + qh
                            pss = pss0 if qh == 0 else pss1
                            v.wait_ge(s_ms, k_ + 1)
                            v.tensor_reduce(sm[:, 0:1], pss[:, :], mybir.AxisListType.X,
                                            ALU.max).then_inc(ch, 1)
                            v.wait_ge(ch, idx * 10 + 5 * qh + 1)
                            if idx >= 1:
                                v.wait_ge(s_sx, 2 * (idx - 1) + qh + 1)
                            v.tensor_scalar(sm[:, 4 + qh:5 + qh], sm[:, 0:1], -SC, None,
                                            ALU.mult).then_inc(ch, 1)
                            v.wait_ge(s_sx, k_ + 1)
                            v.reduce_sum(sm[:, 1:2], pb[:, qh * 256:(qh + 1) * 256],
                                         mybir.AxisListType.X).then_inc(ch, 1)
                            v.wait_ge(ch, idx * 10 + 5 * qh + 3)
                            v.reciprocal(sm[:, 2:3], sm[:, 1:2]).then_inc(ch, 1)
                            v.wait_ge(ch, idx * 10 + 5 * qh + 4)
                            if idx >= 1 and qh == 0:
                                v.wait_ge(s_tr, idx * 6)
                            v.tensor_scalar(atn[:, qh * 256:(qh + 1) * 256],
                                            pb[:, qh * 256:(qh + 1) * 256],
                                            sm[:, 2:3], None, ALU.mult).then_inc(ch, 1)
                        for j in range(6):
                            tcv = idx * 6 + j + 1
                            v.wait_ge(s_tr, tcv)
                            if idx >= 1 and j == 0:
                                v.wait_ge(s_mo, idx * 2)
                            dst = (atT[:, (j % 2) * 256 + (j // 2) * 128:
                                       (j % 2) * 256 + (j // 2) * 128 + 128]
                                   if j < 4 else gk[:, (j - 4) * 128:(j - 3) * 128])
                            v.tensor_copy(dst, (ptr0 if tcv % 2 == 0 else ptr1)[:, :]
                                          ).then_inc(s_te, 1)
                        for qh in range(2):
                            v.wait_ge(s_mo, idx * 2 + qh + 1)
                            if idx >= 1 and qh == 0:
                                v.wait_ge(s_mp, idx * 4)
                            v.tensor_copy(ob[:, qh * 128:(qh + 1) * 128],
                                          pso[:, :]).then_inc(s_oe, 1)
                        for qh in range(2):
                            for hf in range(2):
                                pmi = idx * 4 + 2 * qh + hf
                                v.wait_ge(s_mp, pmi + 1)
                                tb = s_ * 4 + 2 * qh + hf
                                v.tensor_copy(attsb[:, tb * D + h * 128:
                                                    tb * D + (h + 1) * 128],
                                              ppm[:, :]).then_inc(s_pc, 1)

            @block.gpsimd
            def _(g):
                for tb in range(NTB):
                    s_seg = tb // 4
                    g.wait_ge(s_pc, (s_seg + 1) * H * 4)
                    g.tensor_tensor(attsb[:, tb * D:(tb + 1) * D],
                                    attsb[:, tb * D:(tb + 1) * D],
                                    x2f[:, tb * D:(tb + 1) * D], ALU.add).then_inc(s_ra, 1)
                    g.wait_ge(s_ra, tb + 1)
                    g.dma_start(out=out_dram[tb * 128:(tb + 1) * 128, :],
                                in_=attsb[:, tb * D:(tb + 1) * D]).then_inc(s_ro, 16)
                g.wait_ge(s_ro, NTB * 16)
        nc.all_engine_barrier()


def build_attn():
    nc = bass.Bass("TRN2", target_bir_lowering=False)
    ypre = nc.dram_tensor("ypre", [TC, D], F32, kind="ExternalInput").ap()
    outd = nc.dram_tensor("outd", [TC, D], F32, kind="ExternalOutput").ap()
    emit_attn(nc, ypre, outd)
    return nc



def _stage_reset(nc, snap=None, dma_reset=True):
    nc.all_engine_barrier()
    free = sorted(nc.free_semaphores)
    if free:
        with nc.Block() as b:
            @b.gpsimd
            def _(g):
                i = 0
                while i < len(free):
                    j = i
                    while j + 1 < len(free) and free[j + 1] == free[j] + 1:
                        j += 1
                    if dma_reset:
                        g.dma_reset(range(free[i], free[j] + 1))
                    g.sem_clear(range(free[i], free[j] + 1))
                    i = j + 1
    nc.all_engine_barrier()


def emit_t1(nc, y2, ypre1_dram):
    tag = "t1"
    NTB = TC // 128
    NK = D // 128
    identf = nc.inline_tensor(np.eye(128, dtype=np.float32).astype(ml_dtypes.bfloat16), "idt1")
    with (
        nc.sbuf_tensor("tldA" + tag, [128, NK * 128], BF16) as tld0,
        nc.sbuf_tensor("tldB" + tag, [128, NK * 128], BF16) as tld1,
        nc.sbuf_tensor("trow" + tag, [128, 2 * D], F32) as trow,
        nc.sbuf_tensor("tid" + tag, [128, 128], BF16) as tid,
        nc.psum_tensor("tp0" + tag, [128, 128], BF16) as tp0,
        nc.psum_tensor("tp1" + tag, [128, 128], BF16) as tp1,
        nc.semaphore("t1l0") as s_l0,
        nc.semaphore("t1l1") as s_l1,
        nc.semaphore("t1tp") as s_tp,
        nc.semaphore("t1ev") as s_ev,
        nc.semaphore("t1ro") as s_ro,
        nc.semaphore("t1id") as s_id,
        nc.Block() as block,
    ):
        s_l = [s_l0, s_l1]
        tlds = [tld0, tld1]
        agvs = [y2[b].ap().rearrange("(r p) t -> p r t", p=128) for b in range(2)]

        @block.sync
        def _(s):
            s.dma_start(out=tid[:, :], in_=identf.ap()).then_inc(s_id, 16)

        @block.gpsimd
        def _(g):
            for tb in range(NTB):
                if tb >= 2:
                    g.wait_ge(s_tp, (tb - 1) * NK)
                for b in range(2):
                    g.dma_start(
                        out=tlds[tb % 2][:, :].rearrange(
                            "p (r c t) -> p c r t", c=2, t=128)[:, b],
                        in_=agvs[b][:, :, tb * 128:(tb + 1) * 128],
                    ).then_inc(s_l[tb % 2], 16)

        @block.tensor
        def _(te):
            te.wait_ge(s_id, 16)
            cnt = 0
            for tb in range(NTB):
                te.wait_ge(s_l[tb % 2], (tb // 2 + 1) * 32)
                for k in range(NK):
                    cnt += 1
                    if cnt > 2:
                        te.wait_ge(s_ev, cnt - 2)
                    te.transpose((tp0 if cnt % 2 == 0 else tp1)[:, :],
                                 tlds[tb % 2][:, k * 128:(k + 1) * 128],
                                 tid[:, :]).then_inc(s_tp, 1)

        @block.vector
        def _(v):
            cnt = 0
            for tb in range(NTB):
                for k in range(NK):
                    cnt += 1
                    v.wait_ge(s_tp, cnt)
                    if k == 0 and tb >= 2:
                        v.wait_ge(s_ro, tb * 16)
                    v.tensor_copy(trow[:, (tb % 2) * D + k * 128:(tb % 2) * D + (k + 1) * 128],
                                  (tp0 if cnt % 2 == 0 else tp1)[:, :]).then_inc(s_ev, 1)

        @block.scalar
        def _(sc):
            for tb in range(NTB):
                sc.wait_ge(s_ev, (tb + 1) * NK)
                sc.dma_start(out=ypre1_dram[tb * 128:(tb + 1) * 128, :],
                             in_=trow[:, (tb % 2) * D:(tb % 2 + 1) * D]).then_inc(s_ro, 16)
            sc.wait_ge(s_ro, NTB * 16)


def build_fused():
    nc = bass.Bass("TRN2", target_bir_lowering=False)
    IC0 = 8192 // NC
    IC1 = 16384 // NC
    xT_tc = nc.dram_tensor("xT_tc", [D, TC], F32, kind="ExternalInput").ap()
    xT_dc = nc.dram_tensor("xT_dc", [DC, T], F32, kind="ExternalInput").ap()
    WxT = nc.dram_tensor("WxT", [D, 160], F32, kind="ExternalInput").ap()
    WdtT_c = nc.dram_tensor("WdtT_c", [R, DC], F32, kind="ExternalInput").ap()
    bdt_c = nc.dram_tensor("bdt_c", [DC], F32, kind="ExternalInput").ap()
    A_c = nc.dram_tensor("A_c", [DC, N], F32, kind="ExternalInput").ap()
    Dp_c = nc.dram_tensor("Dp_c", [DC], F32, kind="ExternalInput").ap()
    g0w = nc.dram_tensor("g0w", [IC0, D], I32, kind="ExternalInput").ap()
    g0s = nc.dram_tensor("g0s", [IC0], F32, kind="ExternalInput").ap()
    u0w = nc.dram_tensor("u0w", [IC0, D], I32, kind="ExternalInput").ap()
    u0s = nc.dram_tensor("u0s", [IC0], F32, kind="ExternalInput").ap()
    d0w = nc.dram_tensor("d0w", [D, IC0], I32, kind="ExternalInput").ap()
    d0s = nc.dram_tensor("d0s", [D], F32, kind="ExternalInput").ap()
    g1w = nc.dram_tensor("g1w", [IC1, D // 2], I32, kind="ExternalInput").ap()
    g1s = nc.dram_tensor("g1s", [IC1, D // 32], F32, kind="ExternalInput").ap()
    u1w = nc.dram_tensor("u1w", [IC1, D // 2], I32, kind="ExternalInput").ap()
    u1s = nc.dram_tensor("u1s", [IC1, D // 32], F32, kind="ExternalInput").ap()
    d1w = nc.dram_tensor("d1w", [D, IC1 // 2], I32, kind="ExternalInput").ap()
    d1s = nc.dram_tensor("d1s", [D, IC1 // 32], F32, kind="ExternalInput").ap()
    outd = nc.dram_tensor("outd", [TC, D], F32, kind="ExternalOutput").ap()
    ag_dt_in = nc.dram_tensor("ag_dt_in", [R, TC], BF16, kind="Internal")
    ag_dt_out = nc.dram_tensor("ag_dt_out", [NC * R, TC], BF16, kind="Internal",
                               addr_space="Shared")
    ag_bc_in = nc.dram_tensor("ag_bc_in", [2 * N, TC], BF16, kind="Internal")
    ag_bc_out = nc.dram_tensor("ag_bc_out", [NC * 2 * N, TC], BF16, kind="Internal",
                               addr_space="Shared")
    bc_packed = nc.dram_tensor("bc_packed", [2 * N, T], BF16, kind="Internal")
    ya2a_in = [nc.dram_tensor(f"ya2a_in{b}", [NC * 128, TC], BF16, kind="Internal")
               for b in range(2)]
    ya2a_out = [nc.dram_tensor(f"ya2a_out{b}", [NC * 128, TC], BF16, kind="Internal")
                for b in range(2)]
    cw = {k: nc.dram_tensor("c_" + k, shp, BF16, kind="Internal") for k, shp in
          [("w0g", [D, IC0]), ("w0u", [D, IC0]), ("w0d", [IC0, D]),
           ("w1g", [D, IC1]), ("w1u", [D, IC1]), ("w1d", [IC1, D])]}
    aw = {k: nc.dram_tensor("a_" + k, [NC * shp0, shp1], BF16, kind="Internal",
                            addr_space="Shared") for k, (shp0, shp1) in
          [("w0g", (D, IC0)), ("w0u", (D, IC0)), ("w0d", (IC0, D)),
           ("w1g", (D, IC1)), ("w1u", (D, IC1)), ("w1d", (IC1, D))]}
    ypre1_dram = nc.dram_tensor("ypre1_d", [TC, D], F32, kind="Internal")
    ypre2_dram = nc.dram_tensor("ypre2_d", [TC, D], F32, kind="Internal")
    ypre3_dram = nc.dram_tensor("ypre3_d", [TC, D], F32, kind="Internal")
    x1d_f0 = nc.dram_tensor("x1d_f0", [TC, D], F32, kind="Internal")
    x1d_f1 = nc.dram_tensor("x1d_f1", [TC, D], F32, kind="Internal")

    import contextlib
    octx = contextlib.ExitStack()
    with octx:
        s_agw = octx.enter_context(nc.semaphore("g_agw"))
        s_agy = octx.enter_context(nc.semaphore("g_agy"))

        def _mid_prep():
            psnap = nc.snapshot_sems()
            _prep_pair(nc, (g0w, g0s, cw["w0g"].ap(), IC0 // 128, D),
                       (g1w, g1s, cw["w1g"].ap(), IC1 // 128, D), tag="pg")
            _prep_pair(nc, (u0w, u0s, cw["w0u"].ap(), IC0 // 128, D),
                       (u1w, u1s, cw["w1u"].ap(), IC1 // 128, D), tag="pu")
            _prep_pair(nc, (d0w, d0s, cw["w0d"].ap(), D // 128, IC0),
                       (d1w, d1s, cw["w1d"].ap(), D // 128, IC1), tag="pd")
            with nc.Block() as b2:
                @b2.gpsimd
                def _(g):
                    for k in ["w0g", "w0u", "w0d", "w1g", "w1u", "w1d"]:
                        g.collective_compute("AllGather", ALU.bypass,
                                             replica_groups=[list(range(NC))],
                                             ins=[cw[k].ap().opt()],
                                             outs=[aw[k].ap().opt()]).then_inc(s_agw)
            _stage_reset(nc, psnap)

        _sc_s1 = nc.enter_named_scope("ssm", False)
        snap = nc.snapshot_sems()
        emit_stage1(nc, xT_tc, xT_dc, WxT, WdtT_c, bdt_c, A_c, Dp_c, None,
                    ag_dt_in, ag_dt_out, ag_bc_in, ag_bc_out, bc_packed,
                    a2a_y=(ya2a_in, ya2a_out, s_agy), mid_hook=_mid_prep)
        _stage_reset(nc, snap)
        nc.leave_named_scope("ssm", _sc_s1[0], False)

        _sc_t1 = nc.enter_named_scope("t1", False)
        snap = nc.snapshot_sems()
        with nc.Block() as b:
            @b.gpsimd
            def _(g):
                g.wait_ge(s_agy, 2)
        nc.all_engine_barrier()
        emit_t1(nc, ya2a_out, ypre1_dram.ap())
        _stage_reset(nc, snap)
        nc.leave_named_scope("t1", _sc_t1[0], False)

        def wv0(ic):
            r, lc = (ic * 512) // IC0, (ic * 512) % IC0
            gsrc = aw["w0g"].ap().rearrange("(r k p) i -> p r k i", p=128, k=D // 128)[
                :, r, :, lc:lc + 512]
            usrc = aw["w0u"].ap().rearrange("(r k p) i -> p r k i", p=128, k=D // 128)[
                :, r, :, lc:lc + 512]
            return gsrc, usrc
        _sc_f0 = nc.enter_named_scope("ffn0", False)
        snap = nc.snapshot_sems()
        emit_ffn(nc, 0, ypre1_dram.ap(), None, None, aw["w0d"].ap(), ypre2_dram.ap(),
                 8192, final_ln=False, x1_dram=x1d_f0.ap(), w_views=wv0,
                 w_ready=(s_agw, 6))
        _stage_reset(nc, snap)
        nc.leave_named_scope("ffn0", _sc_f0[0], False)

        _sc_at = nc.enter_named_scope("attn", False)
        snap = nc.snapshot_sems()
        emit_attn(nc, ypre2_dram.ap(), ypre3_dram.ap())
        _stage_reset(nc, snap)
        nc.leave_named_scope("attn", _sc_at[0], False)
        _sc_f1 = nc.enter_named_scope("ffn1", False)

        def wv1(ic):
            r, lc = (ic * 512) // IC1, (ic * 512) % IC1
            gsrc = aw["w1g"].ap().rearrange("(r k p) i -> p r k i", p=128, k=D // 128)[
                :, r, :, lc:lc + 512]
            usrc = aw["w1u"].ap().rearrange("(r k p) i -> p r k i", p=128, k=D // 128)[
                :, r, :, lc:lc + 512]
            return gsrc, usrc
        emit_ffn(nc, 1, ypre3_dram.ap(), None, None, aw["w1d"].ap(), outd,
                 16384, final_ln=True, x1_dram=x1d_f1.ap(), w_views=wv1)
        nc.leave_named_scope("ffn1", _sc_f1[0], False)
    return nc


def fused_inputs(full):
    x = np.ascontiguousarray(np.asarray(full["L0_attn_x"], dtype=np.float32
                                        ).reshape(T, D))
    m1 = stage1_inputs(x, np.asarray(full["L0_attn_W_xproj"]),
                       np.asarray(full["L0_attn_W_dt"]),
                       np.asarray(full["L0_attn_b_dt"]),
                       np.asarray(full["L0_attn_A"]),
                       np.asarray(full["L0_attn_D"]))
    mp = prep_inputs(full)
    return [{**m1[c], **mp[c]} for c in range(NC)]



from concourse.bass_utils import run_bass_kernel_spmd


def kernel(L0_attn_x, L0_attn_W_xproj, L0_attn_W_dt, L0_attn_b_dt, L0_attn_A,
           L0_attn_D, L0_ffn_gate_w, L0_ffn_gate_scale, L0_ffn_up_w,
           L0_ffn_up_scale, L0_ffn_down_w, L0_ffn_down_scale,
           L1_ffn_gate_W_packed, L1_ffn_gate_scales, L1_ffn_up_W_packed,
           L1_ffn_up_scales, L1_ffn_down_W_packed, L1_ffn_down_scales):
    full = {
        "L0_attn_x": np.asarray(L0_attn_x),
        "L0_attn_W_xproj": np.asarray(L0_attn_W_xproj),
        "L0_attn_W_dt": np.asarray(L0_attn_W_dt),
        "L0_attn_b_dt": np.asarray(L0_attn_b_dt),
        "L0_attn_A": np.asarray(L0_attn_A),
        "L0_attn_D": np.asarray(L0_attn_D),
        "L0_ffn_gate_w": np.asarray(L0_ffn_gate_w),
        "L0_ffn_gate_scale": np.asarray(L0_ffn_gate_scale),
        "L0_ffn_up_w": np.asarray(L0_ffn_up_w),
        "L0_ffn_up_scale": np.asarray(L0_ffn_up_scale),
        "L0_ffn_down_w": np.asarray(L0_ffn_down_w),
        "L0_ffn_down_scale": np.asarray(L0_ffn_down_scale),
        "L1_ffn_gate_W_packed": np.asarray(L1_ffn_gate_W_packed),
        "L1_ffn_gate_scales": np.asarray(L1_ffn_gate_scales),
        "L1_ffn_up_W_packed": np.asarray(L1_ffn_up_W_packed),
        "L1_ffn_up_scales": np.asarray(L1_ffn_up_scales),
        "L1_ffn_down_W_packed": np.asarray(L1_ffn_down_W_packed),
        "L1_ffn_down_scales": np.asarray(L1_ffn_down_scales),
    }
    nc = build_fused()
    maps = fused_inputs(full)
    res = run_bass_kernel_spmd(nc, maps, core_ids=list(range(NC))).results
    x4 = np.concatenate([res[c]["outd"] for c in range(NC)], axis=0)
    return np.ascontiguousarray(x4.reshape(1, T, D)).astype(np.float32)
